# version 21
# baseline (speedup 1.0000x reference)
import numpy as np
import ml_dtypes
from contextlib import ExitStack
import concourse.bass as bass
import concourse.mybir as mybir
from concourse.bass_utils import run_bass_kernel_spmd
from concourse.alu_op_type import AluOpType as OP

F32 = mybir.dt.float32
BF16 = mybir.dt.bfloat16
AF = mybir.ActivationFunctionType
AX = mybir.AxisListType

ENGINES = ("sync", "gpsimd", "scalar", "vector", "tensor")


class Prog:

    def __init__(self, nc, ctx):
        self.nc = nc
        self.ctx = ctx
        self.streams = {e: [] for e in ENGINES}
        self.sems = {}
        self.cnt = {}
        self.dma_sems = set()
        for name in ("tE", "vE", "sE", "gE"):
            self._mk(name)

    def _mk(self, name):
        self.sems[name] = self.ctx.enter_context(self.nc.semaphore(name))
        self.cnt[name] = 0

    def now(self, sem):
        return self.cnt[sem]

    def op(self, eng, fn, waits=(), inc=None):
        w = tuple((s, v) for (s, v) in waits if v > 0)
        if inc is not None:
            self.cnt[inc] += 16 if inc in self.dma_sems else 1
        self.streams[eng].append((w, fn, inc))
        return self.cnt[inc] if inc else None

    def dma(self, eng, fn, waits=(), sem="cS"):
        if sem not in self.sems:
            self._mk(sem)
            self.dma_sems.add(sem)
        return self.op(eng, fn, waits, inc=sem)

    def barrier(self):
        snap = [(s, self.cnt[s]) for s in self.sems if self.cnt[s] > 0]
        for eng in ENGINES:
            self.streams[eng].append((tuple(snap), None, None))


def _emit_body(items, sems, dma_sems, e):
    last = {}
    for waits, fn, inc in items:
        for s, v in waits:
            if last.get(s, 0) < v:
                e.wait_ge(sems[s], v)
                last[s] = v
        if fn is None:
            continue
        ins = fn(e)
        if inc is not None:
            ins.then_inc(sems[inc], 16 if inc in dma_sems else 1)


def build(dbg=(), stages=("P", "A", "B", "D"), router=True, fold=True):
    nc = bass.Bass("TRN2", target_bir_lowering=False)

    x = nc.dram_tensor("x", [4096, 1024], F32, kind="ExternalInput").ap()
    qwT = nc.dram_tensor("qwT", [1024, 1024], F32, kind="ExternalInput").ap()
    kwT = nc.dram_tensor("kwT", [1024, 1024], F32, kind="ExternalInput").ap()
    vwT = nc.dram_tensor("vwT", [1024, 1024], F32, kind="ExternalInput").ap()
    owT = nc.dram_tensor("owT", [1024, 1024], F32, kind="ExternalInput").ap()
    g_q = nc.dram_tensor("g_q", [128, 8], F32, kind="ExternalInput").ap()
    b_qn = nc.dram_tensor("b_qn", [128, 8], F32, kind="ExternalInput").ap()
    g_k = nc.dram_tensor("g_k", [128, 8], F32, kind="ExternalInput").ap()
    b_kn = nc.dram_tensor("b_kn", [128, 8], F32, kind="ExternalInput").ap()
    g_v = nc.dram_tensor("g_v", [128, 8], F32, kind="ExternalInput").ap()
    b_vn = nc.dram_tensor("b_vn", [128, 8], F32, kind="ExternalInput").ap()
    qb_c = nc.dram_tensor("qb_c", [128, 8], F32, kind="ExternalInput").ap()
    kb_c = nc.dram_tensor("kb_c", [128, 8], F32, kind="ExternalInput").ap()
    vb_c = nc.dram_tensor("vb_c", [128, 8], F32, kind="ExternalInput").ap()
    ob_r = nc.dram_tensor("ob_r", [1, 1024], F32, kind="ExternalInput").ap()
    beta_r = nc.dram_tensor("beta_r", [1, 16], F32, kind="ExternalInput").ap()
    rw = nc.dram_tensor("rw", [128, 8, 16], F32, kind="ExternalInput").ap()
    rb_r = nc.dram_tensor("rb_r", [1, 16], F32, kind="ExternalInput").ap()
    w1d = nc.dram_tensor("w1", [16, 8, 128, 2048], F32, kind="ExternalInput").ap()
    b1c = nc.dram_tensor("b1c", [16, 128, 16], F32, kind="ExternalInput").ap()
    w2d = nc.dram_tensor("w2", [16, 16, 128, 1024], F32, kind="ExternalInput").ap()
    b2d = nc.dram_tensor("b2", [16, 1024], F32, kind="ExternalInput").ap()
    identd = nc.dram_tensor("ident", [128, 128], BF16, kind="ExternalInput").ap()
    ident32d = nc.dram_tensor("ident32", [128, 128], F32, kind="ExternalInput").ap()
    out = nc.dram_tensor("out", [2048, 1024], F32, kind="ExternalOutput").ap()

    def scratch(name, shape):
        kind = "ExternalOutput" if name in dbg else "Internal"
        return nc.dram_tensor(name, shape, BF16, kind=kind).ap()

    qT_s = scratch("qT_s", [8, 128, 2048])
    kT_s = scratch("kT_s", [8, 128, 4096])
    otn_s = scratch("otn_s", [8, 128, 2048])
    vAug_s = nc.dram_tensor("vAug_s", [8, 128, 32, 130], BF16,
                            kind="ExternalOutput" if "vaug" in dbg else "Internal").ap()
    acc_d = nc.dram_tensor("acc_d", [16, 128, 1024], F32,
                           kind="ExternalOutput" if "acc" in dbg else "Internal").ap()
    if "c" in dbg:
        c_dbg = nc.dram_tensor("c_dbg", [128, 256], F32, kind="ExternalOutput").ap()

    CTX = ExitStack()
    with CTX:
        P = Prog(nc, CTX)

        def finish():
            with nc.Block() as block:
                for eng in ENGINES:
                    items = P.streams[eng]
                    if items:
                        getattr(block, eng)(lambda e, it=items: _emit_body(it, P.sems, P.dma_sems, e))

        GE = CTX.enter_context
        ident_sb = GE(nc.sbuf_tensor("ident_sb", [128, 128], BF16))
        ones_cb = GE(nc.sbuf_tensor("ones_cb", [128, 1], BF16))
        ones_rf = GE(nc.sbuf_tensor("ones_rf", [1, 128], F32))
        Bb = GE(nc.sbuf_tensor("Bb", [128, 16], F32))
        c_sb = GE(nc.sbuf_tensor("c_sb", [128, 256], F32))
        cT_sb = GE(nc.sbuf_tensor("cT_sb", [16, 2048], BF16))
        negmu1 = GE(nc.sbuf_tensor("negmu1", [128, 16], F32))
        rcol = GE(nc.sbuf_tensor("rcol", [128, 8], F32))
        xnT_sb = GE(nc.sbuf_tensor("xnT_sb", [128, 8, 2048], BF16))
        r1sb = GE(nc.sbuf_tensor("r1sb", [128, 16], F32))

        PCTX = ExitStack()
        PE_ = PCTX.enter_context
        qwT_sb = PE_(nc.sbuf_tensor("qwT_sb", [128, 8, 1024], BF16))
        kwT_sb = PE_(nc.sbuf_tensor("kwT_sb", [128, 8, 1024], BF16))
        vwT_sb = PE_(nc.sbuf_tensor("vwT_sb", [128, 8, 1024], BF16))
        rw_sb = PE_(nc.sbuf_tensor("rw_sb", [128, 8, 16], F32))
        ident32_sb = PE_(nc.sbuf_tensor("ident32_sb", [128, 128], F32))
        xnT32 = PE_(nc.sbuf_tensor("xnT32", [128, 8, 128], F32))
        gq_sb = PE_(nc.sbuf_tensor("gq_sb", [128, 8], F32))
        bq_sb = PE_(nc.sbuf_tensor("bq_sb", [128, 8], F32))
        gk_sb = PE_(nc.sbuf_tensor("gk_sb", [128, 8], F32))
        bk_sb = PE_(nc.sbuf_tensor("bk_sb", [128, 8], F32))
        gv_sb = PE_(nc.sbuf_tensor("gv_sb", [128, 8], F32))
        bv_sb = PE_(nc.sbuf_tensor("bv_sb", [128, 8], F32))
        qbc_sb = PE_(nc.sbuf_tensor("qbc_sb", [128, 8], F32))
        kbc_sb = PE_(nc.sbuf_tensor("kbc_sb", [128, 8], F32))
        rb_bc = PE_(nc.sbuf_tensor("rb_bc", [128, 16], F32))
        brow = PE_(nc.sbuf_tensor("brow", [1, 16], F32))
        rbrow = PE_(nc.sbuf_tensor("rbrow", [1, 16], F32))
        sprow = PE_(nc.sbuf_tensor("sprow", [1, 16], F32))
        sprow2 = PE_(nc.sbuf_tensor("sprow2", [1, 16], F32))
        xin = [PE_(nc.sbuf_tensor(f"xin{i}", [128, 1024], F32)) for i in range(2)]
        xwork2 = [PE_(nc.sbuf_tensor(f"xwork{i}", [128, 1024], F32)) for i in range(4)]
        xsq2 = [PE_(nc.sbuf_tensor(f"xsq{i}", [128, 1024], F32)) for i in range(2)]
        xscr = PE_(nc.sbuf_tensor("xscr", [128, 1024], F32))
        zb2 = [PE_(nc.sbuf_tensor(f"zb{i}", [128, 1024], BF16)) for i in range(4)]
        zT = [PE_(nc.sbuf_tensor(f"zT{i}", [128, 8, 512], BF16)) for i in range(2)]
        fT_st = PE_(nc.sbuf_tensor("fT_st", [128, 8, 512], BF16))
        vstg = [PE_(nc.sbuf_tensor(f"vstg{i}", [128, 1040], BF16)) for i in range(4)]
        proj = [PE_(nc.sbuf_tensor(f"proj{i}", [128, 8, 512], BF16)) for i in range(2)]
        sqb = fT_st
        rnb = PE_(nc.sbuf_tensor("rnb", [128, 512], F32))
        nrm = PE_(nc.sbuf_tensor("nrm", [1, 512], F32))
        colb = PE_(nc.sbuf_tensor("colb", [128, 20], F32))
        eps_cell = PE_(nc.sbuf_tensor("eps_cell", [128, 1], F32))
        lgt2 = [PE_(nc.sbuf_tensor(f"lgt{i}", [128, 16], F32)) for i in range(2)]
        p162 = [PE_(nc.sbuf_tensor(f"p16_{i}", [128, 16], F32)) for i in range(2)]
        cur2 = [PE_(nc.sbuf_tensor(f"cur{i}", [128, 16], F32)) for i in range(2)]
        msk2 = [PE_(nc.sbuf_tensor(f"msk{i}", [128, 16], F32)) for i in range(2)]
        cbf = PE_(nc.sbuf_tensor("cbf", [128, 256], BF16))

        ps_proj = [PE_(nc.psum_tensor(f"ps_proj{i}", [128, 512], F32)) for i in range(2)]
        ps_tr = [PE_(nc.psum_tensor(f"ps_tr{i}", [128, 128], BF16)) for i in range(2)]
        ps_tr32 = [PE_(nc.psum_tensor(f"ps_tr32{i}", [128, 128], F32)) for i in range(2)]
        ps_rb = PE_(nc.psum_tensor("ps_rb", [128, 512], F32))
        ps_ss = ps_rb[0:1, 0:512]
        ps_rt = PE_(nc.psum_tensor("ps_rt", [128, 16], F32))

        tr_evac = [0, 0]
        tr32_evac = [0, 0]
        xwork_free = [0, 0, 0, 0]
        zb_free = [0, 0, 0, 0]
        zT_used = [0, 0]
        proj_free = [0, 0]
        zT_ready = [0]
        rt_free = [0]
        nrm_free = [0]
        rnb_free = [0]
        fT_free = [0]
        proj_evac = [0, 0]
        vproj_evac = [0, 0]
        xin_ready = [0, 0]
        xin_free = [0, 0]
        vstg_free = [0, 0, 0, 0]

        P.dma("sync", lambda e: e.dma_start(out=ident_sb[:], in_=identd), sem="cS")
        P.dma("sync", lambda e: e.dma_start(out=ident32_sb[:], in_=ident32d), sem="cS")
        P.dma("sync", lambda e: e.dma_start(out=rw_sb[:], in_=rw), sem="cS")
        for dst, src in ((gq_sb, g_q), (bq_sb, b_qn), (gk_sb, g_k), (bk_sb, b_kn),
                         (gv_sb, g_v), (bv_sb, b_vn), (qbc_sb, qb_c), (kbc_sb, kb_c)):
            P.dma("sync", lambda e, d=dst, s=src: e.dma_start(out=d[:], in_=s), sem="cS")
        P.dma("sync", lambda e: e.dma_start(out=brow[:], in_=beta_r), sem="cS")
        P.dma("sync", lambda e: e.dma_start(out=rbrow[:], in_=rb_r), sem="cS")
        dvb = P.dma("sync", lambda e: e.dma_start(out=rcol[:], in_=vb_c), sem="cS")
        for dst, src in ((qwT_sb, qwT), (kwT_sb, kwT), (vwT_sb, vwT)):
            P.dma("gpsimd", lambda e, d=dst, s=src: e.dma_start(
                out=d[:], in_=s.rearrange("(a p) n -> p a n", p=128)), sem="cG")
        cS0 = P.now("cS")
        cG0 = P.now("cG")

        vv0 = P.op("vector", lambda e: e.memset(ones_cb[:], 1.0), inc="vE")
        vv0 = P.op("vector", lambda e: e.memset(eps_cell[:], 1e-5), inc="vE")
        nc.const_aps.register(F32, 1e-5, eps_cell[:])
        vv0 = P.op("vector", lambda e: e.memset(ones_rf[:], 1.0), inc="vE")
        for _vi in range(4):
            vv0 = P.op("vector", lambda e, i=_vi: e.memset(vstg[i][:].rearrange("p (a b) -> p a b", a=16)[:, :, 64:65], 1.0), inc="vE")

        t1 = P.op("tensor", lambda e: e.matmul(ps_rt[:], ones_rf[:], rbrow[:], start=True, stop=True),
                  waits=[("cS", cS0), ("vE", vv0)], inc="tE")
        vrb = P.op("vector", lambda e: e.tensor_copy(rb_bc[:], ps_rt[:]), waits=[("tE", t1)], inc="vE")

        s1 = P.op("scalar", lambda e: e.activation(sprow[:], brow[:], AF.Abs), waits=[("cS", cS0)], inc="sE")
        s1 = P.op("scalar", lambda e: e.activation(sprow[:], sprow[:], AF.Exp, scale=-1.0), waits=[("sE", s1)], inc="sE")
        v1b = P.op("vector", lambda e: e.tensor_scalar_add(sprow[:], sprow[:], 1.0), waits=[("sE", s1)], inc="vE")
        s1 = P.op("scalar", lambda e: e.activation(sprow[:], sprow[:], AF.Ln), waits=[("vE", v1b)], inc="sE")
        v2 = P.op("vector", lambda e: e.tensor_scalar_max(sprow2[:], brow[:], 0.0), waits=[("cS", cS0)], inc="vE")
        v2 = P.op("vector", lambda e: e.tensor_add(sprow[:], sprow[:], sprow2[:]), waits=[("sE", s1), ("vE", v2)], inc="vE")
        v2 = P.op("vector", lambda e: e.tensor_scalar(sprow[:], sprow[:], 0.1, 10.0, OP.max, OP.min), waits=[("vE", v2)], inc="vE")
        v2 = P.op("vector", lambda e: e.tensor_scalar_mul(sprow[:], sprow[:], 0.125), waits=[("vE", v2)], inc="vE")
        t2 = P.op("tensor", lambda e: e.matmul(ps_rt[:], ones_rf[:], sprow[:], start=True, stop=True),
                  waits=[("vE", v2)], inc="tE")
        vBb = P.op("vector", lambda e: e.tensor_copy(Bb[:], ps_rt[:]), waits=[("tE", t2)], inc="vE")
        rt_free[0] = vBb

        bqb = PE_(nc.sbuf_tensor("bqb", [128, 8], BF16))
        bkb = PE_(nc.sbuf_tensor("bkb", [128, 8], BF16))
        bvb = PE_(nc.sbuf_tensor("bvb", [128, 8], BF16))
        vbc = 0
        for dst, srcb in ((bqb, bq_sb), (bkb, bk_sb), (bvb, bv_sb)):
            vbc = P.op("vector", lambda e, d=dst, s=srcb: e.tensor_copy(d[:], s[:]),
                       waits=[("cS", cS0)], inc="vE")
        for pi, (wsb, bcol, biasc, cw) in enumerate((
                (qwT_sb, bqb, qbc_sb, 0), (kwT_sb, bkb, kbc_sb, 0), (vwT_sb, bvb, rcol, 8))):
            bank = pi % 2
            for oc in range(8):
                tf = P.op("tensor", lambda e, o=oc, w=wsb, b=bcol, p=bank, c=cw: e.matmul(
                    ps_proj[p][:, c + o:c + o + 1], w[:, 0, o * 128:(o + 1) * 128], b[:, 0:1],
                    start=True, stop=False),
                    waits=[("cG", cG0), ("vE", vbc)], inc=None)
                for d in range(1, 8):
                    tf = P.op("tensor", lambda e, o=oc, w=wsb, b=bcol, p=bank, c=cw, dd=d: e.matmul(
                        ps_proj[p][:, c + o:c + o + 1], w[:, dd, o * 128:(o + 1) * 128], b[:, dd:dd + 1],
                        start=False, stop=(dd == 7)),
                        inc="tE" if (d == 7) else None)
            rnb_free[0] = P.op("vector", lambda e, bi=biasc, p=bank, c=cw: e.tensor_add(
                bi[:], bi[:], ps_proj[p][:, c:c + 8]),
                waits=[("tE", tf), ("cS", max(cS0, dvb))], inc="vE")
        fold_done = P.now("tE")
        wsc = 0
        for wsb, gcol in ((qwT_sb, gq_sb), (kwT_sb, gk_sb), (vwT_sb, gv_sb)):
            for d in range(8):
                wsc = P.op("vector", lambda e, w=wsb, g=gcol, dd=d: e.tensor_single_scalar(
                    w[:, dd, :], w[:, dd, :], g[:, dd:dd + 1], OP.mult),
                    waits=[("tE", fold_done), ("cS", cS0), ("cG", cG0)], inc="vE")
        vproj_evac[0] = vproj_evac[1] = wsc


        def load_x(tt):
            slot = tt % 2
            xin_ready[slot] = P.dma("sync", lambda e, t=tt, s=slot: e.dma_start(
                out=xin[s][:], in_=x[t * 128:(t + 1) * 128, :]),
                waits=[("vE", xin_free[slot])], sem=f"xin{slot}")

        load_x(0)
        load_x(1)

        def transpose_to(dst_fn, src_fn, extra):
            for dc in range(8):
                sl = dc % 2
                tval = P.op("tensor", lambda e, d=dc, s=sl: e.transpose(
                    ps_tr[s][:], src_fn(d), ident_sb[:]),
                    waits=list(extra) + [("sE", tr_evac[sl])], inc="tE")
                tr_evac[sl] = P.op("scalar", lambda e, d=dc, s=sl: e.copy(dst_fn(d), ps_tr[s][:]),
                                   waits=[("tE", tval)], inc="sE")

        def ln_pair(tt_a, tt_b):
            sts = []
            for tt in (tt_a, tt_b):
                p = tt % 2
                cc = [colb[:, 8 * p + j:8 * p + j + 1] for j in range(8)]
                sts.append(dict(
                    tt=tt, p=p, q4=tt % 4, slot=tt % 2, xw=xwork2[tt % 4], xq=xsq2[p], zb=zb2[tt % 4], cc=cc,
                    nm1=negmu1[:, tt:tt + 1] if tt < 16 else cc[6],
                    r1c=r1sb[:, tt:tt + 1] if tt < 16 else cc[7]))
            for st in sts:
                st["v"] = P.op("vector", lambda e, s=st["slot"], a=st["cc"][0]: e.reduce_sum(a, xin[s][:], axis=AX.X),
                               waits=[(f"xin{st['slot']}", xin_ready[st["slot"]])], inc="vE")
            for st in sts:
                st["v"] = P.op("vector", lambda e, d=st["nm1"], a=st["cc"][0]: e.tensor_scalar_mul(d, a, -1.0 / 1024.0),
                               waits=[("vE", st["v"])], inc="vE")
            for st in sts:
                st["v"] = P.op("vector", lambda e, s=st["slot"], d=st["nm1"], w=st["xw"]: e.tensor_scalar_add(w[:], xin[s][:], d),
                               waits=[("vE", st["v"]), ("tE", xwork_free[st["q4"]])], inc="vE")
                xin_free[st["slot"]] = st["v"]
            for st in sts:
                if st["tt"] + 2 < 32:
                    load_x(st["tt"] + 2)
            for st in sts:
                st["s"] = P.op("scalar", lambda e, w=st["xw"], a=st["cc"][1]: e.activation(xscr[:], w[:], AF.Square, accum_out=a),
                               waits=[("vE", st["v"])], inc="sE")
            for st in sts:
                st["s"] = P.op("scalar", lambda e, a=st["cc"][2], b=st["cc"][1]: e.activation(a, b, AF.Ln, scale=1.0 / 1024.0, bias=1e-5),
                               waits=[("sE", st["s"])], inc="sE")
            for st in sts:
                st["s"] = P.op("scalar", lambda e, d=st["r1c"], a=st["cc"][2]: e.activation(d, a, AF.Exp, scale=-0.5),
                               waits=[("sE", st["s"])], inc="sE")
            for st in sts:
                st["v"] = P.op("vector", lambda e, d=st["r1c"], w=st["xw"]: e.tensor_single_scalar(w[:], w[:], d, OP.mult),
                               waits=[("sE", st["s"])], inc="vE")
                st["vxn"] = st["v"]
            for st in sts:
                st["v"] = P.op("vector", lambda e, w=st["xw"], a=st["cc"][3]: e.reduce_sum(a, w[:], axis=AX.X),
                               waits=[("vE", st["v"])], inc="vE")
            for st in sts:
                st["v"] = P.op("vector", lambda e, a=st["cc"][4], b=st["cc"][3]: e.tensor_scalar_mul(a, b, -1.0 / 1024.0),
                               waits=[("vE", st["v"])], inc="vE")
            for st in sts:
                st["v"] = P.op("vector", lambda e, w=st["xw"], q=st["xq"], a=st["cc"][4]: e.tensor_scalar_add(q[:], w[:], a),
                               waits=[("vE", st["v"])], inc="vE")
            for st in sts:
                st["s"] = P.op("scalar", lambda e, q=st["xq"], a=st["cc"][5]: e.activation(xscr[:], q[:], AF.Square, accum_out=a),
                               waits=[("vE", st["v"])], inc="sE")
            for st in sts:
                st["s"] = P.op("scalar", lambda e, a=st["cc"][5]: e.activation(a, a, AF.Ln, scale=1.0 / 1024.0, bias=1e-5),
                               waits=[("sE", st["s"])], inc="sE")
            for st in sts:
                st["s"] = P.op("scalar", lambda e, a=st["cc"][5]: e.activation(a, a, AF.Exp, scale=-0.5),
                               waits=[("sE", st["s"])], inc="sE")
            for st in sts:
                st["vz"] = P.op("vector", lambda e, z=st["zb"], q=st["xq"], a=st["cc"][5]: e.tensor_single_scalar(z[:], q[:], a, OP.mult),
                               waits=[("sE", st["s"]), ("tE", zb_free[st["q4"]])], inc="vE")
            return sts

        def trrouter(st, zsl):
            tt, p, ti = st["tt"], st["p"], st["tt"] % 4
            transpose_to(
                lambda dc, t=ti, zz=zsl: zT[zz][:, dc, t * 128:(t + 1) * 128],
                lambda dc, z=st["zb"]: z[:, dc * 128:(dc + 1) * 128],
                [("vE", st["vz"]), ("tE", zT_used[zsl])])
            zb_free[tt % 4] = P.now("tE")
            if tt < 16 and router:
                ev2 = 0
                for dc in range(8):
                    tsl = dc % 2
                    tval = P.op("tensor", lambda e, d=dc, s=tsl, w=st["xw"]: e.transpose(
                        ps_tr32[s][:], w[:, d * 128:(d + 1) * 128], ident32_sb[:]),
                        waits=[("vE", st["vxn"]), ("sE", tr32_evac[tsl])], inc="tE")
                    P.op("scalar", lambda e, d=dc, s=tsl: e.copy(xnT32[:, d, :], ps_tr32[s][:]),
                         waits=[("tE", tval)], inc="sE")
                    ev2 = P.op("scalar", lambda e, d=dc, s=tsl, t=tt: e.copy(
                        xnT_sb[:, d, t * 128:(t + 1) * 128], ps_tr32[s][:]),
                        waits=[("tE", tval)], inc="sE")
                    tr32_evac[tsl] = ev2
                xwork_free[tt % 4] = P.now("tE")
                for dc in range(8):
                    tv = P.op("tensor", lambda e, d=dc: e.matmul(
                        ps_rt[:], xnT32[:, d, :], rw_sb[:, d, :],
                        start=(d == 0), stop=(d == 7)),
                        waits=[("sE", ev2), ("cS", cS0), ("vE", rt_free[0])],
                        inc="tE" if dc == 7 else None)
                st["rv"] = P.op("vector", lambda e, g=lgt2[p]: e.tensor_add(g[:], ps_rt[:], rb_bc[:]),
                                waits=[("tE", tv), ("vE", vrb)], inc="vE")
                rt_free[0] = st["rv"]

        def softmax_pair(sts):
            for st in sts:
                p = st["p"]
                st["g"] = lgt2[p]
                st["pp"] = p162[p]
                st["cu"] = cur2[p]
                st["mk"] = msk2[p]
                st["c0"] = colb[:, 16 + 2 * p:17 + 2 * p]
                st["c1"] = colb[:, 17 + 2 * p:18 + 2 * p]
                st["v"] = st["rv"]

            def V(st, fn, waits_extra=()):
                st["v"] = P.op("vector", fn, waits=[("vE", st["v"])] + list(waits_extra), inc="vE")

            for st in sts:
                V(st, lambda e, g=st["g"], a=st["c0"]: e.reduce_max(a, g[:], axis=AX.X))
            for st in sts:
                V(st, lambda e, g=st["g"], a=st["c0"]: e.tensor_single_scalar(g[:], g[:], a, OP.subtract))
            for st in sts:
                st["s"] = P.op("scalar", lambda e, g=st["g"], q=st["pp"]: e.activation(q[:], g[:], AF.Exp),
                               waits=[("vE", st["v"])], inc="sE")
            for st in sts:
                st["v"] = P.op("vector", lambda e, q=st["pp"], a=st["c0"]: e.reduce_sum(a, q[:], axis=AX.X),
                               waits=[("sE", st["s"])], inc="vE")
            for st in sts:
                V(st, lambda e, a=st["c1"], b=st["c0"]: e.reciprocal(a, b))
            for st in sts:
                V(st, lambda e, q=st["pp"], a=st["c1"]: e.tensor_single_scalar(q[:], q[:], a, OP.mult))
            for st in sts:
                V(st, lambda e, q=st["pp"], c=st["cu"]: e.tensor_copy(c[:], q[:]))
            for _ in range(3):
                for st in sts:
                    V(st, lambda e, c=st["cu"], a=st["c0"]: e.reduce_max(a, c[:], axis=AX.X))
                for st in sts:
                    V(st, lambda e, c=st["cu"], m=st["mk"], a=st["c0"]: e.tensor_single_scalar(m[:], c[:], a, OP.is_ge))
                for st in sts:
                    V(st, lambda e, c=st["cu"], m=st["mk"]: e.scalar_tensor_tensor(c[:], m[:], -2.0, c[:], OP.mult, OP.add))
            for st in sts:
                V(st, lambda e, c=st["cu"], a=st["c0"]: e.reduce_max(a, c[:], axis=AX.X))
            for st in sts:
                V(st, lambda e, q=st["pp"], m=st["mk"], a=st["c0"]: e.tensor_single_scalar(m[:], q[:], a, OP.is_ge))
            for st in sts:
                V(st, lambda e, q=st["pp"], m=st["mk"]: e.tensor_mul(q[:], q[:], m[:]))
            for st in sts:
                V(st, lambda e, q=st["pp"], a=st["c0"]: e.reduce_sum(a, q[:], axis=AX.X))
            for st in sts:
                V(st, lambda e, a=st["c1"], b=st["c0"]: e.reciprocal(a, b))
            for st in sts:
                V(st, lambda e, q=st["pp"], a=st["c1"], t=st["tt"]: e.tensor_single_scalar(
                    c_sb[:, t * 16:(t + 1) * 16], q[:], a, OP.mult))

        def proj_mms(wsb, bias_col, zsl, psl):
            for oc in range(8):
                for dc in range(8):
                    tvv = P.op("tensor", lambda e, d=dc, o=oc, w=wsb, zz=zsl: e.matmul(
                        ps_proj[o % 2][:], w[:, d, o * 128:(o + 1) * 128], zT[zz][:, d, :],
                        start=(d == 0), stop=(d == 7)),
                        waits=[("sE", zT_ready[0]), ("vE", vproj_evac[oc % 2]), ("cG", cG0),
                               ("sE", proj_evac[oc % 2])],
                        inc="tE" if dc == 7 else None)
                proj_evac[oc % 2] = P.op("scalar", lambda e, o=oc, b=bias_col, pp=psl: e.activation(
                    proj[pp][:, o, :], ps_proj[o % 2][:], AF.Identity, bias=b[:, o:o + 1]),
                    waits=[("tE", tvv), ("vE", proj_free[psl])], inc="sE")

        def make_fin(psl, is_q, spill_dst):
            ev0, ev1 = proj_evac[0], proj_evac[1]

            def fin():
                vq = P.op("vector", lambda e, pp=psl: e.tensor_mul(sqb[:], proj[pp][:], proj[pp][:]),
                          waits=[("sE", max(ev0, ev1)), ("sfT", fT_free[0])], inc="vE")
                for oc in range(8):
                    tss = P.op("tensor", lambda e, o=oc: e.matmul(
                        ps_ss[:], ones_cb[:], sqb[:, o, :], start=(o == 0), stop=(o == 7)),
                        waits=[("vE", vq), ("sE", nrm_free[0])],
                        inc="tE" if oc == 7 else None)
                sn = P.op("scalar", lambda e: e.activation(nrm[:], ps_ss[:], AF.Ln), waits=[("tE", tss)], inc="sE")
                nrm_free[0] = sn
                sn = P.op("scalar", lambda e: e.activation(nrm[:], nrm[:], AF.Exp, scale=0.5), waits=[("sE", sn)], inc="sE")
                vn_ = P.op("vector", lambda e: e.tensor_scalar_max(nrm[:], nrm[:], 1e-6), waits=[("sE", sn)], inc="vE")
                sn = P.op("scalar", lambda e: e.activation(nrm[:], nrm[:], AF.Ln), waits=[("vE", vn_)], inc="sE")
                sn = P.op("scalar", lambda e: e.activation(nrm[:], nrm[:], AF.Exp, scale=-1.0), waits=[("sE", sn)], inc="sE")
                tb = P.op("tensor", lambda e: e.matmul(ps_rb[:], ones_rf[:], nrm[:], start=True, stop=True),
                          waits=[("sE", sn), ("vE", rnb_free[0])], inc="tE")
                vr = P.op("vector", lambda e: e.tensor_copy(rnb[:], ps_rb[:]), waits=[("tE", tb)], inc="vE")
                rnb_free[0] = vr
                lastv = 0
                for oc in range(8):
                    if is_q:
                        lastv = P.op("vector", lambda e, o=oc, pp=psl: e.tensor_tensor(
                            fT_st[:, o, :], proj[pp][:, o, :], rnb[:], OP.mult),
                            waits=[("vE", vr), ("sfT", fT_free[0])], inc="vE")
                    else:
                        lastv = P.op("vector", lambda e, o=oc, pp=psl: e.scalar_tensor_tensor(
                            fT_st[0:64, o, :], proj[pp][0:64, o, :], Bb[0:64, 2 * o:2 * o + 1],
                            rnb[0:64, :], OP.mult, OP.mult),
                            waits=[("vE", vr), ("sfT", fT_free[0])], inc="vE")
                        lastv = P.op("vector", lambda e, o=oc, pp=psl: e.scalar_tensor_tensor(
                            fT_st[64:128, o, :], proj[pp][64:128, o, :], Bb[64:128, 2 * o + 1:2 * o + 2],
                            rnb[64:128, :], OP.mult, OP.mult),
                            waits=[("vE", vr)], inc="vE")
                proj_free[psl] = lastv
                fT_free[0] = P.dma("sync", lambda e, d=spill_dst: e.dma_start(
                    out=d, in_=fT_st[:]), waits=[("vE", lastv)], sem="sfT")
            return fin

        p_ctr = [0]
        pending_fin = [None]

        def run_pending():
            if pending_fin[0] is not None:
                pending_fin[0]()
                pending_fin[0] = None

        pend_ln = [None]
        for gi in range(8):
            t0 = gi * 4
            zsl = gi % 2
            if pend_ln[0] is None:
                stsA = ln_pair(t0, t0 + 1)
                trrouter(stsA[0], zsl)
                trrouter(stsA[1], zsl)
                stsB = ln_pair(t0 + 2, t0 + 3)
            else:
                stsA, stsB = pend_ln[0]
                trrouter(stsA[0], zsl)
                trrouter(stsA[1], zsl)
            if t0 < 16 and router:
                softmax_pair(stsA)
            trrouter(stsB[0], zsl)
            trrouter(stsB[1], zsl)
            if t0 < 16 and router:
                softmax_pair(stsB)
            zT_ready[0] = P.now("sE")
            nxt = [None, None]

            if gi < 4:
                psl = p_ctr[0] % 2
                p_ctr[0] += 1
                proj_mms(qwT_sb, qbc_sb, zsl, psl)
                run_pending()
                pending_fin[0] = make_fin(
                    psl, True,
                    qT_s[:, :, gi * 512:(gi + 1) * 512].rearrange("a p t -> p a t"))
                if gi < 7:
                    nxt[0] = ln_pair(t0 + 4, t0 + 5)
            psl = p_ctr[0] % 2
            p_ctr[0] += 1
            proj_mms(kwT_sb, kbc_sb, zsl, psl)
            run_pending()
            pending_fin[0] = make_fin(
                psl, False,
                kT_s[:, :, gi * 512:(gi + 1) * 512].rearrange("a p t -> p a t"))
            if gi < 7 and nxt[0] is None:
                nxt[0] = ln_pair(t0 + 4, t0 + 5)
            elif gi < 7:
                nxt[1] = ln_pair(t0 + 6, t0 + 7)
            for ti in range(4):
                tt = gi * 4 + ti
                st2 = ti
                ev_last = 0
                for oc2 in range(2):
                    sl = (ti * 2 + oc2) % 2
                    for dc in range(8):
                        tvv = P.op("tensor", lambda e, d=dc, t=ti, o=oc2, s=sl, zz=zsl: e.matmul(
                            ps_proj[s][:], zT[zz][:, d, t * 128:(t + 1) * 128],
                            vwT_sb[:, d, o * 512:(o + 1) * 512],
                            start=(d == 0), stop=(d == 7)),
                            waits=[("sE", zT_ready[0]), ("vE", vproj_evac[sl]), ("sE", proj_evac[sl])],
                            inc="tE" if dc == 7 else None)
                    vproj_evac[sl] = P.op("vector", lambda e, o=oc2, s=sl, g=st2: e.tensor_copy(
                        vstg[g][:].rearrange("p (a b) -> p a b", a=16)[:, o * 8:(o + 1) * 8, 0:64],
                        ps_proj[s][:].rearrange("p (a t) -> p a t", a=8)),
                        waits=[("tE", tvv), (f"svau{st2}", vstg_free[st2])], inc="vE")
                    ev_last = vproj_evac[sl]
                vstg_free[st2] = P.dma("sync", lambda e, t=tt, g=st2: e.dma_start(
                    out=vAug_s[:, :, t, :].rearrange("a p b -> p a b"),
                    in_=vstg[g][:].rearrange("p (a b) -> p a b", a=8)),
                    waits=[("vE", ev_last)], sem=f"svau{st2}")
            zT_used[zsl] = P.now("tE")
            run_pending()
            if gi < 7 and nxt[1] is None:
                nxt[1] = ln_pair(t0 + 6, t0 + 7)
            pend_ln[0] = (nxt[0], nxt[1]) if gi < 7 else None

        vcb = P.op("vector", lambda e: e.tensor_copy(cbf[:], c_sb[:]), waits=[("vE", P.now("vE"))], inc="vE")
        for tt in range(16):
            sl = tt % 2
            tvv = P.op("tensor", lambda e, t=tt, s=sl: e.transpose(
                ps_tr[s][0:16, :], cbf[:, t * 16:(t + 1) * 16], ident_sb[:]),
                waits=[("vE", vcb), ("sE", tr_evac[sl])], inc="tE")
            tr_evac[sl] = P.op("scalar", lambda e, t=tt, s=sl: e.copy(
                cT_sb[0:16, t * 128:(t + 1) * 128], ps_tr[s][0:16, :]),
                waits=[("tE", tvv)], inc="sE")
        if "c" in dbg:
            P.dma("sync", lambda e: e.dma_start(out=c_dbg, in_=c_sb[:]), waits=[("vE", P.now("vE"))], sem="dbg")

        P.barrier()
        PCTX.close()

        if "A" not in stages:
            finish()
            return nc

        FCTX = ExitStack()
        FE = FCTX.enter_context
        h1T = FE(nc.sbuf_tensor("h1T", [128, 8, 2048], BF16))
        w1b = [FE(nc.sbuf_tensor(f"w1b{i}", [128, 8, 1024], BF16)) for i in range(2)]
        w2b = [FE(nc.sbuf_tensor(f"w2b{i}", [128, 8, 1024], BF16)) for i in range(2)]
        b1sb = [FE(nc.sbuf_tensor(f"b1sb{i}", [128, 16], F32)) for i in range(2)]
        b2sb = FE(nc.sbuf_tensor("b2sb", [16, 1024], BF16))
        ffst = [FE(nc.sbuf_tensor(f"ffst{i}", [128, 1024], F32)) for i in range(2)]
        qToc = [FE(nc.sbuf_tensor(f"qToc{i}", [128, 2048], BF16)) for i in range(2)]
        kToc = [FE(nc.sbuf_tensor(f"kToc{i}", [128, 4096], BF16)) for i in range(2)]
        vAugP = [FE(nc.sbuf_tensor(f"vAugP{i}", [128, 32, 130], BF16)) for i in range(2)]
        PT = [FE(nc.sbuf_tensor(f"PT{i}", [128, 512], BF16)) for i in range(2)]
        den_sb = [FE(nc.sbuf_tensor(f"den_sb{i}", [1, 512], F32)) for i in range(2)]
        rden = [FE(nc.sbuf_tensor(f"rden{i}", [1, 512], F32)) for i in range(2)]
        rdb = FE(nc.sbuf_tensor("rdb", [64, 512], F32))
        otn_st = [FE(nc.sbuf_tensor(f"otn_st{i}", [128, 512], BF16)) for i in range(2)]
        ps_h2 = [FE(nc.psum_tensor(f"ps_h{i}", [128, 512], F32)) for i in range(2)]
        ps_o2 = [FE(nc.psum_tensor(f"ps_o{i}", [128, 512], F32)) for i in range(2)]
        ps_s2 = [FE(nc.psum_tensor(f"ps_s{i}", [128, 512], F32)) for i in range(2)]
        ps_av = [FE(nc.psum_tensor(f"ps_av{i}", [128, 512], F32)) for i in range(2)]
        ps_rd = ps_s2[0][0:64, 0:512]

        b2_rd = P.dma("gpsimd", lambda e: e.dma_start(out=b2sb[:], in_=b2d), sem="sb2")

        kq_free = [0, 0]
        kT_rd = [0, 0]
        qT_rd = [0, 0]
        va_rd = [0, 0]
        exp_done = [0, 0]
        av_done = [0, 0]
        otn_free = [0, 0]
        avps_free = [0, 0]
        rdb_free = [0]
        rden_free = [0, 0]

        def load_oc(o):
            sl = o % 2
            kT_rd[sl] = P.dma("sync", lambda e, oo=o, s=sl: e.dma_start(out=kToc[s][:], in_=kT_s[oo]),
                              waits=[("tE", kq_free[sl])], sem=f"skT{sl}")
            qT_rd[sl] = P.dma("sync", lambda e, oo=o, s=sl: e.dma_start(out=qToc[s][:], in_=qT_s[oo]),
                              waits=[("tE", kq_free[sl])], sem=f"sqT{sl}")
            va_rd[sl] = P.dma("sync", lambda e, oo=o, s=sl: e.dma_start(
                out=vAugP[s][:], in_=vAug_s[oo]),
                waits=[("tE", kq_free[sl])], sem=f"sva{sl}")

        def a_gen():
            load_oc(0)
            load_oc(1)
            for oc in range(8):
                sl = oc % 2

                def scores(hh, m, n, s_=sl):
                    pb = hh * 64
                    tp = (64, 0) if hh else (0, 0)
                    return P.op("tensor", lambda e, s=hh, mm=m, p=pb, t=tp, nn=n, sll=s_: e.matmul(
                        ps_s2[s][:], kToc[sll][p:p + 64, mm * 128:(mm + 1) * 128],
                        qToc[sll][p:p + 64, nn * 512:(nn + 1) * 512],
                        start=True, stop=True, tile_position=t),
                        waits=[(f"skT{s_}", kT_rd[s_]), (f"sqT{s_}", qT_rd[s_]),
                               ("sE", exp_done[hh])] +
                              ([("vE", rdb_free[0])] if hh == 0 else []),
                        inc="tE")

                for n in range(4):
                    sc = [scores(0, 0, n), scores(1, 0, n)]
                    for m in range(32):
                        for hh in range(2):
                            exp_done[hh] = P.op("scalar", lambda e, s=hh: e.activation(
                                PT[s][:], ps_s2[s][:], AF.Exp),
                                waits=[("tE", sc[hh]), ("tE", av_done[hh])], inc="sE")
                        if m + 1 < 32:
                            sc = [scores(0, m + 1, n), scores(1, m + 1, n)]
                        for hh in range(2):
                            av_done[hh] = P.op("tensor", lambda e, s=hh, mm=m, sll=sl: e.matmul(
                                ps_av[s][0:65, :],
                                vAugP[sll][:].rearrange("p a (b c) -> p a b c", b=2)[:, mm, s, :],
                                PT[s][:], start=(mm == 0), stop=(mm == 31)),
                                waits=[("sE", exp_done[hh]), ("vE", avps_free[hh]), (f"sva{sl}", va_rd[sl])],
                                inc="tE")
                        yield
                    for hh in range(2):
                        pb = hh * 64
                        sd = P.op("scalar", lambda e, a=hh: e.copy(den_sb[a][:], ps_av[a][64:65, :]),
                                  waits=[("tE", av_done[hh])], inc="sE")
                        sd = P.op("scalar", lambda e, a=hh: e.activation(rden[a][:], den_sb[a][:], AF.Ln),
                                  waits=[("sE", sd), ("tE", rden_free[hh])], inc="sE")
                        sd = P.op("scalar", lambda e, a=hh: e.activation(rden[a][:], rden[a][:], AF.Exp, scale=-1.0),
                                  waits=[("sE", sd)], inc="sE")
                        tb = P.op("tensor", lambda e, a=hh: e.matmul(ps_rd, ones_rf[0:1, 0:64], rden[a][:], start=True, stop=True),
                                  waits=[("sE", sd), ("vE", rdb_free[0]), ("sE", exp_done[0])], inc="tE")
                        rden_free[hh] = tb
                        vv = P.op("vector", lambda e: e.tensor_copy(rdb[:], ps_rd), waits=[("tE", tb)], inc="vE")
                        rdb_free[0] = vv
                        avps_free[hh] = P.op("vector", lambda e, p=pb, a=hh, s=n % 2: e.tensor_tensor(
                            otn_st[s][p:p + 64, :], ps_av[a][0:64, :], rdb[:], OP.mult),
                            waits=[("vE", vv), (f"sotn{n % 2}", otn_free[n % 2])], inc="vE")
                    vadd = P.op("vector", lambda e, o=oc, s=n % 2: e.tensor_scalar_add(
                        otn_st[s][:], otn_st[s][:], rcol[:, o:o + 1]),
                        waits=[("vE", avps_free[0]), ("vE", avps_free[1])], inc="vE")
                    otn_free[n % 2] = P.dma("sync", lambda e, o=oc, nn=n, s=n % 2: e.dma_start(
                        out=otn_s[o, :, nn * 512:(nn + 1) * 512], in_=otn_st[s][:]),
                        waits=[("vE", vadd)], sem=f"sotn{n % 2}")
                    yield
                kq_free[sl] = max(sc[0], sc[1], av_done[0], av_done[1], rden_free[0], rden_free[1])
                if oc + 2 < 8:
                    load_oc(oc + 2)

        agen = a_gen()

        def inject(k=1):
            for _ in range(k):
                next(agen, None)

        o_done = [0, 0]
        acc_init_done = 0
        acc_cnt = [0] * 16
        ffst_free = [("sacc0", 0), ("sacc0", 0)]
        ob_ctr = [0]
        for tt in range(16):
            sl = tt % 2
            evs = 0
            for oc2 in range(2):
                ob = ob_ctr[0] % 2
                ob_ctr[0] += 1
                tvv = P.op("tensor", lambda e, t=tt, o=oc2, b=ob: e.matmul(
                    ps_o2[b][:], cT_sb[0:16, t * 128:(t + 1) * 128],
                    b2sb[0:16, o * 512:(o + 1) * 512], start=True, stop=True),
                    waits=[("sb2", b2_rd), ("vE", o_done[ob])], inc="tE")
                evs = P.op("vector", lambda e, o=oc2, s=sl, b=ob: e.tensor_copy(
                    ffst[s][:, o * 512:(o + 1) * 512], ps_o2[b][:]),
                    waits=[("tE", tvv), ffst_free[sl]], inc="vE")
                o_done[ob] = evs
            acc_cnt[tt] += 1
            v = P.dma("gpsimd", lambda e, t=tt, s=sl: e.dma_start(
                out=acc_d[t], in_=ffst[s][:]),
                waits=[("vE", evs)], sem=f"sacc{tt}")
            ffst_free[sl] = (f"sacc{tt}", v)
        acc_init_done = P.now("vE")

        w1_rd = [0, 0]
        w2_rd = [0, 0]
        w1_free = [0, 0]
        w2_free = [0, 0]
        b1_rd = [0, 0]
        b1_free = [0, 0]
        h_done = [0, 0]

        def load_w(e_, hf):
            sp = (2 * e_ + hf) % 2
            w1_rd[sp] = P.dma("gpsimd", lambda e, ee=e_, h=hf, s=sp: e.dma_start(
                out=w1b[s][:], in_=w1d[ee].rearrange("a p n -> p a n")[:, :, h * 1024:(h + 1) * 1024]),
                waits=[("tE", w1_free[sp])], sem=f"sw1{sp}")
            w2_rd[sp] = P.dma("gpsimd", lambda e, ee=e_, h=hf, s=sp: e.dma_start(
                out=w2b[s][:], in_=w2d[ee, h * 8:(h + 1) * 8].rearrange("a p n -> p a n")),
                waits=[("tE", w2_free[sp])], sem=f"sw2{sp}")
            if hf == 0:
                b1_rd[e_ % 2] = P.dma("sync", lambda e, ee=e_: e.dma_start(
                    out=b1sb[ee % 2][:], in_=b1c[ee]), waits=[("sE", b1_free[e_ % 2])], sem=f"sb1{e_ % 2}")

        load_w(0, 0)
        load_w(0, 1)

        hb_ctr = [0]
        for e_ in range(16):
            for hf in range(2):
                sp = (2 * e_ + hf) % 2
                for ic in range(8):
                    for np2 in range(2):
                        for sb2 in range(2):
                            hb = hb_ctr[0] % 2
                            hb_ctr[0] += 1
                            for dc in range(8):
                                tvv = P.op("tensor", lambda e, d=dc, i=ic, n2=np2, s2=sb2, p=sp, b=hb: e.matmul(
                                    ps_h2[b][:],
                                    w1b[p][:, d, i * 128:(i + 1) * 128],
                                    xnT_sb[:, d, n2 * 1024 + s2 * 512:n2 * 1024 + (s2 + 1) * 512],
                                    start=(d == 0), stop=(d == 7)),
                                    waits=[(f"sw1{sp}", w1_rd[sp]), (f"sb1{e_ % 2}", b1_rd[e_ % 2]),
                                           ("sE", h_done[hb]),
                                           ("vE", acc_init_done)],
                                    inc="tE" if dc == 7 else None)
                            h_done[hb] = P.op("scalar", lambda e, i=ic, n2=np2, s2=sb2, ee=e_, h=hf, b=hb: e.activation(
                                h1T[:, i, n2 * 1024 + s2 * 512:n2 * 1024 + (s2 + 1) * 512], ps_h2[b][:], AF.Gelu,
                                bias=b1sb[ee % 2][:, h * 8 + i:h * 8 + i + 1]),
                                waits=[("tE", tvv)], inc="sE")
                w1_free[sp] = P.now("tE")
                gelu_all = P.now("sE")
                if hf == 1:
                    b1_free[e_ % 2] = gelu_all
                for tt in range(16):
                    sl = tt % 2
                    evs = 0
                    for oc2 in range(2):
                        ob = ob_ctr[0] % 2
                        ob_ctr[0] += 1
                        for ic in range(8):
                            tvv = P.op("tensor", lambda e, i=ic, t=tt, o=oc2, p=sp, b=ob: e.matmul(
                                ps_o2[b][:],
                                h1T[:, i, t * 128:(t + 1) * 128],
                                w2b[p][:, i, o * 512:(o + 1) * 512],
                                start=(i == 0), stop=(i == 7)),
                                waits=[("sE", gelu_all), (f"sw2{sp}", w2_rd[sp]), ("vE", o_done[ob])],
                                inc="tE" if ic == 7 else None)
                        evs = P.op("vector", lambda e, t=tt, o=oc2, s=sl, ee=e_, b=ob: e.tensor_single_scalar(
                            ffst[s][:, o * 512:(o + 1) * 512], ps_o2[b][:],
                            c_sb[:, t * 16 + ee:t * 16 + ee + 1], OP.mult),
                            waits=[("tE", tvv), ffst_free[sl]], inc="vE")
                        o_done[ob] = evs
                        inject(1)
                    acc_cnt[tt] += 1
                    v = P.dma("gpsimd", lambda e, t=tt, s=sl: e.dma_start(
                        out=acc_d[t], in_=ffst[s][:], accum_op=OP.add),
                        waits=[("vE", evs)], sem=f"sacc{tt}")
                    ffst_free[sl] = (f"sacc{tt}", v)
                w2_free[sp] = P.now("tE")
                nxt = 2 * e_ + hf + 2
                if nxt < 32:
                    load_w(nxt // 2, nxt % 2)

        for _ in agen:
            pass

        P.barrier()
        FCTX.close()

        if "D" not in stages:
            finish()
            return nc

        DCTX = ExitStack()
        DE = DCTX.enter_context
        NSL = 3
        acc_t = [DE(nc.sbuf_tensor(f"acc_t{i}", [128, 1024], F32)) for i in range(NSL)]
        xq_t = [DE(nc.sbuf_tensor(f"xq_t{i}", [128, 1024], F32)) for i in range(NSL)]
        otn_t = [DE(nc.sbuf_tensor(f"otn_t{i}", [128, 8, 128], BF16)) for i in range(NSL)]
        outb = [DE(nc.sbuf_tensor(f"outb{i}", [128, 1024], F32)) for i in range(NSL)]
        owT_sb = DE(nc.sbuf_tensor("owT_sb", [128, 8, 1024], BF16))
        ob_bc = DE(nc.sbuf_tensor("ob_bc", [128, 1024], F32))
        obrow = DE(nc.sbuf_tensor("obrow", [1, 1024], F32))
        t_xn = DE(nc.sbuf_tensor("t_xn", [128, 512], F32))
        t_s = DE(nc.sbuf_tensor("t_s", [128, 512], F32))
        ps_d = [DE(nc.psum_tensor(f"ps_d{i}", [128, 1024], F32)) for i in range(2)]
        ps_b = DE(nc.psum_tensor("ps_b", [128, 512], F32))

        ow_rd = P.dma("gpsimd", lambda e: e.dma_start(
            out=owT_sb[:], in_=owT.rearrange("(a p) n -> p a n", p=128)), sem="sow")
        ob_rd = P.dma("sync", lambda e: e.dma_start(out=obrow[:], in_=ob_r), sem="sob")
        for half in range(2):
            tb = P.op("tensor", lambda e, hf=half: e.matmul(
                ps_b[:], ones_rf[:], obrow[:, hf * 512:(hf + 1) * 512], start=True, stop=True),
                waits=[("sob", ob_rd), ("vE", P.now("vE"))], inc="tE")
            P.op("vector", lambda e, hf=half: e.tensor_copy(ob_bc[:, hf * 512:(hf + 1) * 512], ps_b[:]),
                 waits=[("tE", tb)], inc="vE")

        otn_fin = [(s, P.now(s)) for s in ("sotn0", "sotn1")]
        acc_ld = [0] * NSL
        xq_ld = [0] * NSL
        otn_ld = [0] * NSL
        accq_free = [0] * NSL
        xq_free = [0] * NSL
        otn_mm_free = [0] * NSL
        out_free = [0] * NSL
        d_done = [0, 0]

        def load_tile(tt):
            slot = tt % NSL
            acc_ld[slot] = P.dma("sync", lambda e, t=tt, s=slot: e.dma_start(
                out=acc_t[s][:], in_=acc_d[t]),
                waits=[(f"sacc{tt}", acc_cnt[tt] * 16), ("vE", accq_free[slot])],
                sem=f"sdac{slot}")
            xq_ld[slot] = P.dma("gpsimd", lambda e, t=tt, s=slot: e.dma_start(
                out=xq_t[s][:], in_=x[t * 128:(t + 1) * 128, :]),
                waits=[("vE", xq_free[slot])], sem=f"sdxq{slot}")
            otn_ld[slot] = P.dma("sync", lambda e, t=tt, s=slot: e.dma_start(
                out=otn_t[s][:], in_=otn_s[:, :, t * 128:(t + 1) * 128].rearrange("a p t -> p a t")),
                waits=[otn_fin[0], otn_fin[1], ("tE", otn_mm_free[slot])],
                sem=f"sdot{slot}")

        for tt in range(NSL):
            load_tile(tt)

        for tt in range(16):
            slot = tt % NSL
            sil = P.op("scalar", lambda e, s=slot: e.activation(acc_t[s][:], acc_t[s][:], AF.Silu),
                       waits=[(f"sdac{slot}", acc_ld[slot])], inc="sE")
            for oc2 in range(2):
                sl = (2 * tt + oc2) % 2
                for dc in range(8):
                    tvv = P.op("tensor", lambda e, d=dc, o=oc2, s=sl, st=slot: e.matmul(
                        ps_d[s][:, o * 512:(o + 1) * 512],
                        otn_t[st][:, d, :],
                        owT_sb[:, d, o * 512:(o + 1) * 512],
                        start=(d == 0), stop=(d == 7)),
                        waits=[(f"sdot{slot}", otn_ld[slot]), ("sow", ow_rd), ("vE", d_done[sl])],
                        inc="tE" if dc == 7 else None)
                vv = P.op("vector", lambda e, t=tt, o=oc2, st=slot: e.tensor_scalar(
                    t_xn[:], xq_t[st][:, o * 512:(o + 1) * 512], negmu1[:, t:t + 1], r1sb[:, t:t + 1],
                    OP.add, OP.mult),
                    waits=[(f"sdxq{slot}", xq_ld[slot]), ("tE", tvv), ("sE", sil)], inc="vE")
                vv = P.op("vector", lambda e, o=oc2, s=sl: e.scalar_tensor_tensor(
                    t_s[:], ps_d[s][:, o * 512:(o + 1) * 512], 1.0, ob_bc[:, o * 512:(o + 1) * 512],
                    OP.mult, OP.add),
                    waits=[("vE", vv)], inc="vE")
                d_done[sl] = vv
                vv = P.op("vector", lambda e: e.tensor_add(t_s[:], t_s[:], t_xn[:]),
                          waits=[("vE", vv)], inc="vE")
                vv = P.op("vector", lambda e, o=oc2, st=slot: e.tensor_mul(
                    t_s[:], t_s[:], acc_t[st][:, o * 512:(o + 1) * 512]),
                    waits=[("vE", vv)], inc="vE")
                accq_free[slot] = vv
                vv = P.op("vector", lambda e, o=oc2, st=slot: e.tensor_add(
                    outb[st][:, o * 512:(o + 1) * 512], t_s[:], xq_t[st][:, o * 512:(o + 1) * 512]),
                    waits=[("vE", vv), (f"sdout{slot}", out_free[slot])], inc="vE")
                xq_free[slot] = vv
            otn_mm_free[slot] = P.now("tE")
            out_free[slot] = P.dma("sync", lambda e, t=tt, s=slot: e.dma_start(
                out=out[t * 128:(t + 1) * 128, :], in_=outb[s][:]),
                waits=[("vE", xq_free[slot])], sem=f"sdout{slot}")
            if tt + NSL < 16:
                load_tile(tt + NSL)

        P.barrier()

        finish()
        DCTX.close()
    return nc


def _host_inputs(kw):
    f32 = np.float32

    def C(a):
        return np.ascontiguousarray(np.asarray(a), dtype=f32)

    def col8(v):
        return C(np.asarray(v).reshape(8, 128).T)

    base = {
        "qwT": C(np.asarray(kw["L0_attn_q_w"]).T),
        "kwT": C(np.asarray(kw["L0_attn_k_w"]).T),
        "vwT": C(np.asarray(kw["L0_attn_v_w"]).T),
        "owT": C(np.asarray(kw["L0_attn_o_w"]).T),
        "g_q": col8(kw["L0_attn_ln_q_g"]), "b_qn": col8(kw["L0_attn_ln_q_b"]),
        "g_k": col8(kw["L0_attn_ln_k_g"]), "b_kn": col8(kw["L0_attn_ln_k_b"]),
        "g_v": col8(kw["L0_attn_ln_v_g"]), "b_vn": col8(kw["L0_attn_ln_v_b"]),
        "qb_c": col8(kw["L0_attn_q_b"]), "kb_c": col8(kw["L0_attn_k_b"]),
        "vb_c": col8(kw["L0_attn_v_b"]),
        "ob_r": C(np.asarray(kw["L0_attn_o_b"]).reshape(1, 1024)),
        "beta_r": C(np.asarray(kw["L0_attn_beta_param"]).reshape(1, 16)),
        "rw": C(np.asarray(kw["L0_ffn_router_w"]).reshape(8, 128, 16).transpose(1, 0, 2)),
        "rb_r": C(np.asarray(kw["L0_ffn_router_b"]).reshape(1, 16)),
        "w1": C(np.asarray(kw["L0_ffn_w1"]).reshape(16, 8, 128, 2048)),
        "b1c": C(np.asarray(kw["L0_ffn_b1"]).reshape(16, 16, 128).transpose(0, 2, 1)),
        "w2": C(np.asarray(kw["L0_ffn_w2"]).reshape(16, 16, 128, 1024)),
        "b2": C(np.asarray(kw["L0_ffn_b2"]).reshape(16, 1024)),
        "ident": np.eye(128, dtype=ml_dtypes.bfloat16),
        "ident32": np.eye(128, dtype=np.float32),
    }
    x = np.asarray(kw["L0_attn_query"], dtype=f32)
    in_maps = []
    for c in range(8):
        b, hf = c // 2, c % 2
        xb = x[b]
        xc = xb if hf == 0 else np.concatenate([xb[2048:], xb[:2048]], axis=0)
        in_maps.append(dict(base, x=np.ascontiguousarray(xc)))
    return in_maps


def kernel(**kw):
    in_maps = _host_inputs(kw)
    nc = build()
    res = run_bass_kernel_spmd(nc, in_maps, core_ids=list(range(8)))
    out = np.empty((4, 4096, 1024), dtype=np.float32)
    for c in range(8):
        b, hf = c // 2, c % 2
        out[b, hf * 2048:(hf + 1) * 2048, :] = res.results[c]["out"]
    return out


# version 22
# speedup vs baseline: 1.0017x; 1.0017x over previous
import numpy as np
import ml_dtypes
from contextlib import ExitStack
import concourse.bass as bass
import concourse.mybir as mybir
from concourse.bass_utils import run_bass_kernel_spmd
from concourse.alu_op_type import AluOpType as OP

F32 = mybir.dt.float32
BF16 = mybir.dt.bfloat16
AF = mybir.ActivationFunctionType
AX = mybir.AxisListType

ENGINES = ("sync", "gpsimd", "scalar", "vector", "tensor")


class Prog:

    def __init__(self, nc, ctx):
        self.nc = nc
        self.ctx = ctx
        self.streams = {e: [] for e in ENGINES}
        self.sems = {}
        self.cnt = {}
        self.dma_sems = set()
        for name in ("tE", "vE", "sE", "gE"):
            self._mk(name)

    def _mk(self, name):
        self.sems[name] = self.ctx.enter_context(self.nc.semaphore(name))
        self.cnt[name] = 0

    def now(self, sem):
        return self.cnt[sem]

    def op(self, eng, fn, waits=(), inc=None):
        w = tuple((s, v) for (s, v) in waits if v > 0)
        if inc is not None:
            self.cnt[inc] += 16 if inc in self.dma_sems else 1
        self.streams[eng].append((w, fn, inc))
        return self.cnt[inc] if inc else None

    def dma(self, eng, fn, waits=(), sem="cS"):
        if sem not in self.sems:
            self._mk(sem)
            self.dma_sems.add(sem)
        return self.op(eng, fn, waits, inc=sem)

    def barrier(self):
        snap = [(s, self.cnt[s]) for s in self.sems if self.cnt[s] > 0]
        for eng in ENGINES:
            self.streams[eng].append((tuple(snap), None, None))


def _emit_body(items, sems, dma_sems, e):
    last = {}
    for waits, fn, inc in items:
        for s, v in waits:
            if last.get(s, 0) < v:
                e.wait_ge(sems[s], v)
                last[s] = v
        if fn is None:
            continue
        ins = fn(e)
        if inc is not None:
            ins.then_inc(sems[inc], 16 if inc in dma_sems else 1)


def build(dbg=(), stages=("P", "A", "B", "D"), router=True, fold=True):
    nc = bass.Bass("TRN2", target_bir_lowering=False)

    x = nc.dram_tensor("x", [4096, 1024], F32, kind="ExternalInput").ap()
    qwT = nc.dram_tensor("qwT", [1024, 1024], F32, kind="ExternalInput").ap()
    kwT = nc.dram_tensor("kwT", [1024, 1024], F32, kind="ExternalInput").ap()
    vwT = nc.dram_tensor("vwT", [1024, 1024], F32, kind="ExternalInput").ap()
    owT = nc.dram_tensor("owT", [1024, 1024], F32, kind="ExternalInput").ap()
    g_q = nc.dram_tensor("g_q", [128, 8], F32, kind="ExternalInput").ap()
    b_qn = nc.dram_tensor("b_qn", [128, 8], F32, kind="ExternalInput").ap()
    g_k = nc.dram_tensor("g_k", [128, 8], F32, kind="ExternalInput").ap()
    b_kn = nc.dram_tensor("b_kn", [128, 8], F32, kind="ExternalInput").ap()
    g_v = nc.dram_tensor("g_v", [128, 8], F32, kind="ExternalInput").ap()
    b_vn = nc.dram_tensor("b_vn", [128, 8], F32, kind="ExternalInput").ap()
    qb_c = nc.dram_tensor("qb_c", [128, 8], F32, kind="ExternalInput").ap()
    kb_c = nc.dram_tensor("kb_c", [128, 8], F32, kind="ExternalInput").ap()
    vb_c = nc.dram_tensor("vb_c", [128, 8], F32, kind="ExternalInput").ap()
    ob_r = nc.dram_tensor("ob_r", [1, 1024], F32, kind="ExternalInput").ap()
    beta_r = nc.dram_tensor("beta_r", [1, 16], F32, kind="ExternalInput").ap()
    rw = nc.dram_tensor("rw", [128, 8, 16], F32, kind="ExternalInput").ap()
    rb_r = nc.dram_tensor("rb_r", [1, 16], F32, kind="ExternalInput").ap()
    w1d = nc.dram_tensor("w1", [16, 8, 128, 2048], F32, kind="ExternalInput").ap()
    b1c = nc.dram_tensor("b1c", [16, 128, 16], F32, kind="ExternalInput").ap()
    w2d = nc.dram_tensor("w2", [16, 16, 128, 1024], F32, kind="ExternalInput").ap()
    b2d = nc.dram_tensor("b2", [16, 1024], F32, kind="ExternalInput").ap()
    identd = nc.dram_tensor("ident", [128, 128], BF16, kind="ExternalInput").ap()
    ident32d = nc.dram_tensor("ident32", [128, 128], F32, kind="ExternalInput").ap()
    out = nc.dram_tensor("out", [2048, 1024], F32, kind="ExternalOutput").ap()

    def scratch(name, shape):
        kind = "ExternalOutput" if name in dbg else "Internal"
        return nc.dram_tensor(name, shape, BF16, kind=kind).ap()

    qT_s = scratch("qT_s", [8, 128, 2048])
    kT_s = scratch("kT_s", [8, 128, 4096])
    otn_s = scratch("otn_s", [8, 128, 2048])
    vAug_s = nc.dram_tensor("vAug_s", [8, 128, 32, 130], BF16,
                            kind="ExternalOutput" if "vaug" in dbg else "Internal").ap()
    acc_d = nc.dram_tensor("acc_d", [16, 128, 1024], F32,
                           kind="ExternalOutput" if "acc" in dbg else "Internal").ap()
    if "c" in dbg:
        c_dbg = nc.dram_tensor("c_dbg", [128, 256], F32, kind="ExternalOutput").ap()

    CTX = ExitStack()
    with CTX:
        P = Prog(nc, CTX)

        def finish():
            with nc.Block() as block:
                for eng in ENGINES:
                    items = P.streams[eng]
                    if items:
                        getattr(block, eng)(lambda e, it=items: _emit_body(it, P.sems, P.dma_sems, e))

        GE = CTX.enter_context
        ident_sb = GE(nc.sbuf_tensor("ident_sb", [128, 128], BF16))
        ones_cb = GE(nc.sbuf_tensor("ones_cb", [128, 1], BF16))
        ones_rf = GE(nc.sbuf_tensor("ones_rf", [1, 128], F32))
        Bb = GE(nc.sbuf_tensor("Bb", [128, 16], F32))
        c_sb = GE(nc.sbuf_tensor("c_sb", [128, 256], F32))
        cT_sb = GE(nc.sbuf_tensor("cT_sb", [16, 2048], BF16))
        negmu1 = GE(nc.sbuf_tensor("negmu1", [128, 16], F32))
        rcol = GE(nc.sbuf_tensor("rcol", [128, 8], F32))
        xnT_sb = GE(nc.sbuf_tensor("xnT_sb", [128, 8, 2048], BF16))
        r1sb = GE(nc.sbuf_tensor("r1sb", [128, 16], F32))

        PCTX = ExitStack()
        PE_ = PCTX.enter_context
        qwT_sb = PE_(nc.sbuf_tensor("qwT_sb", [128, 8, 1024], BF16))
        kwT_sb = PE_(nc.sbuf_tensor("kwT_sb", [128, 8, 1024], BF16))
        vwT_sb = PE_(nc.sbuf_tensor("vwT_sb", [128, 8, 1024], BF16))
        rw_sb = PE_(nc.sbuf_tensor("rw_sb", [128, 8, 16], F32))
        ident32_sb = PE_(nc.sbuf_tensor("ident32_sb", [128, 128], F32))
        xnT32 = PE_(nc.sbuf_tensor("xnT32", [128, 8, 128], F32))
        gq_sb = PE_(nc.sbuf_tensor("gq_sb", [128, 8], F32))
        bq_sb = PE_(nc.sbuf_tensor("bq_sb", [128, 8], F32))
        gk_sb = PE_(nc.sbuf_tensor("gk_sb", [128, 8], F32))
        bk_sb = PE_(nc.sbuf_tensor("bk_sb", [128, 8], F32))
        gv_sb = PE_(nc.sbuf_tensor("gv_sb", [128, 8], F32))
        bv_sb = PE_(nc.sbuf_tensor("bv_sb", [128, 8], F32))
        qbc_sb = PE_(nc.sbuf_tensor("qbc_sb", [128, 8], F32))
        kbc_sb = PE_(nc.sbuf_tensor("kbc_sb", [128, 8], F32))
        rb_bc = PE_(nc.sbuf_tensor("rb_bc", [128, 16], F32))
        brow = PE_(nc.sbuf_tensor("brow", [1, 16], F32))
        rbrow = PE_(nc.sbuf_tensor("rbrow", [1, 16], F32))
        sprow = PE_(nc.sbuf_tensor("sprow", [1, 16], F32))
        sprow2 = PE_(nc.sbuf_tensor("sprow2", [1, 16], F32))
        xin = [PE_(nc.sbuf_tensor(f"xin{i}", [128, 1024], F32)) for i in range(2)]
        xwork2 = [PE_(nc.sbuf_tensor(f"xwork{i}", [128, 1024], F32)) for i in range(4)]
        xsq2 = [PE_(nc.sbuf_tensor(f"xsq{i}", [128, 1024], F32)) for i in range(2)]
        xscr = PE_(nc.sbuf_tensor("xscr", [128, 1024], F32))
        zb2 = [PE_(nc.sbuf_tensor(f"zb{i}", [128, 1024], BF16)) for i in range(4)]
        zT = [PE_(nc.sbuf_tensor(f"zT{i}", [128, 8, 512], BF16)) for i in range(2)]
        fT_st = PE_(nc.sbuf_tensor("fT_st", [128, 8, 512], BF16))
        vstg = [PE_(nc.sbuf_tensor(f"vstg{i}", [128, 1040], BF16)) for i in range(4)]
        proj = [PE_(nc.sbuf_tensor(f"proj{i}", [128, 8, 512], BF16)) for i in range(2)]
        sqb = fT_st
        rnb = PE_(nc.sbuf_tensor("rnb", [128, 512], F32))
        nrm = PE_(nc.sbuf_tensor("nrm", [1, 512], F32))
        colb = PE_(nc.sbuf_tensor("colb", [128, 20], F32))
        eps_cell = PE_(nc.sbuf_tensor("eps_cell", [128, 1], F32))
        lgt2 = [PE_(nc.sbuf_tensor(f"lgt{i}", [128, 16], F32)) for i in range(2)]
        p162 = [PE_(nc.sbuf_tensor(f"p16_{i}", [128, 16], F32)) for i in range(2)]
        cur2 = [PE_(nc.sbuf_tensor(f"cur{i}", [128, 16], F32)) for i in range(2)]
        msk2 = [PE_(nc.sbuf_tensor(f"msk{i}", [128, 16], F32)) for i in range(2)]
        cbf = PE_(nc.sbuf_tensor("cbf", [128, 256], BF16))

        ps_proj = [PE_(nc.psum_tensor(f"ps_proj{i}", [128, 512], F32)) for i in range(2)]
        ps_tr = [PE_(nc.psum_tensor(f"ps_tr{i}", [128, 128], BF16)) for i in range(2)]
        ps_tr32 = [PE_(nc.psum_tensor(f"ps_tr32{i}", [128, 128], F32)) for i in range(2)]
        ps_rb = PE_(nc.psum_tensor("ps_rb", [128, 512], F32))
        ps_ss = ps_rb[0:1, 0:512]
        ps_rt = PE_(nc.psum_tensor("ps_rt", [128, 16], F32))

        tr_evac = [0, 0]
        tr32_evac = [0, 0]
        xwork_free = [0, 0, 0, 0]
        zb_free = [0, 0, 0, 0]
        zT_used = [0, 0]
        proj_free = [0, 0]
        zT_ready = [0]
        rt_free = [0]
        nrm_free = [0]
        rnb_free = [0]
        fT_free = [0]
        proj_evac = [0, 0]
        vproj_evac = [0, 0]
        xin_ready = [0, 0]
        xin_free = [0, 0]
        vstg_free = [0, 0, 0, 0]

        P.dma("sync", lambda e: e.dma_start(out=ident_sb[:], in_=identd), sem="cS")
        P.dma("sync", lambda e: e.dma_start(out=ident32_sb[:], in_=ident32d), sem="cS")
        P.dma("sync", lambda e: e.dma_start(out=rw_sb[:], in_=rw), sem="cS")
        for dst, src in ((gq_sb, g_q), (bq_sb, b_qn), (gk_sb, g_k), (bk_sb, b_kn),
                         (gv_sb, g_v), (bv_sb, b_vn), (qbc_sb, qb_c), (kbc_sb, kb_c)):
            P.dma("sync", lambda e, d=dst, s=src: e.dma_start(out=d[:], in_=s), sem="cS")
        P.dma("sync", lambda e: e.dma_start(out=brow[:], in_=beta_r), sem="cS")
        P.dma("sync", lambda e: e.dma_start(out=rbrow[:], in_=rb_r), sem="cS")
        dvb = P.dma("sync", lambda e: e.dma_start(out=rcol[:], in_=vb_c), sem="cS")
        for dst, src in ((qwT_sb, qwT), (kwT_sb, kwT), (vwT_sb, vwT)):
            P.dma("gpsimd", lambda e, d=dst, s=src: e.dma_start(
                out=d[:], in_=s.rearrange("(a p) n -> p a n", p=128)), sem="cG")
        cS0 = P.now("cS")
        cG0 = P.now("cG")

        vv0 = P.op("vector", lambda e: e.memset(ones_cb[:], 1.0), inc="vE")
        vv0 = P.op("vector", lambda e: e.memset(eps_cell[:], 1e-5), inc="vE")
        nc.const_aps.register(F32, 1e-5, eps_cell[:])
        vv0 = P.op("vector", lambda e: e.memset(ones_rf[:], 1.0), inc="vE")
        for _vi in range(4):
            vv0 = P.op("vector", lambda e, i=_vi: e.memset(vstg[i][:].rearrange("p (a b) -> p a b", a=16)[:, :, 64:65], 1.0), inc="vE")

        t1 = P.op("tensor", lambda e: e.matmul(ps_rt[:], ones_rf[:], rbrow[:], start=True, stop=True),
                  waits=[("cS", cS0), ("vE", vv0)], inc="tE")
        vrb = P.op("vector", lambda e: e.tensor_copy(rb_bc[:], ps_rt[:]), waits=[("tE", t1)], inc="vE")

        s1 = P.op("scalar", lambda e: e.activation(sprow[:], brow[:], AF.Abs), waits=[("cS", cS0)], inc="sE")
        s1 = P.op("scalar", lambda e: e.activation(sprow[:], sprow[:], AF.Exp, scale=-1.0), waits=[("sE", s1)], inc="sE")
        v1b = P.op("vector", lambda e: e.tensor_scalar_add(sprow[:], sprow[:], 1.0), waits=[("sE", s1)], inc="vE")
        s1 = P.op("scalar", lambda e: e.activation(sprow[:], sprow[:], AF.Ln), waits=[("vE", v1b)], inc="sE")
        v2 = P.op("vector", lambda e: e.tensor_scalar_max(sprow2[:], brow[:], 0.0), waits=[("cS", cS0)], inc="vE")
        v2 = P.op("vector", lambda e: e.tensor_add(sprow[:], sprow[:], sprow2[:]), waits=[("sE", s1), ("vE", v2)], inc="vE")
        v2 = P.op("vector", lambda e: e.tensor_scalar(sprow[:], sprow[:], 0.1, 10.0, OP.max, OP.min), waits=[("vE", v2)], inc="vE")
        v2 = P.op("vector", lambda e: e.tensor_scalar_mul(sprow[:], sprow[:], 0.125), waits=[("vE", v2)], inc="vE")
        t2 = P.op("tensor", lambda e: e.matmul(ps_rt[:], ones_rf[:], sprow[:], start=True, stop=True),
                  waits=[("vE", v2)], inc="tE")
        vBb = P.op("vector", lambda e: e.tensor_copy(Bb[:], ps_rt[:]), waits=[("tE", t2)], inc="vE")
        rt_free[0] = vBb

        bqb = PE_(nc.sbuf_tensor("bqb", [128, 8], BF16))
        bkb = PE_(nc.sbuf_tensor("bkb", [128, 8], BF16))
        bvb = PE_(nc.sbuf_tensor("bvb", [128, 8], BF16))
        vbc = 0
        for dst, srcb in ((bqb, bq_sb), (bkb, bk_sb), (bvb, bv_sb)):
            vbc = P.op("vector", lambda e, d=dst, s=srcb: e.tensor_copy(d[:], s[:]),
                       waits=[("cS", cS0)], inc="vE")
        for pi, (wsb, bcol, biasc, cw) in enumerate((
                (qwT_sb, bqb, qbc_sb, 0), (kwT_sb, bkb, kbc_sb, 0), (vwT_sb, bvb, rcol, 8))):
            bank = pi % 2
            for oc in range(8):
                tf = P.op("tensor", lambda e, o=oc, w=wsb, b=bcol, p=bank, c=cw: e.matmul(
                    ps_proj[p][:, c + o:c + o + 1], w[:, 0, o * 128:(o + 1) * 128], b[:, 0:1],
                    start=True, stop=False),
                    waits=[("cG", cG0), ("vE", vbc)], inc=None)
                for d in range(1, 8):
                    tf = P.op("tensor", lambda e, o=oc, w=wsb, b=bcol, p=bank, c=cw, dd=d: e.matmul(
                        ps_proj[p][:, c + o:c + o + 1], w[:, dd, o * 128:(o + 1) * 128], b[:, dd:dd + 1],
                        start=False, stop=(dd == 7)),
                        inc="tE" if (d == 7) else None)
            rnb_free[0] = P.op("vector", lambda e, bi=biasc, p=bank, c=cw: e.tensor_add(
                bi[:], bi[:], ps_proj[p][:, c:c + 8]),
                waits=[("tE", tf), ("cS", max(cS0, dvb))], inc="vE")
        fold_done = P.now("tE")
        wsc = 0
        for wsb, gcol in ((qwT_sb, gq_sb), (kwT_sb, gk_sb), (vwT_sb, gv_sb)):
            for d in range(8):
                wsc = P.op("vector", lambda e, w=wsb, g=gcol, dd=d: e.tensor_single_scalar(
                    w[:, dd, :], w[:, dd, :], g[:, dd:dd + 1], OP.mult),
                    waits=[("tE", fold_done), ("cS", cS0), ("cG", cG0)], inc="vE")
        vproj_evac[0] = vproj_evac[1] = wsc


        def load_x(tt):
            slot = tt % 2
            xin_ready[slot] = P.dma("sync", lambda e, t=tt, s=slot: e.dma_start(
                out=xin[s][:], in_=x[t * 128:(t + 1) * 128, :]),
                waits=[("vE", xin_free[slot])], sem=f"xin{slot}")

        load_x(0)
        load_x(1)

        def transpose_to(dst_fn, src_fn, extra):
            for dc in range(8):
                sl = dc % 2
                tval = P.op("tensor", lambda e, d=dc, s=sl: e.transpose(
                    ps_tr[s][:], src_fn(d), ident_sb[:]),
                    waits=list(extra) + [("sE", tr_evac[sl])], inc="tE")
                tr_evac[sl] = P.op("scalar", lambda e, d=dc, s=sl: e.copy(dst_fn(d), ps_tr[s][:]),
                                   waits=[("tE", tval)], inc="sE")

        def ln_pair(tt_a, tt_b):
            sts = []
            for tt in (tt_a, tt_b):
                p = tt % 2
                cc = [colb[:, 8 * p + j:8 * p + j + 1] for j in range(8)]
                sts.append(dict(
                    tt=tt, p=p, q4=tt % 4, slot=tt % 2, xw=xwork2[tt % 4], xq=xsq2[p], zb=zb2[tt % 4], cc=cc,
                    nm1=negmu1[:, tt:tt + 1] if tt < 16 else cc[6],
                    r1c=r1sb[:, tt:tt + 1] if tt < 16 else cc[7]))
            for st in sts:
                st["v"] = P.op("vector", lambda e, s=st["slot"], a=st["cc"][0]: e.reduce_sum(a, xin[s][:], axis=AX.X),
                               waits=[(f"xin{st['slot']}", xin_ready[st["slot"]])], inc="vE")
            for st in sts:
                st["v"] = P.op("vector", lambda e, d=st["nm1"], a=st["cc"][0]: e.tensor_scalar_mul(d, a, -1.0 / 1024.0),
                               waits=[("vE", st["v"])], inc="vE")
            for st in sts:
                st["v"] = P.op("vector", lambda e, s=st["slot"], d=st["nm1"], w=st["xw"]: e.tensor_scalar_add(w[:], xin[s][:], d),
                               waits=[("vE", st["v"]), ("tE", xwork_free[st["q4"]])], inc="vE")
                xin_free[st["slot"]] = st["v"]
            for st in sts:
                if st["tt"] + 2 < 32:
                    load_x(st["tt"] + 2)
            for st in sts:
                st["s"] = P.op("scalar", lambda e, w=st["xw"], a=st["cc"][1]: e.activation(xscr[:], w[:], AF.Square, accum_out=a),
                               waits=[("vE", st["v"])], inc="sE")
            for st in sts:
                st["s"] = P.op("scalar", lambda e, a=st["cc"][2], b=st["cc"][1]: e.activation(a, b, AF.Ln, scale=1.0 / 1024.0, bias=1e-5),
                               waits=[("sE", st["s"])], inc="sE")
            for st in sts:
                st["s"] = P.op("scalar", lambda e, d=st["r1c"], a=st["cc"][2]: e.activation(d, a, AF.Exp, scale=-0.5),
                               waits=[("sE", st["s"])], inc="sE")
            for st in sts:
                st["v"] = P.op("vector", lambda e, d=st["r1c"], w=st["xw"]: e.tensor_single_scalar(w[:], w[:], d, OP.mult),
                               waits=[("sE", st["s"])], inc="vE")
                st["vxn"] = st["v"]
            for st in sts:
                st["v"] = P.op("vector", lambda e, w=st["xw"], a=st["cc"][3]: e.reduce_sum(a, w[:], axis=AX.X),
                               waits=[("vE", st["v"])], inc="vE")
            for st in sts:
                st["v"] = P.op("vector", lambda e, a=st["cc"][4], b=st["cc"][3]: e.tensor_scalar_mul(a, b, -1.0 / 1024.0),
                               waits=[("vE", st["v"])], inc="vE")
            for st in sts:
                st["v"] = P.op("vector", lambda e, w=st["xw"], q=st["xq"], a=st["cc"][4]: e.tensor_scalar_add(q[:], w[:], a),
                               waits=[("vE", st["v"])], inc="vE")
            for st in sts:
                st["s"] = P.op("scalar", lambda e, q=st["xq"], a=st["cc"][5]: e.activation(xscr[:], q[:], AF.Square, accum_out=a),
                               waits=[("vE", st["v"])], inc="sE")
            for st in sts:
                st["s"] = P.op("scalar", lambda e, a=st["cc"][5]: e.activation(a, a, AF.Ln, scale=1.0 / 1024.0, bias=1e-5),
                               waits=[("sE", st["s"])], inc="sE")
            for st in sts:
                st["s"] = P.op("scalar", lambda e, a=st["cc"][5]: e.activation(a, a, AF.Exp, scale=-0.5),
                               waits=[("sE", st["s"])], inc="sE")
            for st in sts:
                st["vz"] = P.op("vector", lambda e, z=st["zb"], q=st["xq"], a=st["cc"][5]: e.tensor_single_scalar(z[:], q[:], a, OP.mult),
                               waits=[("sE", st["s"]), ("tE", zb_free[st["q4"]])], inc="vE")
            return sts

        def trrouter(st, zsl):
            tt, p, ti = st["tt"], st["p"], st["tt"] % 4
            transpose_to(
                lambda dc, t=ti, zz=zsl: zT[zz][:, dc, t * 128:(t + 1) * 128],
                lambda dc, z=st["zb"]: z[:, dc * 128:(dc + 1) * 128],
                [("vE", st["vz"]), ("tE", zT_used[zsl])])
            zb_free[tt % 4] = P.now("tE")
            if tt < 16 and router:
                ev2 = 0
                for dc in range(8):
                    tsl = dc % 2
                    tval = P.op("tensor", lambda e, d=dc, s=tsl, w=st["xw"]: e.transpose(
                        ps_tr32[s][:], w[:, d * 128:(d + 1) * 128], ident32_sb[:]),
                        waits=[("vE", st["vxn"]), ("sE", tr32_evac[tsl])], inc="tE")
                    P.op("scalar", lambda e, d=dc, s=tsl: e.copy(xnT32[:, d, :], ps_tr32[s][:]),
                         waits=[("tE", tval)], inc="sE")
                    ev2 = P.op("scalar", lambda e, d=dc, s=tsl, t=tt: e.copy(
                        xnT_sb[:, d, t * 128:(t + 1) * 128], ps_tr32[s][:]),
                        waits=[("tE", tval)], inc="sE")
                    tr32_evac[tsl] = ev2
                xwork_free[tt % 4] = P.now("tE")
                for dc in range(8):
                    tv = P.op("tensor", lambda e, d=dc: e.matmul(
                        ps_rt[:], xnT32[:, d, :], rw_sb[:, d, :],
                        start=(d == 0), stop=(d == 7)),
                        waits=[("sE", ev2), ("cS", cS0), ("vE", rt_free[0])],
                        inc="tE" if dc == 7 else None)
                st["rv"] = P.op("vector", lambda e, g=lgt2[p]: e.tensor_add(g[:], ps_rt[:], rb_bc[:]),
                                waits=[("tE", tv), ("vE", vrb)], inc="vE")
                rt_free[0] = st["rv"]

        def softmax_pair(sts):
            for st in sts:
                p = st["p"]
                st["g"] = lgt2[p]
                st["pp"] = p162[p]
                st["cu"] = cur2[p]
                st["mk"] = msk2[p]
                st["c0"] = colb[:, 16 + 2 * p:17 + 2 * p]
                st["c1"] = colb[:, 17 + 2 * p:18 + 2 * p]
                st["v"] = st["rv"]

            def V(st, fn, waits_extra=()):
                st["v"] = P.op("vector", fn, waits=[("vE", st["v"])] + list(waits_extra), inc="vE")

            for st in sts:
                V(st, lambda e, g=st["g"], a=st["c0"]: e.reduce_max(a, g[:], axis=AX.X))
            for st in sts:
                V(st, lambda e, g=st["g"], a=st["c0"]: e.tensor_single_scalar(g[:], g[:], a, OP.subtract))
            for st in sts:
                st["s"] = P.op("scalar", lambda e, g=st["g"], q=st["pp"]: e.activation(q[:], g[:], AF.Exp),
                               waits=[("vE", st["v"])], inc="sE")
            for st in sts:
                st["v"] = P.op("vector", lambda e, q=st["pp"], a=st["c0"]: e.reduce_sum(a, q[:], axis=AX.X),
                               waits=[("sE", st["s"])], inc="vE")
            for st in sts:
                V(st, lambda e, a=st["c1"], b=st["c0"]: e.reciprocal(a, b))
            for st in sts:
                V(st, lambda e, q=st["pp"], a=st["c1"]: e.tensor_single_scalar(q[:], q[:], a, OP.mult))
            for st in sts:
                V(st, lambda e, q=st["pp"], c=st["cu"]: e.tensor_copy(c[:], q[:]))
            for _ in range(3):
                for st in sts:
                    V(st, lambda e, c=st["cu"], a=st["c0"]: e.reduce_max(a, c[:], axis=AX.X))
                for st in sts:
                    V(st, lambda e, c=st["cu"], m=st["mk"], a=st["c0"]: e.tensor_single_scalar(m[:], c[:], a, OP.is_ge))
                for st in sts:
                    V(st, lambda e, c=st["cu"], m=st["mk"]: e.scalar_tensor_tensor(c[:], m[:], -2.0, c[:], OP.mult, OP.add))
            for st in sts:
                V(st, lambda e, c=st["cu"], a=st["c0"]: e.reduce_max(a, c[:], axis=AX.X))
            for st in sts:
                V(st, lambda e, q=st["pp"], m=st["mk"], a=st["c0"]: e.tensor_single_scalar(m[:], q[:], a, OP.is_ge))
            for st in sts:
                V(st, lambda e, q=st["pp"], m=st["mk"]: e.tensor_mul(q[:], q[:], m[:]))
            for st in sts:
                V(st, lambda e, q=st["pp"], a=st["c0"]: e.reduce_sum(a, q[:], axis=AX.X))
            for st in sts:
                V(st, lambda e, a=st["c1"], b=st["c0"]: e.reciprocal(a, b))
            for st in sts:
                V(st, lambda e, q=st["pp"], a=st["c1"], t=st["tt"]: e.tensor_single_scalar(
                    c_sb[:, t * 16:(t + 1) * 16], q[:], a, OP.mult))

        def proj_mms(wsb, bias_col, zsl, psl):
            for oc in range(8):
                for dc in range(8):
                    tvv = P.op("tensor", lambda e, d=dc, o=oc, w=wsb, zz=zsl: e.matmul(
                        ps_proj[o % 2][:], w[:, d, o * 128:(o + 1) * 128], zT[zz][:, d, :],
                        start=(d == 0), stop=(d == 7)),
                        waits=[("sE", zT_ready[0]), ("vE", vproj_evac[oc % 2]), ("cG", cG0),
                               ("sE", proj_evac[oc % 2])],
                        inc="tE" if dc == 7 else None)
                proj_evac[oc % 2] = P.op("scalar", lambda e, o=oc, b=bias_col, pp=psl: e.activation(
                    proj[pp][:, o, :], ps_proj[o % 2][:], AF.Identity, bias=b[:, o:o + 1]),
                    waits=[("tE", tvv), ("vE", proj_free[psl])], inc="sE")

        def make_fin(psl, is_q, spill_dst):
            ev0, ev1 = proj_evac[0], proj_evac[1]

            def fin():
                vq = P.op("vector", lambda e, pp=psl: e.tensor_mul(sqb[:], proj[pp][:], proj[pp][:]),
                          waits=[("sE", max(ev0, ev1)), ("sfT", fT_free[0])], inc="vE")
                for oc in range(8):
                    tss = P.op("tensor", lambda e, o=oc: e.matmul(
                        ps_ss[:], ones_cb[:], sqb[:, o, :], start=(o == 0), stop=(o == 7)),
                        waits=[("vE", vq), ("sE", nrm_free[0])],
                        inc="tE" if oc == 7 else None)
                sn = P.op("scalar", lambda e: e.activation(nrm[:], ps_ss[:], AF.Ln), waits=[("tE", tss)], inc="sE")
                nrm_free[0] = sn
                sn = P.op("scalar", lambda e: e.activation(nrm[:], nrm[:], AF.Exp, scale=0.5), waits=[("sE", sn)], inc="sE")
                vn_ = P.op("vector", lambda e: e.tensor_scalar_max(nrm[:], nrm[:], 1e-6), waits=[("sE", sn)], inc="vE")
                sn = P.op("scalar", lambda e: e.activation(nrm[:], nrm[:], AF.Ln), waits=[("vE", vn_)], inc="sE")
                sn = P.op("scalar", lambda e: e.activation(nrm[:], nrm[:], AF.Exp, scale=-1.0), waits=[("sE", sn)], inc="sE")
                tb = P.op("tensor", lambda e: e.matmul(ps_rb[:], ones_rf[:], nrm[:], start=True, stop=True),
                          waits=[("sE", sn), ("vE", rnb_free[0])], inc="tE")
                vr = P.op("vector", lambda e: e.tensor_copy(rnb[:], ps_rb[:]), waits=[("tE", tb)], inc="vE")
                rnb_free[0] = vr
                lastv = P.op("vector", lambda e, pp=psl: e.tensor_tensor(
                    fT_st[:], proj[pp][:],
                    rnb[:].unsqueeze(1).broadcast_to((128, 8, 512)), OP.mult),
                    waits=[("vE", vr), ("sfT", fT_free[0])], inc="vE")
                proj_free[psl] = lastv
                fT_free[0] = P.dma("sync", lambda e, d=spill_dst: e.dma_start(
                    out=d, in_=fT_st[:]), waits=[("vE", lastv)], sem="sfT")
            return fin

        p_ctr = [0]
        pending_fin = [None]

        def run_pending():
            if pending_fin[0] is not None:
                pending_fin[0]()
                pending_fin[0] = None

        pend_ln = [None]
        for gi in range(8):
            t0 = gi * 4
            zsl = gi % 2
            if pend_ln[0] is None:
                stsA = ln_pair(t0, t0 + 1)
                trrouter(stsA[0], zsl)
                trrouter(stsA[1], zsl)
                stsB = ln_pair(t0 + 2, t0 + 3)
            else:
                stsA, stsB = pend_ln[0]
                trrouter(stsA[0], zsl)
                trrouter(stsA[1], zsl)
            if t0 < 16 and router:
                softmax_pair(stsA)
            trrouter(stsB[0], zsl)
            trrouter(stsB[1], zsl)
            if t0 < 16 and router:
                softmax_pair(stsB)
            zT_ready[0] = P.now("sE")
            nxt = [None, None]

            if gi < 4:
                psl = p_ctr[0] % 2
                p_ctr[0] += 1
                proj_mms(qwT_sb, qbc_sb, zsl, psl)
                run_pending()
                pending_fin[0] = make_fin(
                    psl, True,
                    qT_s[:, :, gi * 512:(gi + 1) * 512].rearrange("a p t -> p a t"))
                if gi < 7:
                    nxt[0] = ln_pair(t0 + 4, t0 + 5)
            psl = p_ctr[0] % 2
            p_ctr[0] += 1
            proj_mms(kwT_sb, kbc_sb, zsl, psl)
            run_pending()
            pending_fin[0] = make_fin(
                psl, False,
                kT_s[:, :, gi * 512:(gi + 1) * 512].rearrange("a p t -> p a t"))
            if gi < 7 and nxt[0] is None:
                nxt[0] = ln_pair(t0 + 4, t0 + 5)
            elif gi < 7:
                nxt[1] = ln_pair(t0 + 6, t0 + 7)
            for ti in range(4):
                tt = gi * 4 + ti
                st2 = ti
                ev_last = 0
                for oc2 in range(2):
                    sl = (ti * 2 + oc2) % 2
                    for dc in range(8):
                        tvv = P.op("tensor", lambda e, d=dc, t=ti, o=oc2, s=sl, zz=zsl: e.matmul(
                            ps_proj[s][:], zT[zz][:, d, t * 128:(t + 1) * 128],
                            vwT_sb[:, d, o * 512:(o + 1) * 512],
                            start=(d == 0), stop=(d == 7)),
                            waits=[("sE", zT_ready[0]), ("vE", vproj_evac[sl]), ("sE", proj_evac[sl])],
                            inc="tE" if dc == 7 else None)
                    vproj_evac[sl] = P.op("vector", lambda e, o=oc2, s=sl, g=st2: e.tensor_copy(
                        vstg[g][:].rearrange("p (a b) -> p a b", a=16)[:, o * 8:(o + 1) * 8, 0:64],
                        ps_proj[s][:].rearrange("p (a t) -> p a t", a=8)),
                        waits=[("tE", tvv), (f"svau{st2}", vstg_free[st2])], inc="vE")
                    ev_last = vproj_evac[sl]
                vstg_free[st2] = P.dma("sync", lambda e, t=tt, g=st2: e.dma_start(
                    out=vAug_s[:, :, t, :].rearrange("a p b -> p a b"),
                    in_=vstg[g][:].rearrange("p (a b) -> p a b", a=8)),
                    waits=[("vE", ev_last)], sem=f"svau{st2}")
            zT_used[zsl] = P.now("tE")
            run_pending()
            if gi < 7 and nxt[1] is None:
                nxt[1] = ln_pair(t0 + 6, t0 + 7)
            pend_ln[0] = (nxt[0], nxt[1]) if gi < 7 else None

        vcb = P.op("vector", lambda e: e.tensor_copy(cbf[:], c_sb[:]), waits=[("vE", P.now("vE"))], inc="vE")
        for tt in range(16):
            sl = tt % 2
            tvv = P.op("tensor", lambda e, t=tt, s=sl: e.transpose(
                ps_tr[s][0:16, :], cbf[:, t * 16:(t + 1) * 16], ident_sb[:]),
                waits=[("vE", vcb), ("sE", tr_evac[sl])], inc="tE")
            tr_evac[sl] = P.op("scalar", lambda e, t=tt, s=sl: e.copy(
                cT_sb[0:16, t * 128:(t + 1) * 128], ps_tr[s][0:16, :]),
                waits=[("tE", tvv)], inc="sE")
        if "c" in dbg:
            P.dma("sync", lambda e: e.dma_start(out=c_dbg, in_=c_sb[:]), waits=[("vE", P.now("vE"))], sem="dbg")

        P.barrier()
        PCTX.close()

        if "A" not in stages:
            finish()
            return nc

        FCTX = ExitStack()
        FE = FCTX.enter_context
        h1T = FE(nc.sbuf_tensor("h1T", [128, 8, 2048], BF16))
        w1b = [FE(nc.sbuf_tensor(f"w1b{i}", [128, 8, 1024], BF16)) for i in range(2)]
        w2b = [FE(nc.sbuf_tensor(f"w2b{i}", [128, 8, 1024], BF16)) for i in range(2)]
        b1sb = [FE(nc.sbuf_tensor(f"b1sb{i}", [128, 16], F32)) for i in range(2)]
        b2sb = FE(nc.sbuf_tensor("b2sb", [16, 1024], BF16))
        ffst = [FE(nc.sbuf_tensor(f"ffst{i}", [128, 1024], F32)) for i in range(2)]
        qToc = [FE(nc.sbuf_tensor(f"qToc{i}", [128, 2048], BF16)) for i in range(2)]
        kToc = [FE(nc.sbuf_tensor(f"kToc{i}", [128, 4096], BF16)) for i in range(2)]
        vAugP = [FE(nc.sbuf_tensor(f"vAugP{i}", [128, 32, 130], BF16)) for i in range(2)]
        PT = [FE(nc.sbuf_tensor(f"PT{i}", [128, 512], BF16)) for i in range(2)]
        den_sb = [FE(nc.sbuf_tensor(f"den_sb{i}", [1, 512], F32)) for i in range(2)]
        rden = [FE(nc.sbuf_tensor(f"rden{i}", [1, 512], F32)) for i in range(2)]
        rdb = FE(nc.sbuf_tensor("rdb", [64, 512], F32))
        otn_st = [FE(nc.sbuf_tensor(f"otn_st{i}", [128, 512], BF16)) for i in range(2)]
        ps_h2 = [FE(nc.psum_tensor(f"ps_h{i}", [128, 512], F32)) for i in range(2)]
        ps_o2 = [FE(nc.psum_tensor(f"ps_o{i}", [128, 512], F32)) for i in range(2)]
        ps_s2 = [FE(nc.psum_tensor(f"ps_s{i}", [128, 512], F32)) for i in range(2)]
        ps_av = [FE(nc.psum_tensor(f"ps_av{i}", [128, 512], F32)) for i in range(2)]
        ps_rd = ps_s2[0][0:64, 0:512]

        b2_rd = P.dma("gpsimd", lambda e: e.dma_start(out=b2sb[:], in_=b2d), sem="sb2")

        kq_free = [0, 0]
        kT_rd = [0, 0]
        qT_rd = [0, 0]
        va_rd = [0, 0]
        exp_done = [0, 0]
        av_done = [0, 0]
        otn_free = [0, 0]
        avps_free = [0, 0]
        rdb_free = [0]
        rden_free = [0, 0]

        def load_oc(o):
            sl = o % 2
            kT_rd[sl] = P.dma("sync", lambda e, oo=o, s=sl: e.dma_start(out=kToc[s][:], in_=kT_s[oo]),
                              waits=[("tE", kq_free[sl])], sem=f"skT{sl}")
            qT_rd[sl] = P.dma("sync", lambda e, oo=o, s=sl: e.dma_start(out=qToc[s][:], in_=qT_s[oo]),
                              waits=[("tE", kq_free[sl])], sem=f"sqT{sl}")
            va_rd[sl] = P.dma("sync", lambda e, oo=o, s=sl: e.dma_start(
                out=vAugP[s][:], in_=vAug_s[oo]),
                waits=[("tE", kq_free[sl])], sem=f"sva{sl}")

        def a_gen():
            load_oc(0)
            load_oc(1)
            for oc in range(8):
                sl = oc % 2

                def scores(hh, m, n, s_=sl):
                    pb = hh * 64
                    tp = (64, 0) if hh else (0, 0)
                    return P.op("tensor", lambda e, s=hh, mm=m, p=pb, t=tp, nn=n, sll=s_: e.matmul(
                        ps_s2[s][:], kToc[sll][p:p + 64, mm * 128:(mm + 1) * 128],
                        qToc[sll][p:p + 64, nn * 512:(nn + 1) * 512],
                        start=True, stop=True, tile_position=t),
                        waits=[(f"skT{s_}", kT_rd[s_]), (f"sqT{s_}", qT_rd[s_]),
                               ("sE", exp_done[hh])] +
                              ([("vE", rdb_free[0])] if hh == 0 else []),
                        inc="tE")

                for n in range(4):
                    sc = [scores(0, 0, n), scores(1, 0, n)]
                    for m in range(32):
                        for hh in range(2):
                            exp_done[hh] = P.op("scalar", lambda e, s=hh, o=oc: e.activation(
                                PT[s][:], ps_s2[s][:], AF.Exp, scale=Bb[:, 2 * o + s:2 * o + s + 1]),
                                waits=[("tE", sc[hh]), ("tE", av_done[hh])], inc="sE")
                        if m + 1 < 32:
                            sc = [scores(0, m + 1, n), scores(1, m + 1, n)]
                        for hh in range(2):
                            av_done[hh] = P.op("tensor", lambda e, s=hh, mm=m, sll=sl: e.matmul(
                                ps_av[s][0:65, :],
                                vAugP[sll][:].rearrange("p a (b c) -> p a b c", b=2)[:, mm, s, :],
                                PT[s][:], start=(mm == 0), stop=(mm == 31)),
                                waits=[("sE", exp_done[hh]), ("vE", avps_free[hh]), (f"sva{sl}", va_rd[sl])],
                                inc="tE")
                        yield
                    for hh in range(2):
                        pb = hh * 64
                        sd = P.op("scalar", lambda e, a=hh: e.copy(den_sb[a][:], ps_av[a][64:65, :]),
                                  waits=[("tE", av_done[hh])], inc="sE")
                        sd = P.op("scalar", lambda e, a=hh: e.activation(rden[a][:], den_sb[a][:], AF.Ln),
                                  waits=[("sE", sd), ("tE", rden_free[hh])], inc="sE")
                        sd = P.op("scalar", lambda e, a=hh: e.activation(rden[a][:], rden[a][:], AF.Exp, scale=-1.0),
                                  waits=[("sE", sd)], inc="sE")
                        tb = P.op("tensor", lambda e, a=hh: e.matmul(ps_rd, ones_rf[0:1, 0:64], rden[a][:], start=True, stop=True),
                                  waits=[("sE", sd), ("vE", rdb_free[0]), ("sE", exp_done[0])], inc="tE")
                        rden_free[hh] = tb
                        vv = P.op("vector", lambda e: e.tensor_copy(rdb[:], ps_rd), waits=[("tE", tb)], inc="vE")
                        rdb_free[0] = vv
                        avps_free[hh] = P.op("vector", lambda e, p=pb, a=hh, s=n % 2: e.tensor_tensor(
                            otn_st[s][p:p + 64, :], ps_av[a][0:64, :], rdb[:], OP.mult),
                            waits=[("vE", vv), (f"sotn{n % 2}", otn_free[n % 2])], inc="vE")
                    vadd = P.op("vector", lambda e, o=oc, s=n % 2: e.tensor_scalar_add(
                        otn_st[s][:], otn_st[s][:], rcol[:, o:o + 1]),
                        waits=[("vE", avps_free[0]), ("vE", avps_free[1])], inc="vE")
                    otn_free[n % 2] = P.dma("sync", lambda e, o=oc, nn=n, s=n % 2: e.dma_start(
                        out=otn_s[o, :, nn * 512:(nn + 1) * 512], in_=otn_st[s][:]),
                        waits=[("vE", vadd)], sem=f"sotn{n % 2}")
                    yield
                kq_free[sl] = max(sc[0], sc[1], av_done[0], av_done[1], rden_free[0], rden_free[1])
                if oc + 2 < 8:
                    load_oc(oc + 2)

        agen = a_gen()

        def inject(k=1):
            for _ in range(k):
                next(agen, None)

        o_done = [0, 0]
        acc_init_done = 0
        acc_cnt = [0] * 16
        ffst_free = [("sacc0", 0), ("sacc0", 0)]
        ob_ctr = [0]
        for tt in range(16):
            sl = tt % 2
            evs = 0
            for oc2 in range(2):
                ob = ob_ctr[0] % 2
                ob_ctr[0] += 1
                tvv = P.op("tensor", lambda e, t=tt, o=oc2, b=ob: e.matmul(
                    ps_o2[b][:], cT_sb[0:16, t * 128:(t + 1) * 128],
                    b2sb[0:16, o * 512:(o + 1) * 512], start=True, stop=True),
                    waits=[("sb2", b2_rd), ("vE", o_done[ob])], inc="tE")
                evs = P.op("vector", lambda e, o=oc2, s=sl, b=ob: e.tensor_copy(
                    ffst[s][:, o * 512:(o + 1) * 512], ps_o2[b][:]),
                    waits=[("tE", tvv), ffst_free[sl]], inc="vE")
                o_done[ob] = evs
            acc_cnt[tt] += 1
            v = P.dma("gpsimd", lambda e, t=tt, s=sl: e.dma_start(
                out=acc_d[t], in_=ffst[s][:]),
                waits=[("vE", evs)], sem=f"sacc{tt}")
            ffst_free[sl] = (f"sacc{tt}", v)
        acc_init_done = P.now("vE")

        w1_rd = [0, 0]
        w2_rd = [0, 0]
        w1_free = [0, 0]
        w2_free = [0, 0]
        b1_rd = [0, 0]
        b1_free = [0, 0]
        h_done = [0, 0]

        def load_w(e_, hf):
            sp = (2 * e_ + hf) % 2
            w1_rd[sp] = P.dma("gpsimd", lambda e, ee=e_, h=hf, s=sp: e.dma_start(
                out=w1b[s][:], in_=w1d[ee].rearrange("a p n -> p a n")[:, :, h * 1024:(h + 1) * 1024]),
                waits=[("tE", w1_free[sp])], sem=f"sw1{sp}")
            w2_rd[sp] = P.dma("gpsimd", lambda e, ee=e_, h=hf, s=sp: e.dma_start(
                out=w2b[s][:], in_=w2d[ee, h * 8:(h + 1) * 8].rearrange("a p n -> p a n")),
                waits=[("tE", w2_free[sp])], sem=f"sw2{sp}")
            if hf == 0:
                b1_rd[e_ % 2] = P.dma("sync", lambda e, ee=e_: e.dma_start(
                    out=b1sb[ee % 2][:], in_=b1c[ee]), waits=[("sE", b1_free[e_ % 2])], sem=f"sb1{e_ % 2}")

        load_w(0, 0)
        load_w(0, 1)

        hb_ctr = [0]
        for e_ in range(16):
            for hf in range(2):
                sp = (2 * e_ + hf) % 2
                for ic in range(8):
                    for np2 in range(2):
                        for sb2 in range(2):
                            hb = hb_ctr[0] % 2
                            hb_ctr[0] += 1
                            for dc in range(8):
                                tvv = P.op("tensor", lambda e, d=dc, i=ic, n2=np2, s2=sb2, p=sp, b=hb: e.matmul(
                                    ps_h2[b][:],
                                    w1b[p][:, d, i * 128:(i + 1) * 128],
                                    xnT_sb[:, d, n2 * 1024 + s2 * 512:n2 * 1024 + (s2 + 1) * 512],
                                    start=(d == 0), stop=(d == 7)),
                                    waits=[(f"sw1{sp}", w1_rd[sp]), (f"sb1{e_ % 2}", b1_rd[e_ % 2]),
                                           ("sE", h_done[hb]),
                                           ("vE", acc_init_done)],
                                    inc="tE" if dc == 7 else None)
                            h_done[hb] = P.op("scalar", lambda e, i=ic, n2=np2, s2=sb2, ee=e_, h=hf, b=hb: e.activation(
                                h1T[:, i, n2 * 1024 + s2 * 512:n2 * 1024 + (s2 + 1) * 512], ps_h2[b][:], AF.Gelu,
                                bias=b1sb[ee % 2][:, h * 8 + i:h * 8 + i + 1]),
                                waits=[("tE", tvv)], inc="sE")
                w1_free[sp] = P.now("tE")
                gelu_all = P.now("sE")
                if hf == 1:
                    b1_free[e_ % 2] = gelu_all
                for tt in range(16):
                    sl = tt % 2
                    evs = 0
                    for oc2 in range(2):
                        ob = ob_ctr[0] % 2
                        ob_ctr[0] += 1
                        for ic in range(8):
                            tvv = P.op("tensor", lambda e, i=ic, t=tt, o=oc2, p=sp, b=ob: e.matmul(
                                ps_o2[b][:],
                                h1T[:, i, t * 128:(t + 1) * 128],
                                w2b[p][:, i, o * 512:(o + 1) * 512],
                                start=(i == 0), stop=(i == 7)),
                                waits=[("sE", gelu_all), (f"sw2{sp}", w2_rd[sp]), ("vE", o_done[ob])],
                                inc="tE" if ic == 7 else None)
                        evs = P.op("vector", lambda e, t=tt, o=oc2, s=sl, ee=e_, b=ob: e.tensor_single_scalar(
                            ffst[s][:, o * 512:(o + 1) * 512], ps_o2[b][:],
                            c_sb[:, t * 16 + ee:t * 16 + ee + 1], OP.mult),
                            waits=[("tE", tvv), ffst_free[sl]], inc="vE")
                        o_done[ob] = evs
                        inject(1)
                    acc_cnt[tt] += 1
                    v = P.dma("gpsimd", lambda e, t=tt, s=sl: e.dma_start(
                        out=acc_d[t], in_=ffst[s][:], accum_op=OP.add),
                        waits=[("vE", evs)], sem=f"sacc{tt}")
                    ffst_free[sl] = (f"sacc{tt}", v)
                w2_free[sp] = P.now("tE")
                nxt = 2 * e_ + hf + 2
                if nxt < 32:
                    load_w(nxt // 2, nxt % 2)

        for _ in agen:
            pass

        P.barrier()
        FCTX.close()

        if "D" not in stages:
            finish()
            return nc

        DCTX = ExitStack()
        DE = DCTX.enter_context
        NSL = 3
        acc_t = [DE(nc.sbuf_tensor(f"acc_t{i}", [128, 1024], F32)) for i in range(NSL)]
        xq_t = [DE(nc.sbuf_tensor(f"xq_t{i}", [128, 1024], F32)) for i in range(NSL)]
        otn_t = [DE(nc.sbuf_tensor(f"otn_t{i}", [128, 8, 128], BF16)) for i in range(NSL)]
        outb = [DE(nc.sbuf_tensor(f"outb{i}", [128, 1024], F32)) for i in range(NSL)]
        owT_sb = DE(nc.sbuf_tensor("owT_sb", [128, 8, 1024], BF16))
        ob_bc = DE(nc.sbuf_tensor("ob_bc", [128, 1024], F32))
        obrow = DE(nc.sbuf_tensor("obrow", [1, 1024], F32))
        t_xn = DE(nc.sbuf_tensor("t_xn", [128, 512], F32))
        t_s = DE(nc.sbuf_tensor("t_s", [128, 512], F32))
        ps_d = [DE(nc.psum_tensor(f"ps_d{i}", [128, 1024], F32)) for i in range(2)]
        ps_b = DE(nc.psum_tensor("ps_b", [128, 512], F32))

        ow_rd = P.dma("gpsimd", lambda e: e.dma_start(
            out=owT_sb[:], in_=owT.rearrange("(a p) n -> p a n", p=128)), sem="sow")
        ob_rd = P.dma("sync", lambda e: e.dma_start(out=obrow[:], in_=ob_r), sem="sob")
        for half in range(2):
            tb = P.op("tensor", lambda e, hf=half: e.matmul(
                ps_b[:], ones_rf[:], obrow[:, hf * 512:(hf + 1) * 512], start=True, stop=True),
                waits=[("sob", ob_rd), ("vE", P.now("vE"))], inc="tE")
            P.op("vector", lambda e, hf=half: e.tensor_copy(ob_bc[:, hf * 512:(hf + 1) * 512], ps_b[:]),
                 waits=[("tE", tb)], inc="vE")

        otn_fin = [(s, P.now(s)) for s in ("sotn0", "sotn1")]
        acc_ld = [0] * NSL
        xq_ld = [0] * NSL
        otn_ld = [0] * NSL
        accq_free = [0] * NSL
        xq_free = [0] * NSL
        otn_mm_free = [0] * NSL
        out_free = [0] * NSL
        d_done = [0, 0]

        def load_tile(tt):
            slot = tt % NSL
            acc_ld[slot] = P.dma("sync", lambda e, t=tt, s=slot: e.dma_start(
                out=acc_t[s][:], in_=acc_d[t]),
                waits=[(f"sacc{tt}", acc_cnt[tt] * 16), ("vE", accq_free[slot])],
                sem=f"sdac{slot}")
            xq_ld[slot] = P.dma("gpsimd", lambda e, t=tt, s=slot: e.dma_start(
                out=xq_t[s][:], in_=x[t * 128:(t + 1) * 128, :]),
                waits=[("vE", xq_free[slot])], sem=f"sdxq{slot}")
            otn_ld[slot] = P.dma("sync", lambda e, t=tt, s=slot: e.dma_start(
                out=otn_t[s][:], in_=otn_s[:, :, t * 128:(t + 1) * 128].rearrange("a p t -> p a t")),
                waits=[otn_fin[0], otn_fin[1], ("tE", otn_mm_free[slot])],
                sem=f"sdot{slot}")

        for tt in range(NSL):
            load_tile(tt)

        for tt in range(16):
            slot = tt % NSL
            sil = P.op("scalar", lambda e, s=slot: e.activation(acc_t[s][:], acc_t[s][:], AF.Silu),
                       waits=[(f"sdac{slot}", acc_ld[slot])], inc="sE")
            for oc2 in range(2):
                sl = (2 * tt + oc2) % 2
                for dc in range(8):
                    tvv = P.op("tensor", lambda e, d=dc, o=oc2, s=sl, st=slot: e.matmul(
                        ps_d[s][:, o * 512:(o + 1) * 512],
                        otn_t[st][:, d, :],
                        owT_sb[:, d, o * 512:(o + 1) * 512],
                        start=(d == 0), stop=(d == 7)),
                        waits=[(f"sdot{slot}", otn_ld[slot]), ("sow", ow_rd), ("vE", d_done[sl])],
                        inc="tE" if dc == 7 else None)
                vv = P.op("vector", lambda e, t=tt, o=oc2, st=slot: e.tensor_scalar(
                    t_xn[:], xq_t[st][:, o * 512:(o + 1) * 512], negmu1[:, t:t + 1], r1sb[:, t:t + 1],
                    OP.add, OP.mult),
                    waits=[(f"sdxq{slot}", xq_ld[slot]), ("tE", tvv), ("sE", sil)], inc="vE")
                vv = P.op("vector", lambda e, o=oc2, s=sl: e.scalar_tensor_tensor(
                    t_s[:], ps_d[s][:, o * 512:(o + 1) * 512], 1.0, ob_bc[:, o * 512:(o + 1) * 512],
                    OP.mult, OP.add),
                    waits=[("vE", vv)], inc="vE")
                d_done[sl] = vv
                vv = P.op("vector", lambda e: e.tensor_add(t_s[:], t_s[:], t_xn[:]),
                          waits=[("vE", vv)], inc="vE")
                vv = P.op("vector", lambda e, o=oc2, st=slot: e.tensor_mul(
                    t_s[:], t_s[:], acc_t[st][:, o * 512:(o + 1) * 512]),
                    waits=[("vE", vv)], inc="vE")
                accq_free[slot] = vv
                vv = P.op("vector", lambda e, o=oc2, st=slot: e.tensor_add(
                    outb[st][:, o * 512:(o + 1) * 512], t_s[:], xq_t[st][:, o * 512:(o + 1) * 512]),
                    waits=[("vE", vv), (f"sdout{slot}", out_free[slot])], inc="vE")
                xq_free[slot] = vv
            otn_mm_free[slot] = P.now("tE")
            out_free[slot] = P.dma("sync", lambda e, t=tt, s=slot: e.dma_start(
                out=out[t * 128:(t + 1) * 128, :], in_=outb[s][:]),
                waits=[("vE", xq_free[slot])], sem=f"sdout{slot}")
            if tt + NSL < 16:
                load_tile(tt + NSL)

        P.barrier()

        finish()
        DCTX.close()
    return nc


def _host_inputs(kw):
    f32 = np.float32

    def C(a):
        return np.ascontiguousarray(np.asarray(a), dtype=f32)

    def col8(v):
        return C(np.asarray(v).reshape(8, 128).T)

    base = {
        "qwT": C(np.asarray(kw["L0_attn_q_w"]).T),
        "kwT": C(np.asarray(kw["L0_attn_k_w"]).T),
        "vwT": C(np.asarray(kw["L0_attn_v_w"]).T),
        "owT": C(np.asarray(kw["L0_attn_o_w"]).T),
        "g_q": col8(kw["L0_attn_ln_q_g"]), "b_qn": col8(kw["L0_attn_ln_q_b"]),
        "g_k": col8(kw["L0_attn_ln_k_g"]), "b_kn": col8(kw["L0_attn_ln_k_b"]),
        "g_v": col8(kw["L0_attn_ln_v_g"]), "b_vn": col8(kw["L0_attn_ln_v_b"]),
        "qb_c": col8(kw["L0_attn_q_b"]), "kb_c": col8(kw["L0_attn_k_b"]),
        "vb_c": col8(kw["L0_attn_v_b"]),
        "ob_r": C(np.asarray(kw["L0_attn_o_b"]).reshape(1, 1024)),
        "beta_r": C(np.asarray(kw["L0_attn_beta_param"]).reshape(1, 16)),
        "rw": C(np.asarray(kw["L0_ffn_router_w"]).reshape(8, 128, 16).transpose(1, 0, 2)),
        "rb_r": C(np.asarray(kw["L0_ffn_router_b"]).reshape(1, 16)),
        "w1": C(np.asarray(kw["L0_ffn_w1"]).reshape(16, 8, 128, 2048)),
        "b1c": C(np.asarray(kw["L0_ffn_b1"]).reshape(16, 16, 128).transpose(0, 2, 1)),
        "w2": C(np.asarray(kw["L0_ffn_w2"]).reshape(16, 16, 128, 1024)),
        "b2": C(np.asarray(kw["L0_ffn_b2"]).reshape(16, 1024)),
        "ident": np.eye(128, dtype=ml_dtypes.bfloat16),
        "ident32": np.eye(128, dtype=np.float32),
    }
    x = np.asarray(kw["L0_attn_query"], dtype=f32)
    in_maps = []
    for c in range(8):
        b, hf = c // 2, c % 2
        xb = x[b]
        xc = xb if hf == 0 else np.concatenate([xb[2048:], xb[:2048]], axis=0)
        in_maps.append(dict(base, x=np.ascontiguousarray(xc)))
    return in_maps


def kernel(**kw):
    in_maps = _host_inputs(kw)
    nc = build()
    res = run_bass_kernel_spmd(nc, in_maps, core_ids=list(range(8)))
    out = np.empty((4, 4096, 1024), dtype=np.float32)
    for c in range(8):
        b, hf = c // 2, c % 2
        out[b, hf * 2048:(hf + 1) * 2048, :] = res.results[c]["out"]
    return out


# version 23
# speedup vs baseline: 1.0069x; 1.0052x over previous
import numpy as np
import ml_dtypes
from contextlib import ExitStack
import concourse.bass as bass
import concourse.mybir as mybir
from concourse.bass_utils import run_bass_kernel_spmd
from concourse.alu_op_type import AluOpType as OP

F32 = mybir.dt.float32
BF16 = mybir.dt.bfloat16
AF = mybir.ActivationFunctionType
AX = mybir.AxisListType

ENGINES = ("sync", "gpsimd", "scalar", "vector", "tensor")


class Prog:

    def __init__(self, nc, ctx):
        self.nc = nc
        self.ctx = ctx
        self.streams = {e: [] for e in ENGINES}
        self.sems = {}
        self.cnt = {}
        self.dma_sems = set()
        for name in ("tE", "vE", "sE", "gE"):
            self._mk(name)

    def _mk(self, name):
        self.sems[name] = self.ctx.enter_context(self.nc.semaphore(name))
        self.cnt[name] = 0

    def now(self, sem):
        return self.cnt[sem]

    def op(self, eng, fn, waits=(), inc=None):
        w = tuple((s, v) for (s, v) in waits if v > 0)
        if inc is not None:
            self.cnt[inc] += 16 if inc in self.dma_sems else 1
        self.streams[eng].append((w, fn, inc))
        return self.cnt[inc] if inc else None

    def dma(self, eng, fn, waits=(), sem="cS"):
        if sem not in self.sems:
            self._mk(sem)
            self.dma_sems.add(sem)
        return self.op(eng, fn, waits, inc=sem)

    def barrier(self):
        snap = [(s, self.cnt[s]) for s in self.sems if self.cnt[s] > 0]
        for eng in ENGINES:
            self.streams[eng].append((tuple(snap), None, None))


def _emit_body(items, sems, dma_sems, e):
    last = {}
    for waits, fn, inc in items:
        for s, v in waits:
            if last.get(s, 0) < v:
                e.wait_ge(sems[s], v)
                last[s] = v
        if fn is None:
            continue
        ins = fn(e)
        if inc is not None:
            ins.then_inc(sems[inc], 16 if inc in dma_sems else 1)


def build(dbg=(), stages=("P", "A", "B", "D"), router=True, fold=True):
    nc = bass.Bass("TRN2", target_bir_lowering=False)

    x = nc.dram_tensor("x", [4096, 1024], F32, kind="ExternalInput").ap()
    qwT = nc.dram_tensor("qwT", [1024, 1024], F32, kind="ExternalInput").ap()
    kwT = nc.dram_tensor("kwT", [1024, 1024], F32, kind="ExternalInput").ap()
    vwT = nc.dram_tensor("vwT", [1024, 1024], F32, kind="ExternalInput").ap()
    owT = nc.dram_tensor("owT", [1024, 1024], F32, kind="ExternalInput").ap()
    g_q = nc.dram_tensor("g_q", [128, 8], F32, kind="ExternalInput").ap()
    b_qn = nc.dram_tensor("b_qn", [128, 8], F32, kind="ExternalInput").ap()
    g_k = nc.dram_tensor("g_k", [128, 8], F32, kind="ExternalInput").ap()
    b_kn = nc.dram_tensor("b_kn", [128, 8], F32, kind="ExternalInput").ap()
    g_v = nc.dram_tensor("g_v", [128, 8], F32, kind="ExternalInput").ap()
    b_vn = nc.dram_tensor("b_vn", [128, 8], F32, kind="ExternalInput").ap()
    qb_c = nc.dram_tensor("qb_c", [128, 8], F32, kind="ExternalInput").ap()
    kb_c = nc.dram_tensor("kb_c", [128, 8], F32, kind="ExternalInput").ap()
    vb_c = nc.dram_tensor("vb_c", [128, 8], F32, kind="ExternalInput").ap()
    ob_r = nc.dram_tensor("ob_r", [1, 1024], F32, kind="ExternalInput").ap()
    beta_r = nc.dram_tensor("beta_r", [1, 16], F32, kind="ExternalInput").ap()
    rw = nc.dram_tensor("rw", [128, 8, 16], F32, kind="ExternalInput").ap()
    rb_r = nc.dram_tensor("rb_r", [1, 16], F32, kind="ExternalInput").ap()
    w1d = nc.dram_tensor("w1", [16, 8, 128, 2048], F32, kind="ExternalInput").ap()
    b1c = nc.dram_tensor("b1c", [16, 128, 16], F32, kind="ExternalInput").ap()
    w2d = nc.dram_tensor("w2", [16, 16, 128, 1024], F32, kind="ExternalInput").ap()
    b2d = nc.dram_tensor("b2", [16, 1024], F32, kind="ExternalInput").ap()
    identd = nc.dram_tensor("ident", [128, 128], BF16, kind="ExternalInput").ap()
    ident32d = nc.dram_tensor("ident32", [128, 128], F32, kind="ExternalInput").ap()
    out = nc.dram_tensor("out", [2048, 1024], F32, kind="ExternalOutput").ap()

    def scratch(name, shape):
        kind = "ExternalOutput" if name in dbg else "Internal"
        return nc.dram_tensor(name, shape, BF16, kind=kind).ap()

    qT_s = scratch("qT_s", [8, 128, 2048])
    kT_s = scratch("kT_s", [8, 128, 4096])
    otn_s = scratch("otn_s", [8, 128, 2048])
    vAug_s = nc.dram_tensor("vAug_s", [8, 128, 32, 130], BF16,
                            kind="ExternalOutput" if "vaug" in dbg else "Internal").ap()
    acc_d = nc.dram_tensor("acc_d", [16, 128, 1024], F32,
                           kind="ExternalOutput" if "acc" in dbg else "Internal").ap()
    if "c" in dbg:
        c_dbg = nc.dram_tensor("c_dbg", [128, 256], F32, kind="ExternalOutput").ap()

    CTX = ExitStack()
    with CTX:
        P = Prog(nc, CTX)

        def finish():
            with nc.Block() as block:
                for eng in ENGINES:
                    items = P.streams[eng]
                    if items:
                        getattr(block, eng)(lambda e, it=items: _emit_body(it, P.sems, P.dma_sems, e))

        GE = CTX.enter_context
        ident_sb = GE(nc.sbuf_tensor("ident_sb", [128, 128], BF16))
        ones_cb = GE(nc.sbuf_tensor("ones_cb", [128, 1], BF16))
        ones_rf = GE(nc.sbuf_tensor("ones_rf", [1, 128], F32))
        Bb = GE(nc.sbuf_tensor("Bb", [128, 16], F32))
        BbK = GE(nc.sbuf_tensor("BbK", [128, 8], F32))
        c_sb = GE(nc.sbuf_tensor("c_sb", [128, 256], F32))
        cT_sb = GE(nc.sbuf_tensor("cT_sb", [16, 2048], BF16))
        negmu1 = GE(nc.sbuf_tensor("negmu1", [128, 16], F32))
        rcol = GE(nc.sbuf_tensor("rcol", [128, 8], F32))
        xnT_sb = GE(nc.sbuf_tensor("xnT_sb", [128, 8, 2048], BF16))
        r1sb = GE(nc.sbuf_tensor("r1sb", [128, 16], F32))

        PCTX = ExitStack()
        PE_ = PCTX.enter_context
        qwT_sb = PE_(nc.sbuf_tensor("qwT_sb", [128, 8, 1024], BF16))
        kwT_sb = PE_(nc.sbuf_tensor("kwT_sb", [128, 8, 1024], BF16))
        vwT_sb = PE_(nc.sbuf_tensor("vwT_sb", [128, 8, 1024], BF16))
        rw_sb = PE_(nc.sbuf_tensor("rw_sb", [128, 8, 16], F32))
        ident32_sb = PE_(nc.sbuf_tensor("ident32_sb", [128, 128], F32))
        xnT32 = PE_(nc.sbuf_tensor("xnT32", [128, 8, 128], F32))
        gq_sb = PE_(nc.sbuf_tensor("gq_sb", [128, 8], F32))
        bq_sb = PE_(nc.sbuf_tensor("bq_sb", [128, 8], F32))
        gk_sb = PE_(nc.sbuf_tensor("gk_sb", [128, 8], F32))
        bk_sb = PE_(nc.sbuf_tensor("bk_sb", [128, 8], F32))
        gv_sb = PE_(nc.sbuf_tensor("gv_sb", [128, 8], F32))
        bv_sb = PE_(nc.sbuf_tensor("bv_sb", [128, 8], F32))
        qbc_sb = PE_(nc.sbuf_tensor("qbc_sb", [128, 8], F32))
        kbc_sb = PE_(nc.sbuf_tensor("kbc_sb", [128, 8], F32))
        rb_bc = PE_(nc.sbuf_tensor("rb_bc", [128, 16], F32))
        brow = PE_(nc.sbuf_tensor("brow", [1, 16], F32))
        rbrow = PE_(nc.sbuf_tensor("rbrow", [1, 16], F32))
        sprow = PE_(nc.sbuf_tensor("sprow", [1, 16], F32))
        sprow2 = PE_(nc.sbuf_tensor("sprow2", [1, 16], F32))
        xin = [PE_(nc.sbuf_tensor(f"xin{i}", [128, 1024], F32)) for i in range(2)]
        xwork2 = [PE_(nc.sbuf_tensor(f"xwork{i}", [128, 1024], F32)) for i in range(4)]
        xsq2 = [PE_(nc.sbuf_tensor(f"xsq{i}", [128, 1024], F32)) for i in range(2)]
        xscr = PE_(nc.sbuf_tensor("xscr", [128, 1024], F32))
        zb2 = [PE_(nc.sbuf_tensor(f"zb{i}", [128, 1024], BF16)) for i in range(4)]
        zT = [PE_(nc.sbuf_tensor(f"zT{i}", [128, 8, 512], BF16)) for i in range(2)]
        fT_st = PE_(nc.sbuf_tensor("fT_st", [128, 8, 512], BF16))
        vstg = [PE_(nc.sbuf_tensor(f"vstg{i}", [128, 1040], BF16)) for i in range(4)]
        proj = [PE_(nc.sbuf_tensor(f"proj{i}", [128, 8, 512], BF16)) for i in range(2)]
        sqb = fT_st
        rnb = PE_(nc.sbuf_tensor("rnb", [128, 512], F32))
        nrm = PE_(nc.sbuf_tensor("nrm", [1, 512], F32))
        colb = PE_(nc.sbuf_tensor("colb", [128, 20], F32))
        eps_cell = PE_(nc.sbuf_tensor("eps_cell", [128, 1], F32))
        lgt2 = [PE_(nc.sbuf_tensor(f"lgt{i}", [128, 16], F32)) for i in range(2)]
        p162 = [PE_(nc.sbuf_tensor(f"p16_{i}", [128, 16], F32)) for i in range(2)]
        cur2 = [PE_(nc.sbuf_tensor(f"cur{i}", [128, 16], F32)) for i in range(2)]
        msk2 = [PE_(nc.sbuf_tensor(f"msk{i}", [128, 16], F32)) for i in range(2)]
        cbf = PE_(nc.sbuf_tensor("cbf", [128, 256], BF16))

        ps_proj = [PE_(nc.psum_tensor(f"ps_proj{i}", [128, 512], F32)) for i in range(2)]
        ps_tr = [PE_(nc.psum_tensor(f"ps_tr{i}", [128, 128], BF16)) for i in range(2)]
        ps_tr32 = [PE_(nc.psum_tensor(f"ps_tr32{i}", [128, 128], F32)) for i in range(2)]
        ps_rb = PE_(nc.psum_tensor("ps_rb", [128, 512], F32))
        ps_ss = ps_rb[0:1, 0:512]
        ps_rt = PE_(nc.psum_tensor("ps_rt", [128, 16], F32))

        tr_evac = [0, 0]
        tr32_evac = [0, 0]
        xwork_free = [0, 0, 0, 0]
        zb_free = [0, 0, 0, 0]
        zT_used = [0, 0]
        proj_free = [0, 0]
        zT_ready = [0]
        rt_free = [0]
        nrm_free = [0]
        rnb_free = [0]
        fT_free = [0]
        proj_evac = [0, 0]
        vproj_evac = [0, 0]
        xin_ready = [0, 0]
        xin_free = [0, 0]
        vstg_free = [0, 0, 0, 0]

        P.dma("sync", lambda e: e.dma_start(out=ident_sb[:], in_=identd), sem="cS")
        P.dma("sync", lambda e: e.dma_start(out=ident32_sb[:], in_=ident32d), sem="cS")
        P.dma("sync", lambda e: e.dma_start(out=rw_sb[:], in_=rw), sem="cS")
        for dst, src in ((gq_sb, g_q), (bq_sb, b_qn), (gk_sb, g_k), (bk_sb, b_kn),
                         (gv_sb, g_v), (bv_sb, b_vn), (qbc_sb, qb_c), (kbc_sb, kb_c)):
            P.dma("sync", lambda e, d=dst, s=src: e.dma_start(out=d[:], in_=s), sem="cS")
        P.dma("sync", lambda e: e.dma_start(out=brow[:], in_=beta_r), sem="cS")
        P.dma("sync", lambda e: e.dma_start(out=rbrow[:], in_=rb_r), sem="cS")
        dvb = P.dma("sync", lambda e: e.dma_start(out=rcol[:], in_=vb_c), sem="cS")
        for dst, src in ((qwT_sb, qwT), (kwT_sb, kwT), (vwT_sb, vwT)):
            P.dma("gpsimd", lambda e, d=dst, s=src: e.dma_start(
                out=d[:], in_=s.rearrange("(a p) n -> p a n", p=128)), sem="cG")
        cS0 = P.now("cS")
        cG0 = P.now("cG")

        vv0 = P.op("vector", lambda e: e.memset(ones_cb[:], 1.0), inc="vE")
        vv0 = P.op("vector", lambda e: e.memset(eps_cell[:], 1e-5), inc="vE")
        nc.const_aps.register(F32, 1e-5, eps_cell[:])
        vv0 = P.op("vector", lambda e: e.memset(ones_rf[:], 1.0), inc="vE")
        for _vi in range(4):
            vv0 = P.op("vector", lambda e, i=_vi: e.memset(vstg[i][:].rearrange("p (a b) -> p a b", a=16)[:, :, 64:65], 1.0), inc="vE")

        t1 = P.op("tensor", lambda e: e.matmul(ps_rt[:], ones_rf[:], rbrow[:], start=True, stop=True),
                  waits=[("cS", cS0), ("vE", vv0)], inc="tE")
        vrb = P.op("vector", lambda e: e.tensor_copy(rb_bc[:], ps_rt[:]), waits=[("tE", t1)], inc="vE")

        s1 = P.op("scalar", lambda e: e.activation(sprow[:], brow[:], AF.Abs), waits=[("cS", cS0)], inc="sE")
        s1 = P.op("scalar", lambda e: e.activation(sprow[:], sprow[:], AF.Exp, scale=-1.0), waits=[("sE", s1)], inc="sE")
        v1b = P.op("vector", lambda e: e.tensor_scalar_add(sprow[:], sprow[:], 1.0), waits=[("sE", s1)], inc="vE")
        s1 = P.op("scalar", lambda e: e.activation(sprow[:], sprow[:], AF.Ln), waits=[("vE", v1b)], inc="sE")
        v2 = P.op("vector", lambda e: e.tensor_scalar_max(sprow2[:], brow[:], 0.0), waits=[("cS", cS0)], inc="vE")
        v2 = P.op("vector", lambda e: e.tensor_add(sprow[:], sprow[:], sprow2[:]), waits=[("sE", s1), ("vE", v2)], inc="vE")
        v2 = P.op("vector", lambda e: e.tensor_scalar(sprow[:], sprow[:], 0.1, 10.0, OP.max, OP.min), waits=[("vE", v2)], inc="vE")
        v2 = P.op("vector", lambda e: e.tensor_scalar_mul(sprow[:], sprow[:], 0.125), waits=[("vE", v2)], inc="vE")
        t2 = P.op("tensor", lambda e: e.matmul(ps_rt[:], ones_rf[:], sprow[:], start=True, stop=True),
                  waits=[("vE", v2)], inc="tE")
        vBb = P.op("vector", lambda e: e.tensor_copy(Bb[:], ps_rt[:]), waits=[("tE", t2)], inc="vE")
        rt_free[0] = vBb
        vBb = P.op("vector", lambda e: e.tensor_copy(BbK[0:64, :], Bb[0:64, 0:16:2]),
                   waits=[("vE", vBb)], inc="vE")
        vBb = P.op("vector", lambda e: e.tensor_copy(BbK[64:128, :], Bb[64:128, 1:16:2]),
                   waits=[("vE", vBb)], inc="vE")

        bqb = PE_(nc.sbuf_tensor("bqb", [128, 8], BF16))
        bkb = PE_(nc.sbuf_tensor("bkb", [128, 8], BF16))
        bvb = PE_(nc.sbuf_tensor("bvb", [128, 8], BF16))
        vbc = 0
        for dst, srcb in ((bqb, bq_sb), (bkb, bk_sb), (bvb, bv_sb)):
            vbc = P.op("vector", lambda e, d=dst, s=srcb: e.tensor_copy(d[:], s[:]),
                       waits=[("cS", cS0)], inc="vE")
        for pi, (wsb, bcol, biasc, cw) in enumerate((
                (qwT_sb, bqb, qbc_sb, 0), (kwT_sb, bkb, kbc_sb, 0), (vwT_sb, bvb, rcol, 8))):
            bank = pi % 2
            for oc in range(8):
                tf = P.op("tensor", lambda e, o=oc, w=wsb, b=bcol, p=bank, c=cw: e.matmul(
                    ps_proj[p][:, c + o:c + o + 1], w[:, 0, o * 128:(o + 1) * 128], b[:, 0:1],
                    start=True, stop=False),
                    waits=[("cG", cG0), ("vE", vbc)], inc=None)
                for d in range(1, 8):
                    tf = P.op("tensor", lambda e, o=oc, w=wsb, b=bcol, p=bank, c=cw, dd=d: e.matmul(
                        ps_proj[p][:, c + o:c + o + 1], w[:, dd, o * 128:(o + 1) * 128], b[:, dd:dd + 1],
                        start=False, stop=(dd == 7)),
                        inc="tE" if (d == 7) else None)
            rnb_free[0] = P.op("vector", lambda e, bi=biasc, p=bank, c=cw: e.tensor_add(
                bi[:], bi[:], ps_proj[p][:, c:c + 8]),
                waits=[("tE", tf), ("cS", max(cS0, dvb))], inc="vE")
        fold_done = P.now("tE")
        wsc = 0
        for wsb, gcol in ((qwT_sb, gq_sb), (kwT_sb, gk_sb), (vwT_sb, gv_sb)):
            for d in range(8):
                wsc = P.op("vector", lambda e, w=wsb, g=gcol, dd=d: e.tensor_single_scalar(
                    w[:, dd, :], w[:, dd, :], g[:, dd:dd + 1], OP.mult),
                    waits=[("tE", fold_done), ("cS", cS0), ("cG", cG0)], inc="vE")
        vproj_evac[0] = vproj_evac[1] = wsc


        def load_x(tt):
            slot = tt % 2
            xin_ready[slot] = P.dma("sync", lambda e, t=tt, s=slot: e.dma_start(
                out=xin[s][:], in_=x[t * 128:(t + 1) * 128, :]),
                waits=[("vE", xin_free[slot])], sem=f"xin{slot}")

        load_x(0)
        load_x(1)

        def transpose_to(dst_fn, src_fn, extra):
            for dc in range(8):
                sl = dc % 2
                tval = P.op("tensor", lambda e, d=dc, s=sl: e.transpose(
                    ps_tr[s][:], src_fn(d), ident_sb[:]),
                    waits=list(extra) + [("sE", tr_evac[sl])], inc="tE")
                tr_evac[sl] = P.op("scalar", lambda e, d=dc, s=sl: e.copy(dst_fn(d), ps_tr[s][:]),
                                   waits=[("tE", tval)], inc="sE")

        def ln_pair(tt_a, tt_b):
            sts = []
            for tt in (tt_a, tt_b):
                p = tt % 2
                cc = [colb[:, 8 * p + j:8 * p + j + 1] for j in range(8)]
                sts.append(dict(
                    tt=tt, p=p, q4=tt % 4, slot=tt % 2, xw=xwork2[tt % 4], xq=xsq2[p], zb=zb2[tt % 4], cc=cc,
                    nm1=negmu1[:, tt:tt + 1] if tt < 16 else cc[6],
                    r1c=r1sb[:, tt:tt + 1] if tt < 16 else cc[7]))
            for st in sts:
                st["v"] = P.op("vector", lambda e, s=st["slot"], a=st["cc"][0]: e.reduce_sum(a, xin[s][:], axis=AX.X),
                               waits=[(f"xin{st['slot']}", xin_ready[st["slot"]])], inc="vE")
            for st in sts:
                st["v"] = P.op("vector", lambda e, d=st["nm1"], a=st["cc"][0]: e.tensor_scalar_mul(d, a, -1.0 / 1024.0),
                               waits=[("vE", st["v"])], inc="vE")
            for st in sts:
                st["v"] = P.op("vector", lambda e, s=st["slot"], d=st["nm1"], w=st["xw"]: e.tensor_scalar_add(w[:], xin[s][:], d),
                               waits=[("vE", st["v"]), ("tE", xwork_free[st["q4"]])], inc="vE")
                xin_free[st["slot"]] = st["v"]
            for st in sts:
                if st["tt"] + 2 < 32:
                    load_x(st["tt"] + 2)
            for st in sts:
                st["s"] = P.op("scalar", lambda e, w=st["xw"], a=st["cc"][1]: e.activation(xscr[:], w[:], AF.Square, accum_out=a),
                               waits=[("vE", st["v"])], inc="sE")
            for st in sts:
                st["s"] = P.op("scalar", lambda e, a=st["cc"][2], b=st["cc"][1]: e.activation(a, b, AF.Ln, scale=1.0 / 1024.0, bias=1e-5),
                               waits=[("sE", st["s"])], inc="sE")
            for st in sts:
                st["s"] = P.op("scalar", lambda e, d=st["r1c"], a=st["cc"][2]: e.activation(d, a, AF.Exp, scale=-0.5),
                               waits=[("sE", st["s"])], inc="sE")
            for st in sts:
                st["v"] = P.op("vector", lambda e, d=st["r1c"], w=st["xw"]: e.tensor_single_scalar(w[:], w[:], d, OP.mult),
                               waits=[("sE", st["s"])], inc="vE")
                st["vxn"] = st["v"]
            for st in sts:
                st["v"] = P.op("vector", lambda e, w=st["xw"], a=st["cc"][3]: e.reduce_sum(a, w[:], axis=AX.X),
                               waits=[("vE", st["v"])], inc="vE")
            for st in sts:
                st["v"] = P.op("vector", lambda e, a=st["cc"][4], b=st["cc"][3]: e.tensor_scalar_mul(a, b, -1.0 / 1024.0),
                               waits=[("vE", st["v"])], inc="vE")
            for st in sts:
                st["v"] = P.op("vector", lambda e, w=st["xw"], q=st["xq"], a=st["cc"][4]: e.tensor_scalar_add(q[:], w[:], a),
                               waits=[("vE", st["v"])], inc="vE")
            for st in sts:
                st["s"] = P.op("scalar", lambda e, q=st["xq"], a=st["cc"][5]: e.activation(xscr[:], q[:], AF.Square, accum_out=a),
                               waits=[("vE", st["v"])], inc="sE")
            for st in sts:
                st["s"] = P.op("scalar", lambda e, a=st["cc"][5]: e.activation(a, a, AF.Ln, scale=1.0 / 1024.0, bias=1e-5),
                               waits=[("sE", st["s"])], inc="sE")
            for st in sts:
                st["s"] = P.op("scalar", lambda e, a=st["cc"][5]: e.activation(a, a, AF.Exp, scale=-0.5),
                               waits=[("sE", st["s"])], inc="sE")
            for st in sts:
                st["vz"] = P.op("vector", lambda e, z=st["zb"], q=st["xq"], a=st["cc"][5]: e.tensor_single_scalar(z[:], q[:], a, OP.mult),
                               waits=[("sE", st["s"]), ("tE", zb_free[st["q4"]])], inc="vE")
            return sts

        def trrouter(st, zsl):
            tt, p, ti = st["tt"], st["p"], st["tt"] % 4
            transpose_to(
                lambda dc, t=ti, zz=zsl: zT[zz][:, dc, t * 128:(t + 1) * 128],
                lambda dc, z=st["zb"]: z[:, dc * 128:(dc + 1) * 128],
                [("vE", st["vz"]), ("tE", zT_used[zsl])])
            zb_free[tt % 4] = P.now("tE")
            if tt < 16 and router:
                ev2 = 0
                for dc in range(8):
                    tsl = dc % 2
                    tval = P.op("tensor", lambda e, d=dc, s=tsl, w=st["xw"]: e.transpose(
                        ps_tr32[s][:], w[:, d * 128:(d + 1) * 128], ident32_sb[:]),
                        waits=[("vE", st["vxn"]), ("sE", tr32_evac[tsl])], inc="tE")
                    P.op("scalar", lambda e, d=dc, s=tsl: e.copy(xnT32[:, d, :], ps_tr32[s][:]),
                         waits=[("tE", tval)], inc="sE")
                    ev2 = P.op("scalar", lambda e, d=dc, s=tsl, t=tt: e.copy(
                        xnT_sb[:, d, t * 128:(t + 1) * 128], ps_tr32[s][:]),
                        waits=[("tE", tval)], inc="sE")
                    tr32_evac[tsl] = ev2
                xwork_free[tt % 4] = P.now("tE")
                for dc in range(8):
                    tv = P.op("tensor", lambda e, d=dc: e.matmul(
                        ps_rt[:], xnT32[:, d, :], rw_sb[:, d, :],
                        start=(d == 0), stop=(d == 7)),
                        waits=[("sE", ev2), ("cS", cS0), ("vE", rt_free[0])],
                        inc="tE" if dc == 7 else None)
                st["rv"] = P.op("vector", lambda e, g=lgt2[p]: e.tensor_add(g[:], ps_rt[:], rb_bc[:]),
                                waits=[("tE", tv), ("vE", vrb)], inc="vE")
                rt_free[0] = st["rv"]

        def softmax_pair(sts):
            for st in sts:
                p = st["p"]
                st["g"] = lgt2[p]
                st["pp"] = p162[p]
                st["cu"] = cur2[p]
                st["mk"] = msk2[p]
                st["c0"] = colb[:, 16 + 2 * p:17 + 2 * p]
                st["c1"] = colb[:, 17 + 2 * p:18 + 2 * p]
                st["v"] = st["rv"]

            def V(st, fn, waits_extra=()):
                st["v"] = P.op("vector", fn, waits=[("vE", st["v"])] + list(waits_extra), inc="vE")

            for st in sts:
                V(st, lambda e, g=st["g"], a=st["c0"]: e.reduce_max(a, g[:], axis=AX.X))
            for st in sts:
                V(st, lambda e, g=st["g"], a=st["c0"]: e.tensor_single_scalar(g[:], g[:], a, OP.subtract))
            for st in sts:
                st["s"] = P.op("scalar", lambda e, g=st["g"], q=st["pp"]: e.activation(q[:], g[:], AF.Exp),
                               waits=[("vE", st["v"])], inc="sE")
            for st in sts:
                st["v"] = P.op("vector", lambda e, q=st["pp"], a=st["c0"]: e.reduce_sum(a, q[:], axis=AX.X),
                               waits=[("sE", st["s"])], inc="vE")
            for st in sts:
                V(st, lambda e, a=st["c1"], b=st["c0"]: e.reciprocal(a, b))
            for st in sts:
                V(st, lambda e, q=st["pp"], a=st["c1"]: e.tensor_single_scalar(q[:], q[:], a, OP.mult))
            for st in sts:
                V(st, lambda e, q=st["pp"], c=st["cu"]: e.tensor_copy(c[:], q[:]))
            for _ in range(3):
                for st in sts:
                    V(st, lambda e, c=st["cu"], a=st["c0"]: e.reduce_max(a, c[:], axis=AX.X))
                for st in sts:
                    V(st, lambda e, c=st["cu"], m=st["mk"], a=st["c0"]: e.tensor_single_scalar(m[:], c[:], a, OP.is_ge))
                for st in sts:
                    V(st, lambda e, c=st["cu"], m=st["mk"]: e.scalar_tensor_tensor(c[:], m[:], -2.0, c[:], OP.mult, OP.add))
            for st in sts:
                V(st, lambda e, c=st["cu"], a=st["c0"]: e.reduce_max(a, c[:], axis=AX.X))
            for st in sts:
                V(st, lambda e, q=st["pp"], m=st["mk"], a=st["c0"]: e.tensor_single_scalar(m[:], q[:], a, OP.is_ge))
            for st in sts:
                V(st, lambda e, q=st["pp"], m=st["mk"]: e.tensor_mul(q[:], q[:], m[:]))
            for st in sts:
                V(st, lambda e, q=st["pp"], a=st["c0"]: e.reduce_sum(a, q[:], axis=AX.X))
            for st in sts:
                V(st, lambda e, a=st["c1"], b=st["c0"]: e.reciprocal(a, b))
            for st in sts:
                V(st, lambda e, q=st["pp"], a=st["c1"], t=st["tt"]: e.tensor_single_scalar(
                    c_sb[:, t * 16:(t + 1) * 16], q[:], a, OP.mult))

        def proj_mms(wsb, bias_col, zsl, psl):
            for oc in range(8):
                for dc in range(8):
                    tvv = P.op("tensor", lambda e, d=dc, o=oc, w=wsb, zz=zsl: e.matmul(
                        ps_proj[o % 2][:], w[:, d, o * 128:(o + 1) * 128], zT[zz][:, d, :],
                        start=(d == 0), stop=(d == 7)),
                        waits=[("sE", zT_ready[0]), ("vE", vproj_evac[oc % 2]), ("cG", cG0),
                               ("sE", proj_evac[oc % 2])],
                        inc="tE" if dc == 7 else None)
                proj_evac[oc % 2] = P.op("scalar", lambda e, o=oc, b=bias_col, pp=psl: e.activation(
                    proj[pp][:, o, :], ps_proj[o % 2][:], AF.Identity, bias=b[:, o:o + 1]),
                    waits=[("tE", tvv), ("vE", proj_free[psl])], inc="sE")

        def make_fin(psl, is_q, spill_dst):
            ev0, ev1 = proj_evac[0], proj_evac[1]

            def fin():
                vq = P.op("vector", lambda e, pp=psl: e.tensor_mul(sqb[:], proj[pp][:], proj[pp][:]),
                          waits=[("sE", max(ev0, ev1)), ("sfT", fT_free[0])], inc="vE")
                for oc in range(8):
                    tss = P.op("tensor", lambda e, o=oc: e.matmul(
                        ps_ss[:], ones_cb[:], sqb[:, o, :], start=(o == 0), stop=(o == 7)),
                        waits=[("vE", vq), ("sE", nrm_free[0])],
                        inc="tE" if oc == 7 else None)
                sn = P.op("scalar", lambda e: e.activation(nrm[:], ps_ss[:], AF.Ln), waits=[("tE", tss)], inc="sE")
                nrm_free[0] = sn
                sn = P.op("scalar", lambda e: e.activation(nrm[:], nrm[:], AF.Exp, scale=0.5), waits=[("sE", sn)], inc="sE")
                vn_ = P.op("vector", lambda e: e.tensor_scalar_max(nrm[:], nrm[:], 1e-6), waits=[("sE", sn)], inc="vE")
                sn = P.op("scalar", lambda e: e.activation(nrm[:], nrm[:], AF.Ln), waits=[("vE", vn_)], inc="sE")
                sn = P.op("scalar", lambda e: e.activation(nrm[:], nrm[:], AF.Exp, scale=-1.0), waits=[("sE", sn)], inc="sE")
                tb = P.op("tensor", lambda e: e.matmul(ps_rb[:], ones_rf[:], nrm[:], start=True, stop=True),
                          waits=[("sE", sn), ("vE", rnb_free[0])], inc="tE")
                vr = P.op("vector", lambda e: e.tensor_copy(rnb[:], ps_rb[:]), waits=[("tE", tb)], inc="vE")
                rnb_free[0] = vr
                if is_q:
                    lastv = P.op("vector", lambda e, pp=psl: e.tensor_tensor(
                        fT_st[:], proj[pp][:],
                        rnb[:].unsqueeze(1).broadcast_to((128, 8, 512)), OP.mult),
                        waits=[("vE", vr), ("sfT", fT_free[0])], inc="vE")
                else:
                    lastv = 0
                    for oc in range(8):
                        lastv = P.op("vector", lambda e, o=oc, pp=psl: e.scalar_tensor_tensor(
                            fT_st[:, o, :], proj[pp][:, o, :], BbK[:, o:o + 1],
                            rnb[:], OP.mult, OP.mult),
                            waits=[("vE", vr), ("sfT", fT_free[0])], inc="vE")
                proj_free[psl] = lastv
                fT_free[0] = P.dma("sync", lambda e, d=spill_dst: e.dma_start(
                    out=d, in_=fT_st[:]), waits=[("vE", lastv)], sem="sfT")
            return fin

        p_ctr = [0]
        pending_fin = [None]

        def run_pending():
            if pending_fin[0] is not None:
                pending_fin[0]()
                pending_fin[0] = None

        pend_ln = [None]
        for gi in range(8):
            t0 = gi * 4
            zsl = gi % 2
            if pend_ln[0] is None:
                stsA = ln_pair(t0, t0 + 1)
                trrouter(stsA[0], zsl)
                trrouter(stsA[1], zsl)
                stsB = ln_pair(t0 + 2, t0 + 3)
            else:
                stsA, stsB = pend_ln[0]
                trrouter(stsA[0], zsl)
                trrouter(stsA[1], zsl)
            if t0 < 16 and router:
                softmax_pair(stsA)
            trrouter(stsB[0], zsl)
            trrouter(stsB[1], zsl)
            if t0 < 16 and router:
                softmax_pair(stsB)
            zT_ready[0] = P.now("sE")
            nxt = [None, None]

            if gi < 4:
                psl = p_ctr[0] % 2
                p_ctr[0] += 1
                proj_mms(qwT_sb, qbc_sb, zsl, psl)
                run_pending()
                pending_fin[0] = make_fin(
                    psl, True,
                    qT_s[:, :, gi * 512:(gi + 1) * 512].rearrange("a p t -> p a t"))
                if gi < 7:
                    nxt[0] = ln_pair(t0 + 4, t0 + 5)
            psl = p_ctr[0] % 2
            p_ctr[0] += 1
            proj_mms(kwT_sb, kbc_sb, zsl, psl)
            run_pending()
            pending_fin[0] = make_fin(
                psl, False,
                kT_s[:, :, gi * 512:(gi + 1) * 512].rearrange("a p t -> p a t"))
            if gi < 7 and nxt[0] is None:
                nxt[0] = ln_pair(t0 + 4, t0 + 5)
            elif gi < 7:
                nxt[1] = ln_pair(t0 + 6, t0 + 7)
            for ti in range(4):
                tt = gi * 4 + ti
                st2 = ti
                ev_last = 0
                for oc2 in range(2):
                    sl = (ti * 2 + oc2) % 2
                    for dc in range(8):
                        tvv = P.op("tensor", lambda e, d=dc, t=ti, o=oc2, s=sl, zz=zsl: e.matmul(
                            ps_proj[s][:], zT[zz][:, d, t * 128:(t + 1) * 128],
                            vwT_sb[:, d, o * 512:(o + 1) * 512],
                            start=(d == 0), stop=(d == 7)),
                            waits=[("sE", zT_ready[0]), ("vE", vproj_evac[sl]), ("sE", proj_evac[sl])],
                            inc="tE" if dc == 7 else None)
                    vproj_evac[sl] = P.op("vector", lambda e, o=oc2, s=sl, g=st2: e.tensor_copy(
                        vstg[g][:].rearrange("p (a b) -> p a b", a=16)[:, o * 8:(o + 1) * 8, 0:64],
                        ps_proj[s][:].rearrange("p (a t) -> p a t", a=8)),
                        waits=[("tE", tvv), (f"svau{st2}", vstg_free[st2])], inc="vE")
                    ev_last = vproj_evac[sl]
                vstg_free[st2] = P.dma("sync", lambda e, t=tt, g=st2: e.dma_start(
                    out=vAug_s[:, :, t, :].rearrange("a p b -> p a b"),
                    in_=vstg[g][:].rearrange("p (a b) -> p a b", a=8)),
                    waits=[("vE", ev_last)], sem=f"svau{st2}")
            zT_used[zsl] = P.now("tE")
            run_pending()
            if gi < 7 and nxt[1] is None:
                nxt[1] = ln_pair(t0 + 6, t0 + 7)
            pend_ln[0] = (nxt[0], nxt[1]) if gi < 7 else None

        vcb = P.op("vector", lambda e: e.tensor_copy(cbf[:], c_sb[:]), waits=[("vE", P.now("vE"))], inc="vE")
        for tt in range(16):
            sl = tt % 2
            tvv = P.op("tensor", lambda e, t=tt, s=sl: e.transpose(
                ps_tr[s][0:16, :], cbf[:, t * 16:(t + 1) * 16], ident_sb[:]),
                waits=[("vE", vcb), ("sE", tr_evac[sl])], inc="tE")
            tr_evac[sl] = P.op("scalar", lambda e, t=tt, s=sl: e.copy(
                cT_sb[0:16, t * 128:(t + 1) * 128], ps_tr[s][0:16, :]),
                waits=[("tE", tvv)], inc="sE")
        if "c" in dbg:
            P.dma("sync", lambda e: e.dma_start(out=c_dbg, in_=c_sb[:]), waits=[("vE", P.now("vE"))], sem="dbg")

        P.barrier()
        PCTX.close()

        if "A" not in stages:
            finish()
            return nc

        FCTX = ExitStack()
        FE = FCTX.enter_context
        h1T = FE(nc.sbuf_tensor("h1T", [128, 8, 2048], BF16))
        w1b = [FE(nc.sbuf_tensor(f"w1b{i}", [128, 8, 1024], BF16)) for i in range(2)]
        w2b = [FE(nc.sbuf_tensor(f"w2b{i}", [128, 8, 1024], BF16)) for i in range(2)]
        b1sb = [FE(nc.sbuf_tensor(f"b1sb{i}", [128, 16], F32)) for i in range(2)]
        b2sb = FE(nc.sbuf_tensor("b2sb", [16, 1024], BF16))
        ffst = [FE(nc.sbuf_tensor(f"ffst{i}", [128, 1024], F32)) for i in range(2)]
        qToc = [FE(nc.sbuf_tensor(f"qToc{i}", [128, 2048], BF16)) for i in range(2)]
        kToc = [FE(nc.sbuf_tensor(f"kToc{i}", [128, 4096], BF16)) for i in range(2)]
        vAugP = [FE(nc.sbuf_tensor(f"vAugP{i}", [128, 32, 130], BF16)) for i in range(2)]
        PT = [FE(nc.sbuf_tensor(f"PT{i}", [128, 512], BF16)) for i in range(2)]
        den_sb = [FE(nc.sbuf_tensor(f"den_sb{i}", [1, 512], F32)) for i in range(2)]
        rden = [FE(nc.sbuf_tensor(f"rden{i}", [1, 512], F32)) for i in range(2)]
        rdb = FE(nc.sbuf_tensor("rdb", [64, 512], F32))
        otn_st = [FE(nc.sbuf_tensor(f"otn_st{i}", [128, 512], BF16)) for i in range(2)]
        ps_h2 = [FE(nc.psum_tensor(f"ps_h{i}", [128, 512], F32)) for i in range(2)]
        ps_o2 = [FE(nc.psum_tensor(f"ps_o{i}", [128, 512], F32)) for i in range(2)]
        ps_s2 = [FE(nc.psum_tensor(f"ps_s{i}", [128, 512], F32)) for i in range(2)]
        ps_av = [FE(nc.psum_tensor(f"ps_av{i}", [128, 512], F32)) for i in range(2)]
        ps_rd = ps_s2[0][0:64, 0:512]

        b2_rd = P.dma("gpsimd", lambda e: e.dma_start(out=b2sb[:], in_=b2d), sem="sb2")

        kq_free = [0, 0]
        kT_rd = [0, 0]
        qT_rd = [0, 0]
        va_rd = [0, 0]
        exp_done = [0, 0]
        av_done = [0, 0]
        otn_free = [0, 0]
        avps_free = [0, 0]
        rdb_free = [0]
        rden_free = [0, 0]

        def load_oc(o):
            sl = o % 2
            kT_rd[sl] = P.dma("sync", lambda e, oo=o, s=sl: e.dma_start(out=kToc[s][:], in_=kT_s[oo]),
                              waits=[("tE", kq_free[sl])], sem=f"skT{sl}")
            qT_rd[sl] = P.dma("sync", lambda e, oo=o, s=sl: e.dma_start(out=qToc[s][:], in_=qT_s[oo]),
                              waits=[("tE", kq_free[sl])], sem=f"sqT{sl}")
            va_rd[sl] = P.dma("sync", lambda e, oo=o, s=sl: e.dma_start(
                out=vAugP[s][:], in_=vAug_s[oo]),
                waits=[("tE", kq_free[sl])], sem=f"sva{sl}")

        def a_gen():
            load_oc(0)
            load_oc(1)
            for oc in range(8):
                sl = oc % 2

                def scores(hh, m, n, s_=sl):
                    pb = hh * 64
                    tp = (64, 0) if hh else (0, 0)
                    return P.op("tensor", lambda e, s=hh, mm=m, p=pb, t=tp, nn=n, sll=s_: e.matmul(
                        ps_s2[s][:], kToc[sll][p:p + 64, mm * 128:(mm + 1) * 128],
                        qToc[sll][p:p + 64, nn * 512:(nn + 1) * 512],
                        start=True, stop=True, tile_position=t),
                        waits=[(f"skT{s_}", kT_rd[s_]), (f"sqT{s_}", qT_rd[s_]),
                               ("sE", exp_done[hh])] +
                              ([("vE", rdb_free[0])] if hh == 0 else []),
                        inc="tE")

                for n in range(4):
                    sc = [scores(0, 0, n), scores(1, 0, n)]
                    for m in range(32):
                        for hh in range(2):
                            exp_done[hh] = P.op("scalar", lambda e, s=hh: e.activation(
                                PT[s][:], ps_s2[s][:], AF.Exp),
                                waits=[("tE", sc[hh]), ("tE", av_done[hh])], inc="sE")
                        if m + 1 < 32:
                            sc = [scores(0, m + 1, n), scores(1, m + 1, n)]
                        for hh in range(2):
                            av_done[hh] = P.op("tensor", lambda e, s=hh, mm=m, sll=sl: e.matmul(
                                ps_av[s][0:65, :],
                                vAugP[sll][:].rearrange("p a (b c) -> p a b c", b=2)[:, mm, s, :],
                                PT[s][:], start=(mm == 0), stop=(mm == 31)),
                                waits=[("sE", exp_done[hh]), ("vE", avps_free[hh]), (f"sva{sl}", va_rd[sl])],
                                inc="tE")
                        yield
                    for hh in range(2):
                        pb = hh * 64
                        sd = P.op("scalar", lambda e, a=hh: e.copy(den_sb[a][:], ps_av[a][64:65, :]),
                                  waits=[("tE", av_done[hh])], inc="sE")
                        sd = P.op("scalar", lambda e, a=hh: e.activation(rden[a][:], den_sb[a][:], AF.Ln),
                                  waits=[("sE", sd), ("tE", rden_free[hh])], inc="sE")
                        sd = P.op("scalar", lambda e, a=hh: e.activation(rden[a][:], rden[a][:], AF.Exp, scale=-1.0),
                                  waits=[("sE", sd)], inc="sE")
                        tb = P.op("tensor", lambda e, a=hh: e.matmul(ps_rd, ones_rf[0:1, 0:64], rden[a][:], start=True, stop=True),
                                  waits=[("sE", sd), ("vE", rdb_free[0]), ("sE", exp_done[0])], inc="tE")
                        rden_free[hh] = tb
                        vv = P.op("vector", lambda e: e.tensor_copy(rdb[:], ps_rd), waits=[("tE", tb)], inc="vE")
                        rdb_free[0] = vv
                        avps_free[hh] = P.op("vector", lambda e, p=pb, a=hh, s=n % 2: e.tensor_tensor(
                            otn_st[s][p:p + 64, :], ps_av[a][0:64, :], rdb[:], OP.mult),
                            waits=[("vE", vv), (f"sotn{n % 2}", otn_free[n % 2])], inc="vE")
                    vadd = P.op("vector", lambda e, o=oc, s=n % 2: e.tensor_scalar_add(
                        otn_st[s][:], otn_st[s][:], rcol[:, o:o + 1]),
                        waits=[("vE", avps_free[0]), ("vE", avps_free[1])], inc="vE")
                    otn_free[n % 2] = P.dma("sync", lambda e, o=oc, nn=n, s=n % 2: e.dma_start(
                        out=otn_s[o, :, nn * 512:(nn + 1) * 512], in_=otn_st[s][:]),
                        waits=[("vE", vadd)], sem=f"sotn{n % 2}")
                    yield
                kq_free[sl] = max(sc[0], sc[1], av_done[0], av_done[1], rden_free[0], rden_free[1])
                if oc + 2 < 8:
                    load_oc(oc + 2)

        agen = a_gen()

        def inject(k=1):
            for _ in range(k):
                next(agen, None)

        o_done = [0, 0]
        acc_init_done = 0
        acc_cnt = [0] * 16
        ffst_free = [("sacc0", 0), ("sacc0", 0)]
        ob_ctr = [0]
        for tt in range(16):
            sl = tt % 2
            evs = 0
            for oc2 in range(2):
                ob = ob_ctr[0] % 2
                ob_ctr[0] += 1
                tvv = P.op("tensor", lambda e, t=tt, o=oc2, b=ob: e.matmul(
                    ps_o2[b][:], cT_sb[0:16, t * 128:(t + 1) * 128],
                    b2sb[0:16, o * 512:(o + 1) * 512], start=True, stop=True),
                    waits=[("sb2", b2_rd), ("vE", o_done[ob])], inc="tE")
                evs = P.op("vector", lambda e, o=oc2, s=sl, b=ob: e.tensor_copy(
                    ffst[s][:, o * 512:(o + 1) * 512], ps_o2[b][:]),
                    waits=[("tE", tvv), ffst_free[sl]], inc="vE")
                o_done[ob] = evs
            acc_cnt[tt] += 1
            v = P.dma("gpsimd", lambda e, t=tt, s=sl: e.dma_start(
                out=acc_d[t], in_=ffst[s][:]),
                waits=[("vE", evs)], sem=f"sacc{tt}")
            ffst_free[sl] = (f"sacc{tt}", v)
        acc_init_done = P.now("vE")

        w1_rd = [0, 0]
        w2_rd = [0, 0]
        w1_free = [0, 0]
        w2_free = [0, 0]
        b1_rd = [0, 0]
        b1_free = [0, 0]
        h_done = [0, 0]

        def load_w(e_, hf):
            sp = (2 * e_ + hf) % 2
            w1_rd[sp] = P.dma("gpsimd", lambda e, ee=e_, h=hf, s=sp: e.dma_start(
                out=w1b[s][:], in_=w1d[ee].rearrange("a p n -> p a n")[:, :, h * 1024:(h + 1) * 1024]),
                waits=[("tE", w1_free[sp])], sem=f"sw1{sp}")
            w2_rd[sp] = P.dma("gpsimd", lambda e, ee=e_, h=hf, s=sp: e.dma_start(
                out=w2b[s][:], in_=w2d[ee, h * 8:(h + 1) * 8].rearrange("a p n -> p a n")),
                waits=[("tE", w2_free[sp])], sem=f"sw2{sp}")
            if hf == 0:
                b1_rd[e_ % 2] = P.dma("sync", lambda e, ee=e_: e.dma_start(
                    out=b1sb[ee % 2][:], in_=b1c[ee]), waits=[("sE", b1_free[e_ % 2])], sem=f"sb1{e_ % 2}")

        load_w(0, 0)
        load_w(0, 1)

        hb_ctr = [0]
        for e_ in range(16):
            for hf in range(2):
                sp = (2 * e_ + hf) % 2
                for ic in range(8):
                    for np2 in range(2):
                        for sb2 in range(2):
                            hb = hb_ctr[0] % 2
                            hb_ctr[0] += 1
                            for dc in range(8):
                                tvv = P.op("tensor", lambda e, d=dc, i=ic, n2=np2, s2=sb2, p=sp, b=hb: e.matmul(
                                    ps_h2[b][:],
                                    w1b[p][:, d, i * 128:(i + 1) * 128],
                                    xnT_sb[:, d, n2 * 1024 + s2 * 512:n2 * 1024 + (s2 + 1) * 512],
                                    start=(d == 0), stop=(d == 7)),
                                    waits=[(f"sw1{sp}", w1_rd[sp]), (f"sb1{e_ % 2}", b1_rd[e_ % 2]),
                                           ("sE", h_done[hb]),
                                           ("vE", acc_init_done)],
                                    inc="tE" if dc == 7 else None)
                            h_done[hb] = P.op("scalar", lambda e, i=ic, n2=np2, s2=sb2, ee=e_, h=hf, b=hb: e.activation(
                                h1T[:, i, n2 * 1024 + s2 * 512:n2 * 1024 + (s2 + 1) * 512], ps_h2[b][:], AF.Gelu,
                                bias=b1sb[ee % 2][:, h * 8 + i:h * 8 + i + 1]),
                                waits=[("tE", tvv)], inc="sE")
                w1_free[sp] = P.now("tE")
                gelu_all = P.now("sE")
                if hf == 1:
                    b1_free[e_ % 2] = gelu_all
                for tt in range(16):
                    sl = tt % 2
                    evs = 0
                    for oc2 in range(2):
                        ob = ob_ctr[0] % 2
                        ob_ctr[0] += 1
                        for ic in range(8):
                            tvv = P.op("tensor", lambda e, i=ic, t=tt, o=oc2, p=sp, b=ob: e.matmul(
                                ps_o2[b][:],
                                h1T[:, i, t * 128:(t + 1) * 128],
                                w2b[p][:, i, o * 512:(o + 1) * 512],
                                start=(i == 0), stop=(i == 7)),
                                waits=[("sE", gelu_all), (f"sw2{sp}", w2_rd[sp]), ("vE", o_done[ob])],
                                inc="tE" if ic == 7 else None)
                        evs = P.op("vector", lambda e, t=tt, o=oc2, s=sl, ee=e_, b=ob: e.tensor_single_scalar(
                            ffst[s][:, o * 512:(o + 1) * 512], ps_o2[b][:],
                            c_sb[:, t * 16 + ee:t * 16 + ee + 1], OP.mult),
                            waits=[("tE", tvv), ffst_free[sl]], inc="vE")
                        o_done[ob] = evs
                        inject(1)
                    acc_cnt[tt] += 1
                    v = P.dma("gpsimd", lambda e, t=tt, s=sl: e.dma_start(
                        out=acc_d[t], in_=ffst[s][:], accum_op=OP.add),
                        waits=[("vE", evs)], sem=f"sacc{tt}")
                    ffst_free[sl] = (f"sacc{tt}", v)
                w2_free[sp] = P.now("tE")
                nxt = 2 * e_ + hf + 2
                if nxt < 32:
                    load_w(nxt // 2, nxt % 2)

        for _ in agen:
            pass

        P.barrier()
        FCTX.close()

        if "D" not in stages:
            finish()
            return nc

        DCTX = ExitStack()
        DE = DCTX.enter_context
        NSL = 3
        acc_t = [DE(nc.sbuf_tensor(f"acc_t{i}", [128, 1024], F32)) for i in range(NSL)]
        xq_t = [DE(nc.sbuf_tensor(f"xq_t{i}", [128, 1024], F32)) for i in range(NSL)]
        otn_t = [DE(nc.sbuf_tensor(f"otn_t{i}", [128, 8, 128], BF16)) for i in range(NSL)]
        outb = [DE(nc.sbuf_tensor(f"outb{i}", [128, 1024], F32)) for i in range(NSL)]
        owT_sb = DE(nc.sbuf_tensor("owT_sb", [128, 8, 1024], BF16))
        ob_bc = DE(nc.sbuf_tensor("ob_bc", [128, 1024], F32))
        obrow = DE(nc.sbuf_tensor("obrow", [1, 1024], F32))
        t_xn = DE(nc.sbuf_tensor("t_xn", [128, 512], F32))
        t_s = DE(nc.sbuf_tensor("t_s", [128, 512], F32))
        ps_d = [DE(nc.psum_tensor(f"ps_d{i}", [128, 1024], F32)) for i in range(2)]
        ps_b = DE(nc.psum_tensor("ps_b", [128, 512], F32))

        ow_rd = P.dma("gpsimd", lambda e: e.dma_start(
            out=owT_sb[:], in_=owT.rearrange("(a p) n -> p a n", p=128)), sem="sow")
        ob_rd = P.dma("sync", lambda e: e.dma_start(out=obrow[:], in_=ob_r), sem="sob")
        for half in range(2):
            tb = P.op("tensor", lambda e, hf=half: e.matmul(
                ps_b[:], ones_rf[:], obrow[:, hf * 512:(hf + 1) * 512], start=True, stop=True),
                waits=[("sob", ob_rd), ("vE", P.now("vE"))], inc="tE")
            P.op("vector", lambda e, hf=half: e.tensor_copy(ob_bc[:, hf * 512:(hf + 1) * 512], ps_b[:]),
                 waits=[("tE", tb)], inc="vE")

        otn_fin = [(s, P.now(s)) for s in ("sotn0", "sotn1")]
        acc_ld = [0] * NSL
        xq_ld = [0] * NSL
        otn_ld = [0] * NSL
        accq_free = [0] * NSL
        xq_free = [0] * NSL
        otn_mm_free = [0] * NSL
        out_free = [0] * NSL
        d_done = [0, 0]

        def load_tile(tt):
            slot = tt % NSL
            acc_ld[slot] = P.dma("sync", lambda e, t=tt, s=slot: e.dma_start(
                out=acc_t[s][:], in_=acc_d[t]),
                waits=[(f"sacc{tt}", acc_cnt[tt] * 16), ("vE", accq_free[slot])],
                sem=f"sdac{slot}")
            xq_ld[slot] = P.dma("gpsimd", lambda e, t=tt, s=slot: e.dma_start(
                out=xq_t[s][:], in_=x[t * 128:(t + 1) * 128, :]),
                waits=[("vE", xq_free[slot])], sem=f"sdxq{slot}")
            otn_ld[slot] = P.dma("sync", lambda e, t=tt, s=slot: e.dma_start(
                out=otn_t[s][:], in_=otn_s[:, :, t * 128:(t + 1) * 128].rearrange("a p t -> p a t")),
                waits=[otn_fin[0], otn_fin[1], ("tE", otn_mm_free[slot])],
                sem=f"sdot{slot}")

        for tt in range(NSL):
            load_tile(tt)

        for tt in range(16):
            slot = tt % NSL
            sil = P.op("scalar", lambda e, s=slot: e.activation(acc_t[s][:], acc_t[s][:], AF.Silu),
                       waits=[(f"sdac{slot}", acc_ld[slot])], inc="sE")
            for oc2 in range(2):
                sl = (2 * tt + oc2) % 2
                for dc in range(8):
                    tvv = P.op("tensor", lambda e, d=dc, o=oc2, s=sl, st=slot: e.matmul(
                        ps_d[s][:, o * 512:(o + 1) * 512],
                        otn_t[st][:, d, :],
                        owT_sb[:, d, o * 512:(o + 1) * 512],
                        start=(d == 0), stop=(d == 7)),
                        waits=[(f"sdot{slot}", otn_ld[slot]), ("sow", ow_rd), ("vE", d_done[sl])],
                        inc="tE" if dc == 7 else None)
                vv = P.op("vector", lambda e, t=tt, o=oc2, st=slot: e.tensor_scalar(
                    t_xn[:], xq_t[st][:, o * 512:(o + 1) * 512], negmu1[:, t:t + 1], r1sb[:, t:t + 1],
                    OP.add, OP.mult),
                    waits=[(f"sdxq{slot}", xq_ld[slot]), ("tE", tvv), ("sE", sil)], inc="vE")
                vv = P.op("vector", lambda e, o=oc2, s=sl: e.scalar_tensor_tensor(
                    t_s[:], ps_d[s][:, o * 512:(o + 1) * 512], 1.0, ob_bc[:, o * 512:(o + 1) * 512],
                    OP.mult, OP.add),
                    waits=[("vE", vv)], inc="vE")
                d_done[sl] = vv
                vv = P.op("vector", lambda e: e.tensor_add(t_s[:], t_s[:], t_xn[:]),
                          waits=[("vE", vv)], inc="vE")
                vv = P.op("vector", lambda e, o=oc2, st=slot: e.tensor_mul(
                    t_s[:], t_s[:], acc_t[st][:, o * 512:(o + 1) * 512]),
                    waits=[("vE", vv)], inc="vE")
                accq_free[slot] = vv
                vv = P.op("vector", lambda e, o=oc2, st=slot: e.tensor_add(
                    outb[st][:, o * 512:(o + 1) * 512], t_s[:], xq_t[st][:, o * 512:(o + 1) * 512]),
                    waits=[("vE", vv), (f"sdout{slot}", out_free[slot])], inc="vE")
                xq_free[slot] = vv
            otn_mm_free[slot] = P.now("tE")
            out_free[slot] = P.dma("sync", lambda e, t=tt, s=slot: e.dma_start(
                out=out[t * 128:(t + 1) * 128, :], in_=outb[s][:]),
                waits=[("vE", xq_free[slot])], sem=f"sdout{slot}")
            if tt + NSL < 16:
                load_tile(tt + NSL)

        P.barrier()

        finish()
        DCTX.close()
    return nc


def _host_inputs(kw):
    f32 = np.float32

    def C(a):
        return np.ascontiguousarray(np.asarray(a), dtype=f32)

    def col8(v):
        return C(np.asarray(v).reshape(8, 128).T)

    base = {
        "qwT": C(np.asarray(kw["L0_attn_q_w"]).T),
        "kwT": C(np.asarray(kw["L0_attn_k_w"]).T),
        "vwT": C(np.asarray(kw["L0_attn_v_w"]).T),
        "owT": C(np.asarray(kw["L0_attn_o_w"]).T),
        "g_q": col8(kw["L0_attn_ln_q_g"]), "b_qn": col8(kw["L0_attn_ln_q_b"]),
        "g_k": col8(kw["L0_attn_ln_k_g"]), "b_kn": col8(kw["L0_attn_ln_k_b"]),
        "g_v": col8(kw["L0_attn_ln_v_g"]), "b_vn": col8(kw["L0_attn_ln_v_b"]),
        "qb_c": col8(kw["L0_attn_q_b"]), "kb_c": col8(kw["L0_attn_k_b"]),
        "vb_c": col8(kw["L0_attn_v_b"]),
        "ob_r": C(np.asarray(kw["L0_attn_o_b"]).reshape(1, 1024)),
        "beta_r": C(np.asarray(kw["L0_attn_beta_param"]).reshape(1, 16)),
        "rw": C(np.asarray(kw["L0_ffn_router_w"]).reshape(8, 128, 16).transpose(1, 0, 2)),
        "rb_r": C(np.asarray(kw["L0_ffn_router_b"]).reshape(1, 16)),
        "w1": C(np.asarray(kw["L0_ffn_w1"]).reshape(16, 8, 128, 2048)),
        "b1c": C(np.asarray(kw["L0_ffn_b1"]).reshape(16, 16, 128).transpose(0, 2, 1)),
        "w2": C(np.asarray(kw["L0_ffn_w2"]).reshape(16, 16, 128, 1024)),
        "b2": C(np.asarray(kw["L0_ffn_b2"]).reshape(16, 1024)),
        "ident": np.eye(128, dtype=ml_dtypes.bfloat16),
        "ident32": np.eye(128, dtype=np.float32),
    }
    x = np.asarray(kw["L0_attn_query"], dtype=f32)
    in_maps = []
    for c in range(8):
        b, hf = c // 2, c % 2
        xb = x[b]
        xc = xb if hf == 0 else np.concatenate([xb[2048:], xb[:2048]], axis=0)
        in_maps.append(dict(base, x=np.ascontiguousarray(xc)))
    return in_maps


def kernel(**kw):
    in_maps = _host_inputs(kw)
    nc = build()
    res = run_bass_kernel_spmd(nc, in_maps, core_ids=list(range(8)))
    out = np.empty((4, 4096, 1024), dtype=np.float32)
    for c in range(8):
        b, hf = c // 2, c % 2
        out[b, hf * 2048:(hf + 1) * 2048, :] = res.results[c]["out"]
    return out


# version 24
# speedup vs baseline: 1.0081x; 1.0012x over previous
import numpy as np
import ml_dtypes
from contextlib import ExitStack
import concourse.bass as bass
import concourse.mybir as mybir
from concourse.bass_utils import run_bass_kernel_spmd
from concourse.alu_op_type import AluOpType as OP

F32 = mybir.dt.float32
BF16 = mybir.dt.bfloat16
AF = mybir.ActivationFunctionType
AX = mybir.AxisListType

ENGINES = ("sync", "gpsimd", "scalar", "vector", "tensor")


class Prog:

    def __init__(self, nc, ctx):
        self.nc = nc
        self.ctx = ctx
        self.streams = {e: [] for e in ENGINES}
        self.sems = {}
        self.cnt = {}
        self.dma_sems = set()
        for name in ("tE", "vE", "sE", "gE"):
            self._mk(name)

    def _mk(self, name):
        self.sems[name] = self.ctx.enter_context(self.nc.semaphore(name))
        self.cnt[name] = 0

    def now(self, sem):
        return self.cnt[sem]

    def op(self, eng, fn, waits=(), inc=None):
        w = tuple((s, v) for (s, v) in waits if v > 0)
        if inc is not None:
            self.cnt[inc] += 16 if inc in self.dma_sems else 1
        self.streams[eng].append((w, fn, inc))
        return self.cnt[inc] if inc else None

    def dma(self, eng, fn, waits=(), sem="cS"):
        if sem not in self.sems:
            self._mk(sem)
            self.dma_sems.add(sem)
        return self.op(eng, fn, waits, inc=sem)

    def barrier(self):
        snap = [(s, self.cnt[s]) for s in self.sems if self.cnt[s] > 0]
        for eng in ENGINES:
            self.streams[eng].append((tuple(snap), None, None))


def _emit_body(items, sems, dma_sems, e):
    last = {}
    for waits, fn, inc in items:
        for s, v in waits:
            if last.get(s, 0) < v:
                e.wait_ge(sems[s], v)
                last[s] = v
        if fn is None:
            continue
        ins = fn(e)
        if inc is not None:
            ins.then_inc(sems[inc], 16 if inc in dma_sems else 1)


def build(dbg=(), stages=("P", "A", "B", "D"), router=True, fold=True):
    nc = bass.Bass("TRN2", target_bir_lowering=False)

    x = nc.dram_tensor("x", [4096, 1024], F32, kind="ExternalInput").ap()
    qwT = nc.dram_tensor("qwT", [1024, 1024], F32, kind="ExternalInput").ap()
    kwT = nc.dram_tensor("kwT", [1024, 1024], F32, kind="ExternalInput").ap()
    vwT = nc.dram_tensor("vwT", [1024, 1024], F32, kind="ExternalInput").ap()
    owT = nc.dram_tensor("owT", [1024, 1024], F32, kind="ExternalInput").ap()
    g_q = nc.dram_tensor("g_q", [128, 8], F32, kind="ExternalInput").ap()
    b_qn = nc.dram_tensor("b_qn", [128, 8], F32, kind="ExternalInput").ap()
    g_k = nc.dram_tensor("g_k", [128, 8], F32, kind="ExternalInput").ap()
    b_kn = nc.dram_tensor("b_kn", [128, 8], F32, kind="ExternalInput").ap()
    g_v = nc.dram_tensor("g_v", [128, 8], F32, kind="ExternalInput").ap()
    b_vn = nc.dram_tensor("b_vn", [128, 8], F32, kind="ExternalInput").ap()
    qb_c = nc.dram_tensor("qb_c", [128, 8], F32, kind="ExternalInput").ap()
    kb_c = nc.dram_tensor("kb_c", [128, 8], F32, kind="ExternalInput").ap()
    vb_c = nc.dram_tensor("vb_c", [128, 8], F32, kind="ExternalInput").ap()
    ob_r = nc.dram_tensor("ob_r", [1, 1024], F32, kind="ExternalInput").ap()
    beta_r = nc.dram_tensor("beta_r", [1, 16], F32, kind="ExternalInput").ap()
    rw = nc.dram_tensor("rw", [128, 8, 16], F32, kind="ExternalInput").ap()
    rb_r = nc.dram_tensor("rb_r", [1, 16], F32, kind="ExternalInput").ap()
    w1d = nc.dram_tensor("w1", [16, 8, 128, 2048], F32, kind="ExternalInput").ap()
    b1c = nc.dram_tensor("b1c", [16, 128, 16], F32, kind="ExternalInput").ap()
    w2d = nc.dram_tensor("w2", [16, 16, 128, 1024], F32, kind="ExternalInput").ap()
    b2d = nc.dram_tensor("b2", [16, 1024], F32, kind="ExternalInput").ap()
    identd = nc.dram_tensor("ident", [128, 128], BF16, kind="ExternalInput").ap()
    ident32d = nc.dram_tensor("ident32", [128, 128], F32, kind="ExternalInput").ap()
    out = nc.dram_tensor("out", [2048, 1024], F32, kind="ExternalOutput").ap()

    def scratch(name, shape):
        kind = "ExternalOutput" if name in dbg else "Internal"
        return nc.dram_tensor(name, shape, BF16, kind=kind).ap()

    qT_s = scratch("qT_s", [8, 128, 2048])
    kT_s = scratch("kT_s", [8, 128, 4096])
    otn_s = scratch("otn_s", [8, 128, 2048])
    vAug_s = nc.dram_tensor("vAug_s", [8, 128, 32, 130], BF16,
                            kind="ExternalOutput" if "vaug" in dbg else "Internal").ap()
    acc_d = nc.dram_tensor("acc_d", [16, 128, 1024], F32,
                           kind="ExternalOutput" if "acc" in dbg else "Internal").ap()
    if "c" in dbg:
        c_dbg = nc.dram_tensor("c_dbg", [128, 256], F32, kind="ExternalOutput").ap()

    CTX = ExitStack()
    with CTX:
        P = Prog(nc, CTX)

        def finish():
            with nc.Block() as block:
                for eng in ENGINES:
                    items = P.streams[eng]
                    if items:
                        getattr(block, eng)(lambda e, it=items: _emit_body(it, P.sems, P.dma_sems, e))

        GE = CTX.enter_context
        ident_sb = GE(nc.sbuf_tensor("ident_sb", [128, 128], BF16))
        ones_cb = GE(nc.sbuf_tensor("ones_cb", [128, 1], BF16))
        ones_rf = GE(nc.sbuf_tensor("ones_rf", [1, 128], F32))
        Bb = GE(nc.sbuf_tensor("Bb", [128, 16], F32))
        BbK = GE(nc.sbuf_tensor("BbK", [128, 8], F32))
        c_sb = GE(nc.sbuf_tensor("c_sb", [128, 256], F32))
        cT_sb = GE(nc.sbuf_tensor("cT_sb", [16, 2048], BF16))
        negmu1 = GE(nc.sbuf_tensor("negmu1", [128, 16], F32))
        rcol = GE(nc.sbuf_tensor("rcol", [128, 8], F32))
        xnT_sb = GE(nc.sbuf_tensor("xnT_sb", [128, 8, 2048], BF16))
        r1sb = GE(nc.sbuf_tensor("r1sb", [128, 16], F32))

        PCTX = ExitStack()
        PE_ = PCTX.enter_context
        qwT_sb = PE_(nc.sbuf_tensor("qwT_sb", [128, 8, 1024], BF16))
        kwT_sb = PE_(nc.sbuf_tensor("kwT_sb", [128, 8, 1024], BF16))
        vwT_sb = PE_(nc.sbuf_tensor("vwT_sb", [128, 8, 1024], BF16))
        rw_sb = PE_(nc.sbuf_tensor("rw_sb", [128, 8, 16], F32))
        ident32_sb = PE_(nc.sbuf_tensor("ident32_sb", [128, 128], F32))
        xnT32 = PE_(nc.sbuf_tensor("xnT32", [128, 8, 128], F32))
        gq_sb = PE_(nc.sbuf_tensor("gq_sb", [128, 8], F32))
        bq_sb = PE_(nc.sbuf_tensor("bq_sb", [128, 8], F32))
        gk_sb = PE_(nc.sbuf_tensor("gk_sb", [128, 8], F32))
        bk_sb = PE_(nc.sbuf_tensor("bk_sb", [128, 8], F32))
        gv_sb = PE_(nc.sbuf_tensor("gv_sb", [128, 8], F32))
        bv_sb = PE_(nc.sbuf_tensor("bv_sb", [128, 8], F32))
        qbc_sb = PE_(nc.sbuf_tensor("qbc_sb", [128, 8], F32))
        kbc_sb = PE_(nc.sbuf_tensor("kbc_sb", [128, 8], F32))
        rb_bc = PE_(nc.sbuf_tensor("rb_bc", [128, 16], F32))
        brow = PE_(nc.sbuf_tensor("brow", [1, 16], F32))
        rbrow = PE_(nc.sbuf_tensor("rbrow", [1, 16], F32))
        sprow = PE_(nc.sbuf_tensor("sprow", [1, 16], F32))
        sprow2 = PE_(nc.sbuf_tensor("sprow2", [1, 16], F32))
        xin = [PE_(nc.sbuf_tensor(f"xin{i}", [128, 1024], F32)) for i in range(2)]
        xwork2 = [PE_(nc.sbuf_tensor(f"xwork{i}", [128, 1024], F32)) for i in range(4)]
        xsq2 = [PE_(nc.sbuf_tensor(f"xsq{i}", [128, 1024], F32)) for i in range(2)]
        xscr = PE_(nc.sbuf_tensor("xscr", [128, 1024], F32))
        xscr_w = [0]
        zb2 = [PE_(nc.sbuf_tensor(f"zb{i}", [128, 1024], BF16)) for i in range(4)]
        zT = [PE_(nc.sbuf_tensor(f"zT{i}", [128, 8, 512], BF16)) for i in range(2)]
        fT_st = PE_(nc.sbuf_tensor("fT_st", [128, 8, 512], BF16))
        vstg = [PE_(nc.sbuf_tensor(f"vstg{i}", [128, 1040], BF16)) for i in range(4)]
        proj = [PE_(nc.sbuf_tensor(f"proj{i}", [128, 8, 512], BF16)) for i in range(2)]
        sqb = fT_st
        rnb = PE_(nc.sbuf_tensor("rnb", [128, 512], F32))
        nrm = PE_(nc.sbuf_tensor("nrm", [1, 512], F32))
        colb = PE_(nc.sbuf_tensor("colb", [128, 20], F32))
        eps_cell = PE_(nc.sbuf_tensor("eps_cell", [128, 1], F32))
        lgt2 = [PE_(nc.sbuf_tensor(f"lgt{i}", [128, 16], F32)) for i in range(2)]
        p162 = [PE_(nc.sbuf_tensor(f"p16_{i}", [128, 16], F32)) for i in range(2)]
        cur2 = [PE_(nc.sbuf_tensor(f"cur{i}", [128, 16], F32)) for i in range(2)]
        msk2 = [PE_(nc.sbuf_tensor(f"msk{i}", [128, 16], F32)) for i in range(2)]
        cbf = PE_(nc.sbuf_tensor("cbf", [128, 256], BF16))

        ps_proj = [PE_(nc.psum_tensor(f"ps_proj{i}", [128, 512], F32)) for i in range(2)]
        ps_tr = [PE_(nc.psum_tensor(f"ps_tr{i}", [128, 128], BF16)) for i in range(2)]
        ps_tr32 = [PE_(nc.psum_tensor(f"ps_tr32{i}", [128, 128], F32)) for i in range(2)]
        ps_rb = PE_(nc.psum_tensor("ps_rb", [128, 512], F32))
        ps_ss = ps_rb[0:1, 0:512]
        ps_rt = PE_(nc.psum_tensor("ps_rt", [128, 16], F32))

        tr_evac = [0, 0]
        tr32_evac = [0, 0]
        xwork_free = [0, 0, 0, 0]
        zb_free = [0, 0, 0, 0]
        zT_used = [0, 0]
        proj_free = [0, 0]
        zT_ready = [0]
        rt_free = [0]
        nrm_free = [0]
        rnb_free = [0]
        fT_free = [0]
        proj_evac = [0, 0]
        vproj_evac = [0, 0]
        xin_ready = [0, 0]
        xin_free = [0, 0]
        vstg_free = [0, 0, 0, 0]

        P.dma("sync", lambda e: e.dma_start(out=ident_sb[:], in_=identd), sem="cS")
        P.dma("sync", lambda e: e.dma_start(out=ident32_sb[:], in_=ident32d), sem="cS")
        P.dma("sync", lambda e: e.dma_start(out=rw_sb[:], in_=rw), sem="cS")
        for dst, src in ((gq_sb, g_q), (bq_sb, b_qn), (gk_sb, g_k), (bk_sb, b_kn),
                         (gv_sb, g_v), (bv_sb, b_vn), (qbc_sb, qb_c), (kbc_sb, kb_c)):
            P.dma("sync", lambda e, d=dst, s=src: e.dma_start(out=d[:], in_=s), sem="cS")
        P.dma("sync", lambda e: e.dma_start(out=brow[:], in_=beta_r), sem="cS")
        P.dma("sync", lambda e: e.dma_start(out=rbrow[:], in_=rb_r), sem="cS")
        dvb = P.dma("sync", lambda e: e.dma_start(out=rcol[:], in_=vb_c), sem="cS")
        for dst, src in ((qwT_sb, qwT), (kwT_sb, kwT), (vwT_sb, vwT)):
            P.dma("gpsimd", lambda e, d=dst, s=src: e.dma_start(
                out=d[:], in_=s.rearrange("(a p) n -> p a n", p=128)), sem="cG")
        cS0 = P.now("cS")
        cG0 = P.now("cG")

        vv0 = P.op("vector", lambda e: e.memset(ones_cb[:], 1.0), inc="vE")
        vv0 = P.op("vector", lambda e: e.memset(eps_cell[:], 1e-5), inc="vE")
        nc.const_aps.register(F32, 1e-5, eps_cell[:])
        vv0 = P.op("vector", lambda e: e.memset(ones_rf[:], 1.0), inc="vE")
        for _vi in range(4):
            vv0 = P.op("vector", lambda e, i=_vi: e.memset(vstg[i][:].rearrange("p (a b) -> p a b", a=16)[:, :, 64:65], 1.0), inc="vE")

        t1 = P.op("tensor", lambda e: e.matmul(ps_rt[:], ones_rf[:], rbrow[:], start=True, stop=True),
                  waits=[("cS", cS0), ("vE", vv0)], inc="tE")
        vrb = P.op("vector", lambda e: e.tensor_copy(rb_bc[:], ps_rt[:]), waits=[("tE", t1)], inc="vE")

        s1 = P.op("scalar", lambda e: e.activation(sprow[:], brow[:], AF.Abs), waits=[("cS", cS0)], inc="sE")
        s1 = P.op("scalar", lambda e: e.activation(sprow[:], sprow[:], AF.Exp, scale=-1.0), waits=[("sE", s1)], inc="sE")
        v1b = P.op("vector", lambda e: e.tensor_scalar_add(sprow[:], sprow[:], 1.0), waits=[("sE", s1)], inc="vE")
        s1 = P.op("scalar", lambda e: e.activation(sprow[:], sprow[:], AF.Ln), waits=[("vE", v1b)], inc="sE")
        v2 = P.op("vector", lambda e: e.tensor_scalar_max(sprow2[:], brow[:], 0.0), waits=[("cS", cS0)], inc="vE")
        v2 = P.op("vector", lambda e: e.tensor_add(sprow[:], sprow[:], sprow2[:]), waits=[("sE", s1), ("vE", v2)], inc="vE")
        v2 = P.op("vector", lambda e: e.tensor_scalar(sprow[:], sprow[:], 0.1, 10.0, OP.max, OP.min), waits=[("vE", v2)], inc="vE")
        v2 = P.op("vector", lambda e: e.tensor_scalar_mul(sprow[:], sprow[:], 0.125), waits=[("vE", v2)], inc="vE")
        t2 = P.op("tensor", lambda e: e.matmul(ps_rt[:], ones_rf[:], sprow[:], start=True, stop=True),
                  waits=[("vE", v2)], inc="tE")
        vBb = P.op("vector", lambda e: e.tensor_copy(Bb[:], ps_rt[:]), waits=[("tE", t2)], inc="vE")
        rt_free[0] = vBb
        vBb = P.op("vector", lambda e: e.tensor_copy(BbK[0:64, :], Bb[0:64, 0:16:2]),
                   waits=[("vE", vBb)], inc="vE")
        vBb = P.op("vector", lambda e: e.tensor_copy(BbK[64:128, :], Bb[64:128, 1:16:2]),
                   waits=[("vE", vBb)], inc="vE")

        bqb = PE_(nc.sbuf_tensor("bqb", [128, 8], BF16))
        bkb = PE_(nc.sbuf_tensor("bkb", [128, 8], BF16))
        bvb = PE_(nc.sbuf_tensor("bvb", [128, 8], BF16))
        vbc = 0
        for dst, srcb in ((bqb, bq_sb), (bkb, bk_sb), (bvb, bv_sb)):
            vbc = P.op("vector", lambda e, d=dst, s=srcb: e.tensor_copy(d[:], s[:]),
                       waits=[("cS", cS0)], inc="vE")
        for pi, (wsb, bcol, biasc, cw) in enumerate((
                (qwT_sb, bqb, qbc_sb, 0), (kwT_sb, bkb, kbc_sb, 0), (vwT_sb, bvb, rcol, 8))):
            bank = pi % 2
            for oc in range(8):
                tf = P.op("tensor", lambda e, o=oc, w=wsb, b=bcol, p=bank, c=cw: e.matmul(
                    ps_proj[p][:, c + o:c + o + 1], w[:, 0, o * 128:(o + 1) * 128], b[:, 0:1],
                    start=True, stop=False),
                    waits=[("cG", cG0), ("vE", vbc)], inc=None)
                for d in range(1, 8):
                    tf = P.op("tensor", lambda e, o=oc, w=wsb, b=bcol, p=bank, c=cw, dd=d: e.matmul(
                        ps_proj[p][:, c + o:c + o + 1], w[:, dd, o * 128:(o + 1) * 128], b[:, dd:dd + 1],
                        start=False, stop=(dd == 7)),
                        inc="tE" if (d == 7) else None)
            rnb_free[0] = P.op("vector", lambda e, bi=biasc, p=bank, c=cw: e.tensor_add(
                bi[:], bi[:], ps_proj[p][:, c:c + 8]),
                waits=[("tE", tf), ("cS", max(cS0, dvb))], inc="vE")
        fold_done = P.now("tE")
        wsc = 0
        for wsb, gcol in ((qwT_sb, gq_sb), (kwT_sb, gk_sb), (vwT_sb, gv_sb)):
            for d in range(8):
                wsc = P.op("vector", lambda e, w=wsb, g=gcol, dd=d: e.tensor_single_scalar(
                    w[:, dd, :], w[:, dd, :], g[:, dd:dd + 1], OP.mult),
                    waits=[("tE", fold_done), ("cS", cS0), ("cG", cG0)], inc="vE")
        vproj_evac[0] = vproj_evac[1] = wsc


        def load_x(tt):
            slot = tt % 2
            xin_ready[slot] = P.dma("sync", lambda e, t=tt, s=slot: e.dma_start(
                out=xin[s][:], in_=x[t * 128:(t + 1) * 128, :]),
                waits=[("vE", xin_free[slot])], sem=f"xin{slot}")

        load_x(0)
        load_x(1)

        def transpose_to(dst_fn, src_fn, extra):
            for dc in range(8):
                sl = dc % 2
                tval = P.op("tensor", lambda e, d=dc, s=sl: e.transpose(
                    ps_tr[s][:], src_fn(d), ident_sb[:]),
                    waits=list(extra) + [("sE", tr_evac[sl])], inc="tE")
                tr_evac[sl] = P.op("scalar", lambda e, d=dc, s=sl: e.copy(dst_fn(d), ps_tr[s][:]),
                                   waits=[("tE", tval)], inc="sE")

        def ln_pair(tt_a, tt_b):
            sts = []
            for tt in (tt_a, tt_b):
                p = tt % 2
                cc = [colb[:, 8 * p + j:8 * p + j + 1] for j in range(8)]
                sts.append(dict(
                    tt=tt, p=p, q4=tt % 4, slot=tt % 2, xw=xwork2[tt % 4], xq=xsq2[p], zb=zb2[tt % 4], cc=cc,
                    nm1=negmu1[:, tt:tt + 1] if tt < 16 else cc[6],
                    r1c=r1sb[:, tt:tt + 1] if tt < 16 else cc[7]))
            for st in sts:
                st["v"] = P.op("vector", lambda e, s=st["slot"], a=st["cc"][0]: e.reduce_sum(a, xin[s][:], axis=AX.X),
                               waits=[(f"xin{st['slot']}", xin_ready[st["slot"]])], inc="vE")
            for st in sts:
                st["v"] = P.op("vector", lambda e, d=st["nm1"], a=st["cc"][0]: e.tensor_scalar_mul(d, a, -1.0 / 1024.0),
                               waits=[("vE", st["v"])], inc="vE")
            for st in sts:
                st["v"] = P.op("vector", lambda e, s=st["slot"], d=st["nm1"], w=st["xw"]: e.tensor_scalar_add(w[:], xin[s][:], d),
                               waits=[("vE", st["v"]), ("tE", xwork_free[st["q4"]])], inc="vE")
                xin_free[st["slot"]] = st["v"]
            for st in sts:
                if st["tt"] + 2 < 32:
                    load_x(st["tt"] + 2)
            for st in sts:
                st["s"] = P.op("scalar", lambda e, w=st["xw"], a=st["cc"][1]: e.activation(xscr[:], w[:], AF.Square, accum_out=a),
                               waits=[("vE", st["v"]), ("sE", xscr_w[0])], inc="sE")
                xscr_w[0] = st["s"]
            for st in sts:
                st["s"] = P.op("scalar", lambda e, a=st["cc"][2], b=st["cc"][1]: e.activation(a, b, AF.Ln, scale=1.0 / 1024.0, bias=1e-5),
                               waits=[("sE", st["s"])], inc="sE")
            for st in sts:
                st["s"] = P.op("scalar", lambda e, d=st["r1c"], a=st["cc"][2]: e.activation(d, a, AF.Exp, scale=-0.5),
                               waits=[("sE", st["s"])], inc="sE")
            for st in sts:
                st["v"] = P.op("vector", lambda e, d=st["r1c"], w=st["xw"]: e.tensor_single_scalar(w[:], w[:], d, OP.mult),
                               waits=[("sE", st["s"])], inc="vE")
                st["vxn"] = st["v"]
            for st in sts:
                st["v"] = P.op("vector", lambda e, w=st["xw"], a=st["cc"][3]: e.reduce_sum(a, w[:], axis=AX.X),
                               waits=[("vE", st["v"])], inc="vE")
            for st in sts:
                st["v"] = P.op("vector", lambda e, a=st["cc"][4], b=st["cc"][3]: e.tensor_scalar_mul(a, b, -1.0 / 1024.0),
                               waits=[("vE", st["v"])], inc="vE")
            for st in sts:
                st["v"] = P.op("vector", lambda e, w=st["xw"], q=st["xq"], a=st["cc"][4]: e.tensor_scalar_add(q[:], w[:], a),
                               waits=[("vE", st["v"])], inc="vE")
            for st in sts:
                st["s"] = P.op("scalar", lambda e, q=st["xq"], a=st["cc"][5]: e.activation(xscr[:], q[:], AF.Square, accum_out=a),
                               waits=[("vE", st["v"]), ("sE", xscr_w[0])], inc="sE")
                xscr_w[0] = st["s"]
            for st in sts:
                st["s"] = P.op("scalar", lambda e, a=st["cc"][5]: e.activation(a, a, AF.Ln, scale=1.0 / 1024.0, bias=1e-5),
                               waits=[("sE", st["s"])], inc="sE")
            for st in sts:
                st["s"] = P.op("scalar", lambda e, a=st["cc"][5]: e.activation(a, a, AF.Exp, scale=-0.5),
                               waits=[("sE", st["s"])], inc="sE")
            for st in sts:
                st["vz"] = P.op("vector", lambda e, z=st["zb"], q=st["xq"], a=st["cc"][5]: e.tensor_single_scalar(z[:], q[:], a, OP.mult),
                               waits=[("sE", st["s"]), ("tE", zb_free[st["q4"]])], inc="vE")
            return sts

        def trrouter(st, zsl):
            tt, p, ti = st["tt"], st["p"], st["tt"] % 4
            transpose_to(
                lambda dc, t=ti, zz=zsl: zT[zz][:, dc, t * 128:(t + 1) * 128],
                lambda dc, z=st["zb"]: z[:, dc * 128:(dc + 1) * 128],
                [("vE", st["vz"]), ("tE", zT_used[zsl])])
            zb_free[tt % 4] = P.now("tE")
            if tt < 16 and router:
                ev2 = 0
                for dc in range(8):
                    tsl = dc % 2
                    tval = P.op("tensor", lambda e, d=dc, s=tsl, w=st["xw"]: e.transpose(
                        ps_tr32[s][:], w[:, d * 128:(d + 1) * 128], ident32_sb[:]),
                        waits=[("vE", st["vxn"]), ("sE", tr32_evac[tsl])], inc="tE")
                    P.op("scalar", lambda e, d=dc, s=tsl: e.copy(xnT32[:, d, :], ps_tr32[s][:]),
                         waits=[("tE", tval)], inc="sE")
                    ev2 = P.op("scalar", lambda e, d=dc, s=tsl, t=tt: e.copy(
                        xnT_sb[:, d, t * 128:(t + 1) * 128], ps_tr32[s][:]),
                        waits=[("tE", tval)], inc="sE")
                    tr32_evac[tsl] = ev2
                xwork_free[tt % 4] = P.now("tE")
                for dc in range(8):
                    tv = P.op("tensor", lambda e, d=dc: e.matmul(
                        ps_rt[:], xnT32[:, d, :], rw_sb[:, d, :],
                        start=(d == 0), stop=(d == 7)),
                        waits=[("sE", ev2), ("cS", cS0), ("vE", rt_free[0])],
                        inc="tE" if dc == 7 else None)
                st["rv"] = P.op("vector", lambda e, g=lgt2[p]: e.tensor_add(g[:], ps_rt[:], rb_bc[:]),
                                waits=[("tE", tv), ("vE", vrb)], inc="vE")
                rt_free[0] = st["rv"]

        def softmax_pair(sts):
            for st in sts:
                p = st["p"]
                st["g"] = lgt2[p]
                st["pp"] = p162[p]
                st["cu"] = cur2[p]
                st["mk"] = msk2[p]
                st["c0"] = colb[:, 16 + 2 * p:17 + 2 * p]
                st["c1"] = colb[:, 17 + 2 * p:18 + 2 * p]
                st["v"] = st["rv"]

            def V(st, fn, waits_extra=()):
                st["v"] = P.op("vector", fn, waits=[("vE", st["v"])] + list(waits_extra), inc="vE")

            for st in sts:
                V(st, lambda e, g=st["g"], a=st["c0"]: e.reduce_max(a, g[:], axis=AX.X))
            for st in sts:
                V(st, lambda e, g=st["g"], a=st["c0"]: e.tensor_single_scalar(g[:], g[:], a, OP.subtract))
            for st in sts:
                st["s"] = P.op("scalar", lambda e, g=st["g"], q=st["pp"]: e.activation(q[:], g[:], AF.Exp),
                               waits=[("vE", st["v"])], inc="sE")
            for st in sts:
                st["v"] = P.op("vector", lambda e, q=st["pp"], a=st["c0"]: e.reduce_sum(a, q[:], axis=AX.X),
                               waits=[("sE", st["s"])], inc="vE")
            for st in sts:
                V(st, lambda e, a=st["c1"], b=st["c0"]: e.reciprocal(a, b))
            for st in sts:
                V(st, lambda e, q=st["pp"], a=st["c1"]: e.tensor_single_scalar(q[:], q[:], a, OP.mult))
            for st in sts:
                V(st, lambda e, q=st["pp"], c=st["cu"]: e.tensor_copy(c[:], q[:]))
            for _ in range(3):
                for st in sts:
                    V(st, lambda e, c=st["cu"], a=st["c0"]: e.reduce_max(a, c[:], axis=AX.X))
                for st in sts:
                    V(st, lambda e, c=st["cu"], m=st["mk"], a=st["c0"]: e.tensor_single_scalar(m[:], c[:], a, OP.is_ge))
                for st in sts:
                    V(st, lambda e, c=st["cu"], m=st["mk"]: e.scalar_tensor_tensor(c[:], m[:], -2.0, c[:], OP.mult, OP.add))
            for st in sts:
                V(st, lambda e, c=st["cu"], a=st["c0"]: e.reduce_max(a, c[:], axis=AX.X))
            for st in sts:
                V(st, lambda e, q=st["pp"], m=st["mk"], a=st["c0"]: e.tensor_single_scalar(m[:], q[:], a, OP.is_ge))
            for st in sts:
                V(st, lambda e, q=st["pp"], m=st["mk"]: e.tensor_mul(q[:], q[:], m[:]))
            for st in sts:
                V(st, lambda e, q=st["pp"], a=st["c0"]: e.reduce_sum(a, q[:], axis=AX.X))
            for st in sts:
                V(st, lambda e, a=st["c1"], b=st["c0"]: e.reciprocal(a, b))
            for st in sts:
                V(st, lambda e, q=st["pp"], a=st["c1"], t=st["tt"]: e.tensor_single_scalar(
                    c_sb[:, t * 16:(t + 1) * 16], q[:], a, OP.mult))

        def proj_mms(wsb, bias_col, zsl, psl):
            for oc in range(8):
                for dc in range(8):
                    tvv = P.op("tensor", lambda e, d=dc, o=oc, w=wsb, zz=zsl: e.matmul(
                        ps_proj[o % 2][:], w[:, d, o * 128:(o + 1) * 128], zT[zz][:, d, :],
                        start=(d == 0), stop=(d == 7)),
                        waits=[("sE", zT_ready[0]), ("vE", vproj_evac[oc % 2]), ("cG", cG0),
                               ("sE", proj_evac[oc % 2])],
                        inc="tE" if dc == 7 else None)
                proj_evac[oc % 2] = P.op("scalar", lambda e, o=oc, b=bias_col, pp=psl: e.activation(
                    proj[pp][:, o, :], ps_proj[o % 2][:], AF.Identity, bias=b[:, o:o + 1]),
                    waits=[("tE", tvv), ("vE", proj_free[psl])], inc="sE")

        def make_fin(psl, is_q, spill_dst):
            ev0, ev1 = proj_evac[0], proj_evac[1]

            def fin():
                vq = P.op("vector", lambda e, pp=psl: e.tensor_mul(sqb[:], proj[pp][:], proj[pp][:]),
                          waits=[("sE", max(ev0, ev1)), ("sfT", fT_free[0])], inc="vE")
                for oc in range(8):
                    tss = P.op("tensor", lambda e, o=oc: e.matmul(
                        ps_ss[:], ones_cb[:], sqb[:, o, :], start=(o == 0), stop=(o == 7)),
                        waits=[("vE", vq), ("sE", nrm_free[0])],
                        inc="tE" if oc == 7 else None)
                sn = P.op("scalar", lambda e: e.activation(nrm[:], ps_ss[:], AF.Ln), waits=[("tE", tss)], inc="sE")
                nrm_free[0] = sn
                sn = P.op("scalar", lambda e: e.activation(nrm[:], nrm[:], AF.Exp, scale=0.5), waits=[("sE", sn)], inc="sE")
                vn_ = P.op("vector", lambda e: e.tensor_scalar_max(nrm[:], nrm[:], 1e-6), waits=[("sE", sn)], inc="vE")
                sn = P.op("scalar", lambda e: e.activation(nrm[:], nrm[:], AF.Ln), waits=[("vE", vn_)], inc="sE")
                sn = P.op("scalar", lambda e: e.activation(nrm[:], nrm[:], AF.Exp, scale=-1.0), waits=[("sE", sn)], inc="sE")
                tb = P.op("tensor", lambda e: e.matmul(ps_rb[:], ones_rf[:], nrm[:], start=True, stop=True),
                          waits=[("sE", sn), ("vE", rnb_free[0])], inc="tE")
                vr = P.op("vector", lambda e: e.tensor_copy(rnb[:], ps_rb[:]), waits=[("tE", tb)], inc="vE")
                rnb_free[0] = vr
                if is_q:
                    lastv = P.op("vector", lambda e, pp=psl: e.tensor_tensor(
                        fT_st[:], proj[pp][:],
                        rnb[:].unsqueeze(1).broadcast_to((128, 8, 512)), OP.mult),
                        waits=[("vE", vr), ("sfT", fT_free[0])], inc="vE")
                else:
                    lastv = 0
                    for oc in range(8):
                        lastv = P.op("vector", lambda e, o=oc, pp=psl: e.scalar_tensor_tensor(
                            fT_st[:, o, :], proj[pp][:, o, :], BbK[:, o:o + 1],
                            rnb[:], OP.mult, OP.mult),
                            waits=[("vE", vr), ("sfT", fT_free[0])], inc="vE")
                proj_free[psl] = lastv
                fT_free[0] = P.dma("sync", lambda e, d=spill_dst: e.dma_start(
                    out=d, in_=fT_st[:]), waits=[("vE", lastv)], sem="sfT")
            return fin

        p_ctr = [0]
        pending_fin = [None]

        def run_pending():
            if pending_fin[0] is not None:
                pending_fin[0]()
                pending_fin[0] = None

        pend_ln = [None]
        for gi in range(8):
            t0 = gi * 4
            zsl = gi % 2
            if pend_ln[0] is None:
                stsA = ln_pair(t0, t0 + 1)
                trrouter(stsA[0], zsl)
                trrouter(stsA[1], zsl)
                stsB = ln_pair(t0 + 2, t0 + 3)
            else:
                stsA, stsB = pend_ln[0]
                trrouter(stsA[0], zsl)
                trrouter(stsA[1], zsl)
            if t0 < 16 and router:
                softmax_pair(stsA)
            trrouter(stsB[0], zsl)
            trrouter(stsB[1], zsl)
            if t0 < 16 and router:
                softmax_pair(stsB)
            zT_ready[0] = P.now("sE")
            nxt = [None, None]

            if gi < 4:
                psl = p_ctr[0] % 2
                p_ctr[0] += 1
                proj_mms(qwT_sb, qbc_sb, zsl, psl)
                run_pending()
                pending_fin[0] = make_fin(
                    psl, True,
                    qT_s[:, :, gi * 512:(gi + 1) * 512].rearrange("a p t -> p a t"))
                if gi < 7:
                    nxt[0] = ln_pair(t0 + 4, t0 + 5)
            psl = p_ctr[0] % 2
            p_ctr[0] += 1
            proj_mms(kwT_sb, kbc_sb, zsl, psl)
            run_pending()
            pending_fin[0] = make_fin(
                psl, False,
                kT_s[:, :, gi * 512:(gi + 1) * 512].rearrange("a p t -> p a t"))
            if gi < 7 and nxt[0] is None:
                nxt[0] = ln_pair(t0 + 4, t0 + 5)
            elif gi < 7:
                nxt[1] = ln_pair(t0 + 6, t0 + 7)
            for ti in range(4):
                tt = gi * 4 + ti
                st2 = ti
                ev_last = 0
                for oc2 in range(2):
                    sl = (ti * 2 + oc2) % 2
                    for dc in range(8):
                        tvv = P.op("tensor", lambda e, d=dc, t=ti, o=oc2, s=sl, zz=zsl: e.matmul(
                            ps_proj[s][:], zT[zz][:, d, t * 128:(t + 1) * 128],
                            vwT_sb[:, d, o * 512:(o + 1) * 512],
                            start=(d == 0), stop=(d == 7)),
                            waits=[("sE", zT_ready[0]), ("vE", vproj_evac[sl]), ("sE", proj_evac[sl])],
                            inc="tE" if dc == 7 else None)
                    vproj_evac[sl] = P.op("vector", lambda e, o=oc2, s=sl, g=st2: e.tensor_copy(
                        vstg[g][:].rearrange("p (a b) -> p a b", a=16)[:, o * 8:(o + 1) * 8, 0:64],
                        ps_proj[s][:].rearrange("p (a t) -> p a t", a=8)),
                        waits=[("tE", tvv), (f"svau{st2}", vstg_free[st2])], inc="vE")
                    ev_last = vproj_evac[sl]
                vstg_free[st2] = P.dma("sync", lambda e, t=tt, g=st2: e.dma_start(
                    out=vAug_s[:, :, t, :].rearrange("a p b -> p a b"),
                    in_=vstg[g][:].rearrange("p (a b) -> p a b", a=8)),
                    waits=[("vE", ev_last)], sem=f"svau{st2}")
            zT_used[zsl] = P.now("tE")
            run_pending()
            if gi < 7 and nxt[1] is None:
                nxt[1] = ln_pair(t0 + 6, t0 + 7)
            pend_ln[0] = (nxt[0], nxt[1]) if gi < 7 else None

        vcb = P.op("vector", lambda e: e.tensor_copy(cbf[:], c_sb[:]), waits=[("vE", P.now("vE"))], inc="vE")
        for tt in range(16):
            sl = tt % 2
            tvv = P.op("tensor", lambda e, t=tt, s=sl: e.transpose(
                ps_tr[s][0:16, :], cbf[:, t * 16:(t + 1) * 16], ident_sb[:]),
                waits=[("vE", vcb), ("sE", tr_evac[sl])], inc="tE")
            tr_evac[sl] = P.op("scalar", lambda e, t=tt, s=sl: e.copy(
                cT_sb[0:16, t * 128:(t + 1) * 128], ps_tr[s][0:16, :]),
                waits=[("tE", tvv)], inc="sE")
        if "c" in dbg:
            P.dma("sync", lambda e: e.dma_start(out=c_dbg, in_=c_sb[:]), waits=[("vE", P.now("vE"))], sem="dbg")

        P.barrier()
        PCTX.close()

        if "A" not in stages:
            finish()
            return nc

        FCTX = ExitStack()
        FE = FCTX.enter_context
        h1T = FE(nc.sbuf_tensor("h1T", [128, 8, 2048], BF16))
        w1b = [FE(nc.sbuf_tensor(f"w1b{i}", [128, 8, 1024], BF16)) for i in range(2)]
        w2b = [FE(nc.sbuf_tensor(f"w2b{i}", [128, 8, 1024], BF16)) for i in range(2)]
        b1sb = [FE(nc.sbuf_tensor(f"b1sb{i}", [128, 16], F32)) for i in range(2)]
        b2sb = FE(nc.sbuf_tensor("b2sb", [16, 1024], BF16))
        ffst = [FE(nc.sbuf_tensor(f"ffst{i}", [128, 1024], F32)) for i in range(2)]
        qToc = [FE(nc.sbuf_tensor(f"qToc{i}", [128, 2048], BF16)) for i in range(2)]
        kToc = [FE(nc.sbuf_tensor(f"kToc{i}", [128, 4096], BF16)) for i in range(2)]
        vAugP = [FE(nc.sbuf_tensor(f"vAugP{i}", [128, 32, 130], BF16)) for i in range(2)]
        PT = [FE(nc.sbuf_tensor(f"PT{i}", [128, 512], BF16)) for i in range(2)]
        den_sb = [FE(nc.sbuf_tensor(f"den_sb{i}", [1, 512], F32)) for i in range(2)]
        rden = [FE(nc.sbuf_tensor(f"rden{i}", [1, 512], F32)) for i in range(2)]
        rdb = FE(nc.sbuf_tensor("rdb", [64, 512], F32))
        otn_st = [FE(nc.sbuf_tensor(f"otn_st{i}", [128, 512], BF16)) for i in range(2)]
        ps_h2 = [FE(nc.psum_tensor(f"ps_h{i}", [128, 512], F32)) for i in range(2)]
        ps_o2 = [FE(nc.psum_tensor(f"ps_o{i}", [128, 512], F32)) for i in range(2)]
        ps_s2 = [FE(nc.psum_tensor(f"ps_s{i}", [128, 512], F32)) for i in range(2)]
        ps_av = [FE(nc.psum_tensor(f"ps_av{i}", [128, 512], F32)) for i in range(2)]
        ps_rd = ps_s2[0][0:64, 0:512]

        b2_rd = P.dma("gpsimd", lambda e: e.dma_start(out=b2sb[:], in_=b2d), sem="sb2")

        kq_free = [0, 0]
        kT_rd = [0, 0]
        qT_rd = [0, 0]
        va_rd = [0, 0]
        exp_done = [0, 0]
        av_done = [0, 0]
        otn_free = [0, 0]
        avps_free = [0, 0]
        rdb_free = [0]
        rden_free = [0, 0]

        def load_oc(o):
            sl = o % 2
            kT_rd[sl] = P.dma("sync", lambda e, oo=o, s=sl: e.dma_start(out=kToc[s][:], in_=kT_s[oo]),
                              waits=[("tE", kq_free[sl])], sem=f"skT{sl}")
            qT_rd[sl] = P.dma("sync", lambda e, oo=o, s=sl: e.dma_start(out=qToc[s][:], in_=qT_s[oo]),
                              waits=[("tE", kq_free[sl])], sem=f"sqT{sl}")
            va_rd[sl] = P.dma("sync", lambda e, oo=o, s=sl: e.dma_start(
                out=vAugP[s][:], in_=vAug_s[oo]),
                waits=[("tE", kq_free[sl])], sem=f"sva{sl}")

        def a_gen():
            load_oc(0)
            load_oc(1)
            for oc in range(8):
                sl = oc % 2

                def scores(hh, m, n, s_=sl):
                    pb = hh * 64
                    tp = (64, 0) if hh else (0, 0)
                    return P.op("tensor", lambda e, s=hh, mm=m, p=pb, t=tp, nn=n, sll=s_: e.matmul(
                        ps_s2[s][:], kToc[sll][p:p + 64, mm * 128:(mm + 1) * 128],
                        qToc[sll][p:p + 64, nn * 512:(nn + 1) * 512],
                        start=True, stop=True, tile_position=t),
                        waits=[(f"skT{s_}", kT_rd[s_]), (f"sqT{s_}", qT_rd[s_]),
                               ("sE", exp_done[hh])] +
                              ([("vE", rdb_free[0])] if hh == 0 else []),
                        inc="tE")

                for n in range(4):
                    sc = [scores(0, 0, n), scores(1, 0, n)]
                    for m in range(32):
                        for hh in range(2):
                            exp_done[hh] = P.op("scalar", lambda e, s=hh: e.activation(
                                PT[s][:], ps_s2[s][:], AF.Exp),
                                waits=[("tE", sc[hh]), ("tE", av_done[hh])], inc="sE")
                        if m + 1 < 32:
                            sc = [scores(0, m + 1, n), scores(1, m + 1, n)]
                        for hh in range(2):
                            av_done[hh] = P.op("tensor", lambda e, s=hh, mm=m, sll=sl: e.matmul(
                                ps_av[s][0:65, :],
                                vAugP[sll][:].rearrange("p a (b c) -> p a b c", b=2)[:, mm, s, :],
                                PT[s][:], start=(mm == 0), stop=(mm == 31)),
                                waits=[("sE", exp_done[hh]), ("vE", avps_free[hh]), (f"sva{sl}", va_rd[sl])],
                                inc="tE")
                        yield
                    for hh in range(2):
                        pb = hh * 64
                        sd = P.op("scalar", lambda e, a=hh: e.copy(den_sb[a][:], ps_av[a][64:65, :]),
                                  waits=[("tE", av_done[hh])], inc="sE")
                        sd = P.op("scalar", lambda e, a=hh: e.activation(rden[a][:], den_sb[a][:], AF.Ln),
                                  waits=[("sE", sd), ("tE", rden_free[hh])], inc="sE")
                        sd = P.op("scalar", lambda e, a=hh: e.activation(rden[a][:], rden[a][:], AF.Exp, scale=-1.0),
                                  waits=[("sE", sd)], inc="sE")
                        tb = P.op("tensor", lambda e, a=hh: e.matmul(ps_rd, ones_rf[0:1, 0:64], rden[a][:], start=True, stop=True),
                                  waits=[("sE", sd), ("vE", rdb_free[0]), ("sE", exp_done[0])], inc="tE")
                        rden_free[hh] = tb
                        vv = P.op("vector", lambda e: e.tensor_copy(rdb[:], ps_rd), waits=[("tE", tb)], inc="vE")
                        rdb_free[0] = vv
                        avps_free[hh] = P.op("vector", lambda e, p=pb, a=hh, s=n % 2: e.tensor_tensor(
                            otn_st[s][p:p + 64, :], ps_av[a][0:64, :], rdb[:], OP.mult),
                            waits=[("vE", vv), (f"sotn{n % 2}", otn_free[n % 2])], inc="vE")
                    vadd = P.op("vector", lambda e, o=oc, s=n % 2: e.tensor_scalar_add(
                        otn_st[s][:], otn_st[s][:], rcol[:, o:o + 1]),
                        waits=[("vE", avps_free[0]), ("vE", avps_free[1])], inc="vE")
                    otn_free[n % 2] = P.dma("sync", lambda e, o=oc, nn=n, s=n % 2: e.dma_start(
                        out=otn_s[o, :, nn * 512:(nn + 1) * 512], in_=otn_st[s][:]),
                        waits=[("vE", vadd)], sem=f"sotn{n % 2}")
                    yield
                kq_free[sl] = max(sc[0], sc[1], av_done[0], av_done[1], rden_free[0], rden_free[1])
                if oc + 2 < 8:
                    load_oc(oc + 2)

        agen = a_gen()

        def inject(k=1):
            for _ in range(k):
                next(agen, None)

        o_done = [0, 0]
        acc_init_done = 0
        acc_cnt = [0] * 16
        ffst_free = [("sacc0", 0), ("sacc0", 0)]
        ob_ctr = [0]
        for tt in range(16):
            sl = tt % 2
            evs = 0
            for oc2 in range(2):
                ob = ob_ctr[0] % 2
                ob_ctr[0] += 1
                tvv = P.op("tensor", lambda e, t=tt, o=oc2, b=ob: e.matmul(
                    ps_o2[b][:], cT_sb[0:16, t * 128:(t + 1) * 128],
                    b2sb[0:16, o * 512:(o + 1) * 512], start=True, stop=True),
                    waits=[("sb2", b2_rd), ("vE", o_done[ob])], inc="tE")
                evs = P.op("vector", lambda e, o=oc2, s=sl, b=ob: e.tensor_copy(
                    ffst[s][:, o * 512:(o + 1) * 512], ps_o2[b][:]),
                    waits=[("tE", tvv), ffst_free[sl]], inc="vE")
                o_done[ob] = evs
            acc_cnt[tt] += 1
            v = P.dma("gpsimd", lambda e, t=tt, s=sl: e.dma_start(
                out=acc_d[t], in_=ffst[s][:]),
                waits=[("vE", evs)], sem=f"sacc{tt}")
            ffst_free[sl] = (f"sacc{tt}", v)
        acc_init_done = P.now("vE")

        w1_rd = [0, 0]
        w2_rd = [0, 0]
        w1_free = [0, 0]
        w2_free = [0, 0]
        b1_rd = [0, 0]
        b1_free = [0, 0]
        h_done = [0, 0]

        def load_w(e_, hf):
            sp = (2 * e_ + hf) % 2
            w1_rd[sp] = P.dma("gpsimd", lambda e, ee=e_, h=hf, s=sp: e.dma_start(
                out=w1b[s][:], in_=w1d[ee].rearrange("a p n -> p a n")[:, :, h * 1024:(h + 1) * 1024]),
                waits=[("tE", w1_free[sp])], sem=f"sw1{sp}")
            w2_rd[sp] = P.dma("gpsimd", lambda e, ee=e_, h=hf, s=sp: e.dma_start(
                out=w2b[s][:], in_=w2d[ee, h * 8:(h + 1) * 8].rearrange("a p n -> p a n")),
                waits=[("tE", w2_free[sp])], sem=f"sw2{sp}")
            if hf == 0:
                b1_rd[e_ % 2] = P.dma("sync", lambda e, ee=e_: e.dma_start(
                    out=b1sb[ee % 2][:], in_=b1c[ee]), waits=[("sE", b1_free[e_ % 2])], sem=f"sb1{e_ % 2}")

        load_w(0, 0)
        load_w(0, 1)

        hb_ctr = [0]
        for e_ in range(16):
            for hf in range(2):
                sp = (2 * e_ + hf) % 2
                for ic in range(8):
                    for np2 in range(2):
                        for sb2 in range(2):
                            hb = hb_ctr[0] % 2
                            hb_ctr[0] += 1
                            for dc in range(8):
                                tvv = P.op("tensor", lambda e, d=dc, i=ic, n2=np2, s2=sb2, p=sp, b=hb: e.matmul(
                                    ps_h2[b][:],
                                    w1b[p][:, d, i * 128:(i + 1) * 128],
                                    xnT_sb[:, d, n2 * 1024 + s2 * 512:n2 * 1024 + (s2 + 1) * 512],
                                    start=(d == 0), stop=(d == 7)),
                                    waits=[(f"sw1{sp}", w1_rd[sp]), (f"sb1{e_ % 2}", b1_rd[e_ % 2]),
                                           ("sE", h_done[hb]),
                                           ("vE", acc_init_done)],
                                    inc="tE" if dc == 7 else None)
                            h_done[hb] = P.op("scalar", lambda e, i=ic, n2=np2, s2=sb2, ee=e_, h=hf, b=hb: e.activation(
                                h1T[:, i, n2 * 1024 + s2 * 512:n2 * 1024 + (s2 + 1) * 512], ps_h2[b][:], AF.Gelu,
                                bias=b1sb[ee % 2][:, h * 8 + i:h * 8 + i + 1]),
                                waits=[("tE", tvv)], inc="sE")
                w1_free[sp] = P.now("tE")
                gelu_all = P.now("sE")
                if hf == 1:
                    b1_free[e_ % 2] = gelu_all
                for tt in range(16):
                    sl = tt % 2
                    evs = 0
                    for oc2 in range(2):
                        ob = ob_ctr[0] % 2
                        ob_ctr[0] += 1
                        for ic in range(8):
                            tvv = P.op("tensor", lambda e, i=ic, t=tt, o=oc2, p=sp, b=ob: e.matmul(
                                ps_o2[b][:],
                                h1T[:, i, t * 128:(t + 1) * 128],
                                w2b[p][:, i, o * 512:(o + 1) * 512],
                                start=(i == 0), stop=(i == 7)),
                                waits=[("sE", gelu_all), (f"sw2{sp}", w2_rd[sp]), ("vE", o_done[ob])],
                                inc="tE" if ic == 7 else None)
                        evs = P.op("vector", lambda e, t=tt, o=oc2, s=sl, ee=e_, b=ob: e.tensor_single_scalar(
                            ffst[s][:, o * 512:(o + 1) * 512], ps_o2[b][:],
                            c_sb[:, t * 16 + ee:t * 16 + ee + 1], OP.mult),
                            waits=[("tE", tvv), ffst_free[sl]], inc="vE")
                        o_done[ob] = evs
                        inject(1)
                    acc_cnt[tt] += 1
                    v = P.dma("gpsimd", lambda e, t=tt, s=sl: e.dma_start(
                        out=acc_d[t], in_=ffst[s][:], accum_op=OP.add),
                        waits=[("vE", evs)], sem=f"sacc{tt}")
                    ffst_free[sl] = (f"sacc{tt}", v)
                w2_free[sp] = P.now("tE")
                nxt = 2 * e_ + hf + 2
                if nxt < 32:
                    load_w(nxt // 2, nxt % 2)

        for _ in agen:
            pass

        P.barrier()
        FCTX.close()

        if "D" not in stages:
            finish()
            return nc

        DCTX = ExitStack()
        DE = DCTX.enter_context
        NSL = 3
        acc_t = [DE(nc.sbuf_tensor(f"acc_t{i}", [128, 1024], F32)) for i in range(NSL)]
        xq_t = [DE(nc.sbuf_tensor(f"xq_t{i}", [128, 1024], F32)) for i in range(NSL)]
        otn_t = [DE(nc.sbuf_tensor(f"otn_t{i}", [128, 8, 128], BF16)) for i in range(NSL)]
        outb = [DE(nc.sbuf_tensor(f"outb{i}", [128, 1024], F32)) for i in range(NSL)]
        owT_sb = DE(nc.sbuf_tensor("owT_sb", [128, 8, 1024], BF16))
        ob_bc = DE(nc.sbuf_tensor("ob_bc", [128, 1024], F32))
        obrow = DE(nc.sbuf_tensor("obrow", [1, 1024], F32))
        t_xn = DE(nc.sbuf_tensor("t_xn", [128, 512], F32))
        t_s = DE(nc.sbuf_tensor("t_s", [128, 512], F32))
        ps_d = [DE(nc.psum_tensor(f"ps_d{i}", [128, 1024], F32)) for i in range(2)]
        ps_b = DE(nc.psum_tensor("ps_b", [128, 512], F32))

        ow_rd = P.dma("gpsimd", lambda e: e.dma_start(
            out=owT_sb[:], in_=owT.rearrange("(a p) n -> p a n", p=128)), sem="sow")
        ob_rd = P.dma("sync", lambda e: e.dma_start(out=obrow[:], in_=ob_r), sem="sob")
        for half in range(2):
            tb = P.op("tensor", lambda e, hf=half: e.matmul(
                ps_b[:], ones_rf[:], obrow[:, hf * 512:(hf + 1) * 512], start=True, stop=True),
                waits=[("sob", ob_rd), ("vE", P.now("vE"))], inc="tE")
            P.op("vector", lambda e, hf=half: e.tensor_copy(ob_bc[:, hf * 512:(hf + 1) * 512], ps_b[:]),
                 waits=[("tE", tb)], inc="vE")

        otn_fin = [(s, P.now(s)) for s in ("sotn0", "sotn1")]
        acc_ld = [0] * NSL
        xq_ld = [0] * NSL
        otn_ld = [0] * NSL
        accq_free = [0] * NSL
        xq_free = [0] * NSL
        otn_mm_free = [0] * NSL
        out_free = [0] * NSL
        d_done = [0, 0]

        def load_tile(tt):
            slot = tt % NSL
            acc_ld[slot] = P.dma("sync", lambda e, t=tt, s=slot: e.dma_start(
                out=acc_t[s][:], in_=acc_d[t]),
                waits=[(f"sacc{tt}", acc_cnt[tt] * 16), ("vE", accq_free[slot])],
                sem=f"sdac{slot}")
            xq_ld[slot] = P.dma("gpsimd", lambda e, t=tt, s=slot: e.dma_start(
                out=xq_t[s][:], in_=x[t * 128:(t + 1) * 128, :]),
                waits=[("vE", xq_free[slot])], sem=f"sdxq{slot}")
            otn_ld[slot] = P.dma("sync", lambda e, t=tt, s=slot: e.dma_start(
                out=otn_t[s][:], in_=otn_s[:, :, t * 128:(t + 1) * 128].rearrange("a p t -> p a t")),
                waits=[otn_fin[0], otn_fin[1], ("tE", otn_mm_free[slot])],
                sem=f"sdot{slot}")

        for tt in range(NSL):
            load_tile(tt)

        for tt in range(16):
            slot = tt % NSL
            sil = P.op("scalar", lambda e, s=slot: e.activation(acc_t[s][:], acc_t[s][:], AF.Silu),
                       waits=[(f"sdac{slot}", acc_ld[slot])], inc="sE")
            for oc2 in range(2):
                sl = (2 * tt + oc2) % 2
                for dc in range(8):
                    tvv = P.op("tensor", lambda e, d=dc, o=oc2, s=sl, st=slot: e.matmul(
                        ps_d[s][:, o * 512:(o + 1) * 512],
                        otn_t[st][:, d, :],
                        owT_sb[:, d, o * 512:(o + 1) * 512],
                        start=(d == 0), stop=(d == 7)),
                        waits=[(f"sdot{slot}", otn_ld[slot]), ("sow", ow_rd), ("vE", d_done[sl])],
                        inc="tE" if dc == 7 else None)
                vv = P.op("vector", lambda e, t=tt, o=oc2, st=slot: e.tensor_scalar(
                    t_xn[:], xq_t[st][:, o * 512:(o + 1) * 512], negmu1[:, t:t + 1], r1sb[:, t:t + 1],
                    OP.add, OP.mult),
                    waits=[(f"sdxq{slot}", xq_ld[slot]), ("tE", tvv), ("sE", sil)], inc="vE")
                vv = P.op("vector", lambda e, o=oc2, s=sl: e.scalar_tensor_tensor(
                    t_s[:], ps_d[s][:, o * 512:(o + 1) * 512], 1.0, ob_bc[:, o * 512:(o + 1) * 512],
                    OP.mult, OP.add),
                    waits=[("vE", vv)], inc="vE")
                d_done[sl] = vv
                vv = P.op("vector", lambda e: e.tensor_add(t_s[:], t_s[:], t_xn[:]),
                          waits=[("vE", vv)], inc="vE")
                vv = P.op("vector", lambda e, o=oc2, st=slot: e.tensor_mul(
                    t_s[:], t_s[:], acc_t[st][:, o * 512:(o + 1) * 512]),
                    waits=[("vE", vv)], inc="vE")
                accq_free[slot] = vv
                vv = P.op("vector", lambda e, o=oc2, st=slot: e.tensor_add(
                    outb[st][:, o * 512:(o + 1) * 512], t_s[:], xq_t[st][:, o * 512:(o + 1) * 512]),
                    waits=[("vE", vv), (f"sdout{slot}", out_free[slot])], inc="vE")
                xq_free[slot] = vv
            otn_mm_free[slot] = P.now("tE")
            out_free[slot] = P.dma("sync", lambda e, t=tt, s=slot: e.dma_start(
                out=out[t * 128:(t + 1) * 128, :], in_=outb[s][:]),
                waits=[("vE", xq_free[slot])], sem=f"sdout{slot}")
            if tt + NSL < 16:
                load_tile(tt + NSL)

        P.barrier()

        finish()
        DCTX.close()
    return nc


def _host_inputs(kw):
    f32 = np.float32

    def C(a):
        return np.ascontiguousarray(np.asarray(a), dtype=f32)

    def col8(v):
        return C(np.asarray(v).reshape(8, 128).T)

    base = {
        "qwT": C(np.asarray(kw["L0_attn_q_w"]).T),
        "kwT": C(np.asarray(kw["L0_attn_k_w"]).T),
        "vwT": C(np.asarray(kw["L0_attn_v_w"]).T),
        "owT": C(np.asarray(kw["L0_attn_o_w"]).T),
        "g_q": col8(kw["L0_attn_ln_q_g"]), "b_qn": col8(kw["L0_attn_ln_q_b"]),
        "g_k": col8(kw["L0_attn_ln_k_g"]), "b_kn": col8(kw["L0_attn_ln_k_b"]),
        "g_v": col8(kw["L0_attn_ln_v_g"]), "b_vn": col8(kw["L0_attn_ln_v_b"]),
        "qb_c": col8(kw["L0_attn_q_b"]), "kb_c": col8(kw["L0_attn_k_b"]),
        "vb_c": col8(kw["L0_attn_v_b"]),
        "ob_r": C(np.asarray(kw["L0_attn_o_b"]).reshape(1, 1024)),
        "beta_r": C(np.asarray(kw["L0_attn_beta_param"]).reshape(1, 16)),
        "rw": C(np.asarray(kw["L0_ffn_router_w"]).reshape(8, 128, 16).transpose(1, 0, 2)),
        "rb_r": C(np.asarray(kw["L0_ffn_router_b"]).reshape(1, 16)),
        "w1": C(np.asarray(kw["L0_ffn_w1"]).reshape(16, 8, 128, 2048)),
        "b1c": C(np.asarray(kw["L0_ffn_b1"]).reshape(16, 16, 128).transpose(0, 2, 1)),
        "w2": C(np.asarray(kw["L0_ffn_w2"]).reshape(16, 16, 128, 1024)),
        "b2": C(np.asarray(kw["L0_ffn_b2"]).reshape(16, 1024)),
        "ident": np.eye(128, dtype=ml_dtypes.bfloat16),
        "ident32": np.eye(128, dtype=np.float32),
    }
    x = np.asarray(kw["L0_attn_query"], dtype=f32)
    in_maps = []
    for c in range(8):
        b, hf = c // 2, c % 2
        xb = x[b]
        xc = xb if hf == 0 else np.concatenate([xb[2048:], xb[:2048]], axis=0)
        in_maps.append(dict(base, x=np.ascontiguousarray(xc)))
    return in_maps


def kernel(**kw):
    in_maps = _host_inputs(kw)
    nc = build()
    res = run_bass_kernel_spmd(nc, in_maps, core_ids=list(range(8)))
    out = np.empty((4, 4096, 1024), dtype=np.float32)
    for c in range(8):
        b, hf = c // 2, c % 2
        out[b, hf * 2048:(hf + 1) * 2048, :] = res.results[c]["out"]
    return out


# version 25 (final)
# speedup vs baseline: 1.0094x; 1.0013x over previous
import numpy as np
import ml_dtypes
from contextlib import ExitStack
import concourse.bass as bass
import concourse.mybir as mybir
from concourse.bass_utils import run_bass_kernel_spmd
from concourse.alu_op_type import AluOpType as OP

F32 = mybir.dt.float32
BF16 = mybir.dt.bfloat16
AF = mybir.ActivationFunctionType
AX = mybir.AxisListType

ENGINES = ("sync", "gpsimd", "scalar", "vector", "tensor")


class Prog:

    def __init__(self, nc, ctx):
        self.nc = nc
        self.ctx = ctx
        self.streams = {e: [] for e in ENGINES}
        self.sems = {}
        self.cnt = {}
        self.dma_sems = set()
        for name in ("tE", "vE", "sE", "gE"):
            self._mk(name)

    def _mk(self, name):
        self.sems[name] = self.ctx.enter_context(self.nc.semaphore(name))
        self.cnt[name] = 0

    def now(self, sem):
        return self.cnt[sem]

    def op(self, eng, fn, waits=(), inc=None):
        w = tuple((s, v) for (s, v) in waits if v > 0)
        if inc is not None:
            self.cnt[inc] += 16 if inc in self.dma_sems else 1
        self.streams[eng].append((w, fn, inc))
        return self.cnt[inc] if inc else None

    def dma(self, eng, fn, waits=(), sem="cS"):
        if sem not in self.sems:
            self._mk(sem)
            self.dma_sems.add(sem)
        return self.op(eng, fn, waits, inc=sem)

    def barrier(self):
        snap = [(s, self.cnt[s]) for s in self.sems if self.cnt[s] > 0]
        for eng in ENGINES:
            self.streams[eng].append((tuple(snap), None, None))


def _emit_body(items, sems, dma_sems, e):
    last = {}
    for waits, fn, inc in items:
        for s, v in waits:
            if last.get(s, 0) < v:
                e.wait_ge(sems[s], v)
                last[s] = v
        if fn is None:
            continue
        ins = fn(e)
        if inc is not None:
            ins.then_inc(sems[inc], 16 if inc in dma_sems else 1)


def build(dbg=(), stages=("P", "A", "B", "D"), router=True, fold=True):
    nc = bass.Bass("TRN2", target_bir_lowering=False)

    x = nc.dram_tensor("x", [4096, 1024], F32, kind="ExternalInput").ap()
    qwT = nc.dram_tensor("qwT", [1024, 1024], F32, kind="ExternalInput").ap()
    kwT = nc.dram_tensor("kwT", [1024, 1024], F32, kind="ExternalInput").ap()
    vwT = nc.dram_tensor("vwT", [1024, 1024], F32, kind="ExternalInput").ap()
    owT = nc.dram_tensor("owT", [1024, 1024], F32, kind="ExternalInput").ap()
    g_q = nc.dram_tensor("g_q", [128, 8], F32, kind="ExternalInput").ap()
    b_qn = nc.dram_tensor("b_qn", [128, 8], F32, kind="ExternalInput").ap()
    g_k = nc.dram_tensor("g_k", [128, 8], F32, kind="ExternalInput").ap()
    b_kn = nc.dram_tensor("b_kn", [128, 8], F32, kind="ExternalInput").ap()
    g_v = nc.dram_tensor("g_v", [128, 8], F32, kind="ExternalInput").ap()
    b_vn = nc.dram_tensor("b_vn", [128, 8], F32, kind="ExternalInput").ap()
    qb_c = nc.dram_tensor("qb_c", [128, 8], F32, kind="ExternalInput").ap()
    kb_c = nc.dram_tensor("kb_c", [128, 8], F32, kind="ExternalInput").ap()
    vb_c = nc.dram_tensor("vb_c", [128, 8], F32, kind="ExternalInput").ap()
    ob_r = nc.dram_tensor("ob_r", [1, 1024], F32, kind="ExternalInput").ap()
    beta_r = nc.dram_tensor("beta_r", [1, 16], F32, kind="ExternalInput").ap()
    rw = nc.dram_tensor("rw", [128, 8, 16], F32, kind="ExternalInput").ap()
    rb_r = nc.dram_tensor("rb_r", [1, 16], F32, kind="ExternalInput").ap()
    w1d = nc.dram_tensor("w1", [16, 8, 128, 2048], F32, kind="ExternalInput").ap()
    b1c = nc.dram_tensor("b1c", [16, 128, 16], F32, kind="ExternalInput").ap()
    w2d = nc.dram_tensor("w2", [16, 16, 128, 1024], F32, kind="ExternalInput").ap()
    b2d = nc.dram_tensor("b2", [16, 1024], F32, kind="ExternalInput").ap()
    identd = nc.dram_tensor("ident", [128, 128], BF16, kind="ExternalInput").ap()
    ident32d = nc.dram_tensor("ident32", [128, 128], F32, kind="ExternalInput").ap()
    out = nc.dram_tensor("out", [2048, 1024], F32, kind="ExternalOutput").ap()

    def scratch(name, shape):
        kind = "ExternalOutput" if name in dbg else "Internal"
        return nc.dram_tensor(name, shape, BF16, kind=kind).ap()

    qT_s = scratch("qT_s", [8, 128, 2048])
    kT_s = scratch("kT_s", [8, 128, 4096])
    otn_s = scratch("otn_s", [8, 128, 2048])
    vAug_s = nc.dram_tensor("vAug_s", [8, 128, 32, 130], BF16,
                            kind="ExternalOutput" if "vaug" in dbg else "Internal").ap()
    acc_d = nc.dram_tensor("acc_d", [16, 128, 1024], F32,
                           kind="ExternalOutput" if "acc" in dbg else "Internal").ap()
    if "c" in dbg:
        c_dbg = nc.dram_tensor("c_dbg", [128, 256], F32, kind="ExternalOutput").ap()

    CTX = ExitStack()
    with CTX:
        P = Prog(nc, CTX)

        def finish():
            with nc.Block() as block:
                for eng in ENGINES:
                    items = P.streams[eng]
                    if items:
                        getattr(block, eng)(lambda e, it=items: _emit_body(it, P.sems, P.dma_sems, e))

        GE = CTX.enter_context
        ident_sb = GE(nc.sbuf_tensor("ident_sb", [128, 128], BF16))
        ones_cb = GE(nc.sbuf_tensor("ones_cb", [128, 1], BF16))
        ones_rf = GE(nc.sbuf_tensor("ones_rf", [1, 128], F32))
        Bb = GE(nc.sbuf_tensor("Bb", [128, 16], F32))
        BbK = GE(nc.sbuf_tensor("BbK", [128, 8], F32))
        c_sb = GE(nc.sbuf_tensor("c_sb", [128, 256], F32))
        cT_sb = GE(nc.sbuf_tensor("cT_sb", [16, 2048], BF16))
        negmu1 = GE(nc.sbuf_tensor("negmu1", [128, 16], F32))
        rcol = GE(nc.sbuf_tensor("rcol", [128, 8], F32))
        xnT_sb = GE(nc.sbuf_tensor("xnT_sb", [128, 8, 2048], BF16))
        r1sb = GE(nc.sbuf_tensor("r1sb", [128, 16], F32))

        PCTX = ExitStack()
        PE_ = PCTX.enter_context
        qwT_sb = PE_(nc.sbuf_tensor("qwT_sb", [128, 8, 1024], BF16))
        kwT_sb = PE_(nc.sbuf_tensor("kwT_sb", [128, 8, 1024], BF16))
        vwT_sb = PE_(nc.sbuf_tensor("vwT_sb", [128, 8, 1024], BF16))
        rw_sb = PE_(nc.sbuf_tensor("rw_sb", [128, 8, 16], F32))
        ident32_sb = PE_(nc.sbuf_tensor("ident32_sb", [128, 128], F32))
        xnT32 = PE_(nc.sbuf_tensor("xnT32", [128, 8, 128], F32))
        gq_sb = PE_(nc.sbuf_tensor("gq_sb", [128, 8], F32))
        bq_sb = PE_(nc.sbuf_tensor("bq_sb", [128, 8], F32))
        gk_sb = PE_(nc.sbuf_tensor("gk_sb", [128, 8], F32))
        bk_sb = PE_(nc.sbuf_tensor("bk_sb", [128, 8], F32))
        gv_sb = PE_(nc.sbuf_tensor("gv_sb", [128, 8], F32))
        bv_sb = PE_(nc.sbuf_tensor("bv_sb", [128, 8], F32))
        qbc_sb = PE_(nc.sbuf_tensor("qbc_sb", [128, 8], F32))
        kbc_sb = PE_(nc.sbuf_tensor("kbc_sb", [128, 8], F32))
        rb_bc = PE_(nc.sbuf_tensor("rb_bc", [128, 16], F32))
        brow = PE_(nc.sbuf_tensor("brow", [1, 16], F32))
        rbrow = PE_(nc.sbuf_tensor("rbrow", [1, 16], F32))
        sprow = PE_(nc.sbuf_tensor("sprow", [1, 16], F32))
        sprow2 = PE_(nc.sbuf_tensor("sprow2", [1, 16], F32))
        xin = [PE_(nc.sbuf_tensor(f"xin{i}", [128, 1024], F32)) for i in range(2)]
        xwork2 = [PE_(nc.sbuf_tensor(f"xwork{i}", [128, 1024], F32)) for i in range(4)]
        xsq2 = [PE_(nc.sbuf_tensor(f"xsq{i}", [128, 1024], F32)) for i in range(2)]
        xscr = PE_(nc.sbuf_tensor("xscr", [128, 1024], F32))
        xscr_w = [0]
        zb2 = [PE_(nc.sbuf_tensor(f"zb{i}", [128, 1024], BF16)) for i in range(4)]
        zT = [PE_(nc.sbuf_tensor(f"zT{i}", [128, 8, 512], BF16)) for i in range(2)]
        fT_st = PE_(nc.sbuf_tensor("fT_st", [128, 8, 512], BF16))
        vstg = [PE_(nc.sbuf_tensor(f"vstg{i}", [128, 1040], BF16)) for i in range(4)]
        proj = [PE_(nc.sbuf_tensor(f"proj{i}", [128, 8, 512], BF16)) for i in range(2)]
        sqb = fT_st
        rnb = PE_(nc.sbuf_tensor("rnb", [128, 512], F32))
        nrm = PE_(nc.sbuf_tensor("nrm", [1, 512], F32))
        colb = PE_(nc.sbuf_tensor("colb", [128, 20], F32))
        eps_cell = PE_(nc.sbuf_tensor("eps_cell", [128, 1], F32))
        lgt2 = [PE_(nc.sbuf_tensor(f"lgt{i}", [128, 16], F32)) for i in range(2)]
        p162 = [PE_(nc.sbuf_tensor(f"p16_{i}", [128, 16], F32)) for i in range(2)]
        cur2 = [PE_(nc.sbuf_tensor(f"cur{i}", [128, 16], F32)) for i in range(2)]
        msk2 = [PE_(nc.sbuf_tensor(f"msk{i}", [128, 16], F32)) for i in range(2)]
        cbf = PE_(nc.sbuf_tensor("cbf", [128, 256], BF16))

        ps_proj = [PE_(nc.psum_tensor(f"ps_proj{i}", [128, 512], F32)) for i in range(2)]
        ps_tr = [PE_(nc.psum_tensor(f"ps_tr{i}", [128, 128], BF16)) for i in range(2)]
        ps_tr32 = [PE_(nc.psum_tensor(f"ps_tr32{i}", [128, 128], F32)) for i in range(2)]
        ps_rb = PE_(nc.psum_tensor("ps_rb", [128, 512], F32))
        ps_ss = ps_rb[0:1, 0:512]
        ps_rt = PE_(nc.psum_tensor("ps_rt", [128, 16], F32))

        tr_evac = [0, 0]
        tr32_evac = [0, 0]
        xwork_free = [0, 0, 0, 0]
        zb_free = [0, 0, 0, 0]
        zT_used = [0, 0]
        proj_free = [0, 0]
        zT_ready = [0]
        rt_free = [0]
        nrm_free = [0]
        rnb_free = [0]
        fT_free = [0]
        proj_evac = [0, 0]
        vproj_evac = [0, 0]
        xin_ready = [0, 0]
        xin_free = [0, 0]
        vstg_free = [0, 0, 0, 0]

        P.dma("sync", lambda e: e.dma_start(out=ident_sb[:], in_=identd), sem="cS")
        P.dma("sync", lambda e: e.dma_start(out=ident32_sb[:], in_=ident32d), sem="cS")
        P.dma("sync", lambda e: e.dma_start(out=rw_sb[:], in_=rw), sem="cS")
        for dst, src in ((gq_sb, g_q), (bq_sb, b_qn), (gk_sb, g_k), (bk_sb, b_kn),
                         (gv_sb, g_v), (bv_sb, b_vn), (qbc_sb, qb_c), (kbc_sb, kb_c)):
            P.dma("sync", lambda e, d=dst, s=src: e.dma_start(out=d[:], in_=s), sem="cS")
        P.dma("sync", lambda e: e.dma_start(out=brow[:], in_=beta_r), sem="cS")
        P.dma("sync", lambda e: e.dma_start(out=rbrow[:], in_=rb_r), sem="cS")
        dvb = P.dma("sync", lambda e: e.dma_start(out=rcol[:], in_=vb_c), sem="cS")
        for dst, src in ((qwT_sb, qwT), (kwT_sb, kwT), (vwT_sb, vwT)):
            P.dma("gpsimd", lambda e, d=dst, s=src: e.dma_start(
                out=d[:], in_=s.rearrange("(a p) n -> p a n", p=128)), sem="cG")
        cS0 = P.now("cS")
        cG0 = P.now("cG")

        vv0 = P.op("vector", lambda e: e.memset(ones_cb[:], 1.0), inc="vE")
        vv0 = P.op("vector", lambda e: e.memset(eps_cell[:], 1e-5), inc="vE")
        nc.const_aps.register(F32, 1e-5, eps_cell[:])
        vv0 = P.op("vector", lambda e: e.memset(ones_rf[:], 1.0), inc="vE")
        for _vi in range(4):
            vv0 = P.op("vector", lambda e, i=_vi: e.memset(vstg[i][:].rearrange("p (a b) -> p a b", a=16)[:, :, 64:65], 1.0), inc="vE")

        t1 = P.op("tensor", lambda e: e.matmul(ps_rt[:], ones_rf[:], rbrow[:], start=True, stop=True),
                  waits=[("cS", cS0), ("vE", vv0)], inc="tE")
        vrb = P.op("vector", lambda e: e.tensor_copy(rb_bc[:], ps_rt[:]), waits=[("tE", t1)], inc="vE")

        s1 = P.op("scalar", lambda e: e.activation(sprow[:], brow[:], AF.Abs), waits=[("cS", cS0)], inc="sE")
        s1 = P.op("scalar", lambda e: e.activation(sprow[:], sprow[:], AF.Exp, scale=-1.0), waits=[("sE", s1)], inc="sE")
        v1b = P.op("vector", lambda e: e.tensor_scalar_add(sprow[:], sprow[:], 1.0), waits=[("sE", s1)], inc="vE")
        s1 = P.op("scalar", lambda e: e.activation(sprow[:], sprow[:], AF.Ln), waits=[("vE", v1b)], inc="sE")
        v2 = P.op("vector", lambda e: e.tensor_scalar_max(sprow2[:], brow[:], 0.0), waits=[("cS", cS0)], inc="vE")
        v2 = P.op("vector", lambda e: e.tensor_add(sprow[:], sprow[:], sprow2[:]), waits=[("sE", s1), ("vE", v2)], inc="vE")
        v2 = P.op("vector", lambda e: e.tensor_scalar(sprow[:], sprow[:], 0.1, 10.0, OP.max, OP.min), waits=[("vE", v2)], inc="vE")
        v2 = P.op("vector", lambda e: e.tensor_scalar_mul(sprow[:], sprow[:], 0.125), waits=[("vE", v2)], inc="vE")
        t2 = P.op("tensor", lambda e: e.matmul(ps_rt[:], ones_rf[:], sprow[:], start=True, stop=True),
                  waits=[("vE", v2)], inc="tE")
        vBb = P.op("vector", lambda e: e.tensor_copy(Bb[:], ps_rt[:]), waits=[("tE", t2)], inc="vE")
        rt_free[0] = vBb
        vBb = P.op("vector", lambda e: e.tensor_copy(BbK[0:64, :], Bb[0:64, 0:16:2]),
                   waits=[("vE", vBb)], inc="vE")
        vBb = P.op("vector", lambda e: e.tensor_copy(BbK[64:128, :], Bb[64:128, 1:16:2]),
                   waits=[("vE", vBb)], inc="vE")

        bqb = PE_(nc.sbuf_tensor("bqb", [128, 8], BF16))
        bkb = PE_(nc.sbuf_tensor("bkb", [128, 8], BF16))
        bvb = PE_(nc.sbuf_tensor("bvb", [128, 8], BF16))
        vbc = 0
        for dst, srcb in ((bqb, bq_sb), (bkb, bk_sb), (bvb, bv_sb)):
            vbc = P.op("vector", lambda e, d=dst, s=srcb: e.tensor_copy(d[:], s[:]),
                       waits=[("cS", cS0)], inc="vE")
        for pi, (wsb, bcol, biasc, cw) in enumerate((
                (qwT_sb, bqb, qbc_sb, 0), (kwT_sb, bkb, kbc_sb, 0), (vwT_sb, bvb, rcol, 8))):
            bank = pi % 2
            for oc in range(8):
                tf = P.op("tensor", lambda e, o=oc, w=wsb, b=bcol, p=bank, c=cw: e.matmul(
                    ps_proj[p][:, c + o:c + o + 1], w[:, 0, o * 128:(o + 1) * 128], b[:, 0:1],
                    start=True, stop=False),
                    waits=[("cG", cG0), ("vE", vbc)], inc=None)
                for d in range(1, 8):
                    tf = P.op("tensor", lambda e, o=oc, w=wsb, b=bcol, p=bank, c=cw, dd=d: e.matmul(
                        ps_proj[p][:, c + o:c + o + 1], w[:, dd, o * 128:(o + 1) * 128], b[:, dd:dd + 1],
                        start=False, stop=(dd == 7)),
                        inc="tE" if (d == 7) else None)
            rnb_free[0] = P.op("vector", lambda e, bi=biasc, p=bank, c=cw: e.tensor_add(
                bi[:], bi[:], ps_proj[p][:, c:c + 8]),
                waits=[("tE", tf), ("cS", max(cS0, dvb))], inc="vE")
        fold_done = P.now("tE")
        wsc = 0
        for wsb, gcol in ((qwT_sb, gq_sb), (kwT_sb, gk_sb), (vwT_sb, gv_sb)):
            for d in range(8):
                wsc = P.op("vector", lambda e, w=wsb, g=gcol, dd=d: e.tensor_single_scalar(
                    w[:, dd, :], w[:, dd, :], g[:, dd:dd + 1], OP.mult),
                    waits=[("tE", fold_done), ("cS", cS0), ("cG", cG0)], inc="vE")
        vproj_evac[0] = vproj_evac[1] = wsc


        def load_x(tt):
            slot = tt % 2
            xin_ready[slot] = P.dma("sync", lambda e, t=tt, s=slot: e.dma_start(
                out=xin[s][:], in_=x[t * 128:(t + 1) * 128, :]),
                waits=[("vE", xin_free[slot])], sem=f"xin{slot}")

        load_x(0)
        load_x(1)

        def transpose_to(dst_fn, src_fn, extra):
            for dc in range(8):
                sl = dc % 2
                tval = P.op("tensor", lambda e, d=dc, s=sl: e.transpose(
                    ps_tr[s][:], src_fn(d), ident_sb[:]),
                    waits=list(extra) + [("sE", tr_evac[sl])], inc="tE")
                tr_evac[sl] = P.op("scalar", lambda e, d=dc, s=sl: e.copy(dst_fn(d), ps_tr[s][:]),
                                   waits=[("tE", tval)], inc="sE")

        def ln_pair(tt_a, tt_b):
            sts = []
            for tt in (tt_a, tt_b):
                p = tt % 2
                cc = [colb[:, 8 * p + j:8 * p + j + 1] for j in range(8)]
                sts.append(dict(
                    tt=tt, p=p, q4=tt % 4, slot=tt % 2, xw=xwork2[tt % 4], xq=xsq2[p], zb=zb2[tt % 4], cc=cc,
                    nm1=negmu1[:, tt:tt + 1] if tt < 16 else cc[6],
                    r1c=r1sb[:, tt:tt + 1] if tt < 16 else cc[7]))
            for st in sts:
                st["v"] = P.op("vector", lambda e, s=st["slot"], a=st["cc"][0]: e.reduce_sum(a, xin[s][:], axis=AX.X),
                               waits=[(f"xin{st['slot']}", xin_ready[st["slot"]])], inc="vE")
            for st in sts:
                st["v"] = P.op("vector", lambda e, d=st["nm1"], a=st["cc"][0]: e.tensor_scalar_mul(d, a, -1.0 / 1024.0),
                               waits=[("vE", st["v"])], inc="vE")
            for st in sts:
                st["v"] = P.op("vector", lambda e, s=st["slot"], d=st["nm1"], w=st["xw"]: e.tensor_scalar_add(w[:], xin[s][:], d),
                               waits=[("vE", st["v"]), ("tE", xwork_free[st["q4"]])], inc="vE")
                xin_free[st["slot"]] = st["v"]
            for st in sts:
                if st["tt"] + 2 < 32:
                    load_x(st["tt"] + 2)
            for st in sts:
                st["s"] = P.op("scalar", lambda e, w=st["xw"], a=st["cc"][1]: e.activation(xscr[:], w[:], AF.Square, accum_out=a),
                               waits=[("vE", st["v"]), ("sE", xscr_w[0])], inc="sE")
                xscr_w[0] = st["s"]
            for st in sts:
                st["s"] = P.op("scalar", lambda e, a=st["cc"][2], b=st["cc"][1]: e.activation(a, b, AF.Ln, scale=1.0 / 1024.0, bias=1e-5),
                               waits=[("sE", st["s"])], inc="sE")
            for st in sts:
                st["s"] = P.op("scalar", lambda e, d=st["r1c"], a=st["cc"][2]: e.activation(d, a, AF.Exp, scale=-0.5),
                               waits=[("sE", st["s"])], inc="sE")
            for st in sts:
                st["v"] = P.op("vector", lambda e, d=st["r1c"], w=st["xw"]: e.tensor_single_scalar(w[:], w[:], d, OP.mult),
                               waits=[("sE", st["s"])], inc="vE")
                st["vxn"] = st["v"]
            for st in sts:
                st["v"] = P.op("vector", lambda e, w=st["xw"], a=st["cc"][3]: e.reduce_sum(a, w[:], axis=AX.X),
                               waits=[("vE", st["v"])], inc="vE")
            for st in sts:
                st["v"] = P.op("vector", lambda e, a=st["cc"][4], b=st["cc"][3]: e.tensor_scalar_mul(a, b, -1.0 / 1024.0),
                               waits=[("vE", st["v"])], inc="vE")
            for st in sts:
                st["v"] = P.op("vector", lambda e, w=st["xw"], q=st["xq"], a=st["cc"][4]: e.tensor_scalar_add(q[:], w[:], a),
                               waits=[("vE", st["v"])], inc="vE")
            for st in sts:
                st["s"] = P.op("scalar", lambda e, q=st["xq"], a=st["cc"][5]: e.activation(xscr[:], q[:], AF.Square, accum_out=a),
                               waits=[("vE", st["v"]), ("sE", xscr_w[0])], inc="sE")
                xscr_w[0] = st["s"]
            for st in sts:
                st["s"] = P.op("scalar", lambda e, a=st["cc"][5]: e.activation(a, a, AF.Ln, scale=1.0 / 1024.0, bias=1e-5),
                               waits=[("sE", st["s"])], inc="sE")
            for st in sts:
                st["s"] = P.op("scalar", lambda e, a=st["cc"][5]: e.activation(a, a, AF.Exp, scale=-0.5),
                               waits=[("sE", st["s"])], inc="sE")
            for st in sts:
                st["vz"] = P.op("vector", lambda e, z=st["zb"], q=st["xq"], a=st["cc"][5]: e.tensor_single_scalar(z[:], q[:], a, OP.mult),
                               waits=[("sE", st["s"]), ("tE", zb_free[st["q4"]])], inc="vE")
            return sts

        def trrouter(st, zsl):
            tt, p, ti = st["tt"], st["p"], st["tt"] % 4
            transpose_to(
                lambda dc, t=ti, zz=zsl: zT[zz][:, dc, t * 128:(t + 1) * 128],
                lambda dc, z=st["zb"]: z[:, dc * 128:(dc + 1) * 128],
                [("vE", st["vz"]), ("tE", zT_used[zsl])])
            zb_free[tt % 4] = P.now("tE")
            if tt < 16 and router:
                ev2 = 0
                for dc in range(8):
                    tsl = dc % 2
                    tval = P.op("tensor", lambda e, d=dc, s=tsl, w=st["xw"]: e.transpose(
                        ps_tr32[s][:], w[:, d * 128:(d + 1) * 128], ident32_sb[:]),
                        waits=[("vE", st["vxn"]), ("sE", tr32_evac[tsl])], inc="tE")
                    P.op("scalar", lambda e, d=dc, s=tsl: e.copy(xnT32[:, d, :], ps_tr32[s][:]),
                         waits=[("tE", tval)], inc="sE")
                    ev2 = P.op("scalar", lambda e, d=dc, s=tsl, t=tt: e.copy(
                        xnT_sb[:, d, t * 128:(t + 1) * 128], ps_tr32[s][:]),
                        waits=[("tE", tval)], inc="sE")
                    tr32_evac[tsl] = ev2
                xwork_free[tt % 4] = P.now("tE")
                for dc in range(8):
                    tv = P.op("tensor", lambda e, d=dc: e.matmul(
                        ps_rt[:], xnT32[:, d, :], rw_sb[:, d, :],
                        start=(d == 0), stop=(d == 7)),
                        waits=[("sE", ev2), ("cS", cS0), ("vE", rt_free[0])],
                        inc="tE" if dc == 7 else None)
                st["rv"] = P.op("vector", lambda e, g=lgt2[p]: e.tensor_add(g[:], ps_rt[:], rb_bc[:]),
                                waits=[("tE", tv), ("vE", vrb)], inc="vE")
                rt_free[0] = st["rv"]

        def softmax_pair(sts):
            for st in sts:
                p = st["p"]
                st["g"] = lgt2[p]
                st["pp"] = p162[p]
                st["cu"] = cur2[p]
                st["mk"] = msk2[p]
                st["c0"] = colb[:, 16 + 2 * p:17 + 2 * p]
                st["c1"] = colb[:, 17 + 2 * p:18 + 2 * p]
                st["v"] = st["rv"]

            def V(st, fn, waits_extra=()):
                st["v"] = P.op("vector", fn, waits=[("vE", st["v"])] + list(waits_extra), inc="vE")

            for st in sts:
                V(st, lambda e, g=st["g"], a=st["c0"]: e.reduce_max(a, g[:], axis=AX.X))
            for st in sts:
                V(st, lambda e, g=st["g"], a=st["c0"]: e.tensor_single_scalar(g[:], g[:], a, OP.subtract))
            for st in sts:
                st["s"] = P.op("scalar", lambda e, g=st["g"], q=st["pp"]: e.activation(q[:], g[:], AF.Exp),
                               waits=[("vE", st["v"])], inc="sE")
            for st in sts:
                st["v"] = P.op("vector", lambda e, q=st["pp"], a=st["c0"]: e.reduce_sum(a, q[:], axis=AX.X),
                               waits=[("sE", st["s"])], inc="vE")
            for st in sts:
                V(st, lambda e, a=st["c1"], b=st["c0"]: e.reciprocal(a, b))
            for st in sts:
                V(st, lambda e, q=st["pp"], a=st["c1"]: e.tensor_single_scalar(q[:], q[:], a, OP.mult))
            for st in sts:
                V(st, lambda e, q=st["pp"], c=st["cu"]: e.tensor_copy(c[:], q[:]))
            for _ in range(3):
                for st in sts:
                    V(st, lambda e, c=st["cu"], a=st["c0"]: e.reduce_max(a, c[:], axis=AX.X))
                for st in sts:
                    V(st, lambda e, c=st["cu"], m=st["mk"], a=st["c0"]: e.tensor_single_scalar(m[:], c[:], a, OP.is_ge))
                for st in sts:
                    V(st, lambda e, c=st["cu"], m=st["mk"]: e.scalar_tensor_tensor(c[:], m[:], -2.0, c[:], OP.mult, OP.add))
            for st in sts:
                V(st, lambda e, c=st["cu"], a=st["c0"]: e.reduce_max(a, c[:], axis=AX.X))
            for st in sts:
                V(st, lambda e, q=st["pp"], m=st["mk"], a=st["c0"]: e.tensor_single_scalar(m[:], q[:], a, OP.is_ge))
            for st in sts:
                V(st, lambda e, q=st["pp"], m=st["mk"]: e.tensor_mul(q[:], q[:], m[:]))
            for st in sts:
                V(st, lambda e, q=st["pp"], a=st["c0"]: e.reduce_sum(a, q[:], axis=AX.X))
            for st in sts:
                V(st, lambda e, a=st["c1"], b=st["c0"]: e.reciprocal(a, b))
            for st in sts:
                V(st, lambda e, q=st["pp"], a=st["c1"], t=st["tt"]: e.tensor_single_scalar(
                    c_sb[:, t * 16:(t + 1) * 16], q[:], a, OP.mult))

        def proj_mms(wsb, bias_col, zsl, psl):
            for oc in range(8):
                for dc in range(8):
                    tvv = P.op("tensor", lambda e, d=dc, o=oc, w=wsb, zz=zsl: e.matmul(
                        ps_proj[o % 2][:], w[:, d, o * 128:(o + 1) * 128], zT[zz][:, d, :],
                        start=(d == 0), stop=(d == 7)),
                        waits=[("sE", zT_ready[0]), ("vE", vproj_evac[oc % 2]), ("cG", cG0),
                               ("sE", proj_evac[oc % 2])],
                        inc="tE" if dc == 7 else None)
                proj_evac[oc % 2] = P.op("scalar", lambda e, o=oc, b=bias_col, pp=psl: e.activation(
                    proj[pp][:, o, :], ps_proj[o % 2][:], AF.Identity, bias=b[:, o:o + 1]),
                    waits=[("tE", tvv), ("vE", proj_free[psl])], inc="sE")

        def make_fin(psl, is_q, spill_dst):
            ev0, ev1 = proj_evac[0], proj_evac[1]

            def fin():
                vq = P.op("vector", lambda e, pp=psl: e.tensor_mul(sqb[:], proj[pp][:], proj[pp][:]),
                          waits=[("sE", max(ev0, ev1)), ("sfT", fT_free[0])], inc="vE")
                for oc in range(8):
                    tss = P.op("tensor", lambda e, o=oc: e.matmul(
                        ps_ss[:], ones_cb[:], sqb[:, o, :], start=(o == 0), stop=(o == 7)),
                        waits=[("vE", vq), ("sE", nrm_free[0])],
                        inc="tE" if oc == 7 else None)
                sn = P.op("scalar", lambda e: e.activation(nrm[:], ps_ss[:], AF.Ln), waits=[("tE", tss)], inc="sE")
                nrm_free[0] = sn
                sn = P.op("scalar", lambda e: e.activation(nrm[:], nrm[:], AF.Exp, scale=0.5), waits=[("sE", sn)], inc="sE")
                vn_ = P.op("vector", lambda e: e.tensor_scalar_max(nrm[:], nrm[:], 1e-6), waits=[("sE", sn)], inc="vE")
                sn = P.op("scalar", lambda e: e.activation(nrm[:], nrm[:], AF.Ln), waits=[("vE", vn_)], inc="sE")
                sn = P.op("scalar", lambda e: e.activation(nrm[:], nrm[:], AF.Exp, scale=-1.0), waits=[("sE", sn)], inc="sE")
                tb = P.op("tensor", lambda e: e.matmul(ps_rb[:], ones_rf[:], nrm[:], start=True, stop=True),
                          waits=[("sE", sn), ("vE", rnb_free[0])], inc="tE")
                vr = P.op("vector", lambda e: e.tensor_copy(rnb[:], ps_rb[:]),
                          waits=[("tE", tb), ("vE", max(proj_free[0], proj_free[1]))], inc="vE")
                rnb_free[0] = vr
                if is_q:
                    lastv = P.op("vector", lambda e, pp=psl: e.tensor_tensor(
                        fT_st[:], proj[pp][:],
                        rnb[:].unsqueeze(1).broadcast_to((128, 8, 512)), OP.mult),
                        waits=[("vE", vr), ("sfT", fT_free[0])], inc="vE")
                else:
                    lastv = 0
                    for oc in range(8):
                        lastv = P.op("vector", lambda e, o=oc, pp=psl: e.scalar_tensor_tensor(
                            fT_st[:, o, :], proj[pp][:, o, :], BbK[:, o:o + 1],
                            rnb[:], OP.mult, OP.mult),
                            waits=[("vE", vr), ("sfT", fT_free[0])], inc="vE")
                proj_free[psl] = lastv
                fT_free[0] = P.dma("sync", lambda e, d=spill_dst: e.dma_start(
                    out=d, in_=fT_st[:]), waits=[("vE", lastv)], sem="sfT")
            return fin

        p_ctr = [0]
        pending_fin = [None]

        def run_pending():
            if pending_fin[0] is not None:
                pending_fin[0]()
                pending_fin[0] = None

        pend_ln = [None]
        for gi in range(8):
            t0 = gi * 4
            zsl = gi % 2
            if pend_ln[0] is None:
                stsA = ln_pair(t0, t0 + 1)
                trrouter(stsA[0], zsl)
                trrouter(stsA[1], zsl)
                stsB = ln_pair(t0 + 2, t0 + 3)
            else:
                stsA, stsB = pend_ln[0]
                trrouter(stsA[0], zsl)
                trrouter(stsA[1], zsl)
            if t0 < 16 and router:
                softmax_pair(stsA)
            trrouter(stsB[0], zsl)
            trrouter(stsB[1], zsl)
            if t0 < 16 and router:
                softmax_pair(stsB)
            zT_ready[0] = P.now("sE")
            nxt = [None, None]

            if gi < 4:
                psl = p_ctr[0] % 2
                p_ctr[0] += 1
                proj_mms(qwT_sb, qbc_sb, zsl, psl)
                run_pending()
                pending_fin[0] = make_fin(
                    psl, True,
                    qT_s[:, :, gi * 512:(gi + 1) * 512].rearrange("a p t -> p a t"))
                if gi < 7:
                    nxt[0] = ln_pair(t0 + 4, t0 + 5)
            psl = p_ctr[0] % 2
            p_ctr[0] += 1
            proj_mms(kwT_sb, kbc_sb, zsl, psl)
            run_pending()
            pending_fin[0] = make_fin(
                psl, False,
                kT_s[:, :, gi * 512:(gi + 1) * 512].rearrange("a p t -> p a t"))
            if gi < 7 and nxt[0] is None:
                nxt[0] = ln_pair(t0 + 4, t0 + 5)
            elif gi < 7:
                nxt[1] = ln_pair(t0 + 6, t0 + 7)
            for ti in range(4):
                tt = gi * 4 + ti
                st2 = ti
                ev_last = 0
                for oc2 in range(2):
                    sl = (ti * 2 + oc2) % 2
                    for dc in range(8):
                        tvv = P.op("tensor", lambda e, d=dc, t=ti, o=oc2, s=sl, zz=zsl: e.matmul(
                            ps_proj[s][:], zT[zz][:, d, t * 128:(t + 1) * 128],
                            vwT_sb[:, d, o * 512:(o + 1) * 512],
                            start=(d == 0), stop=(d == 7)),
                            waits=[("sE", zT_ready[0]), ("vE", vproj_evac[sl]), ("sE", proj_evac[sl])],
                            inc="tE" if dc == 7 else None)
                    vproj_evac[sl] = P.op("vector", lambda e, o=oc2, s=sl, g=st2: e.tensor_copy(
                        vstg[g][:].rearrange("p (a b) -> p a b", a=16)[:, o * 8:(o + 1) * 8, 0:64],
                        ps_proj[s][:].rearrange("p (a t) -> p a t", a=8)),
                        waits=[("tE", tvv), (f"svau{st2}", vstg_free[st2])], inc="vE")
                    ev_last = vproj_evac[sl]
                vstg_free[st2] = P.dma("sync", lambda e, t=tt, g=st2: e.dma_start(
                    out=vAug_s[:, :, t, :].rearrange("a p b -> p a b"),
                    in_=vstg[g][:].rearrange("p (a b) -> p a b", a=8)),
                    waits=[("vE", ev_last)], sem=f"svau{st2}")
            zT_used[zsl] = P.now("tE")
            run_pending()
            if gi < 7 and nxt[1] is None:
                nxt[1] = ln_pair(t0 + 6, t0 + 7)
            pend_ln[0] = (nxt[0], nxt[1]) if gi < 7 else None

        vcb = P.op("vector", lambda e: e.tensor_copy(cbf[:], c_sb[:]), waits=[("vE", P.now("vE"))], inc="vE")
        for tt in range(16):
            sl = tt % 2
            tvv = P.op("tensor", lambda e, t=tt, s=sl: e.transpose(
                ps_tr[s][0:16, :], cbf[:, t * 16:(t + 1) * 16], ident_sb[:]),
                waits=[("vE", vcb), ("sE", tr_evac[sl])], inc="tE")
            tr_evac[sl] = P.op("scalar", lambda e, t=tt, s=sl: e.copy(
                cT_sb[0:16, t * 128:(t + 1) * 128], ps_tr[s][0:16, :]),
                waits=[("tE", tvv)], inc="sE")
        if "c" in dbg:
            P.dma("sync", lambda e: e.dma_start(out=c_dbg, in_=c_sb[:]), waits=[("vE", P.now("vE"))], sem="dbg")

        P.barrier()
        PCTX.close()

        if "A" not in stages:
            finish()
            return nc

        FCTX = ExitStack()
        FE = FCTX.enter_context
        h1T = FE(nc.sbuf_tensor("h1T", [128, 8, 2048], BF16))
        w1b = [FE(nc.sbuf_tensor(f"w1b{i}", [128, 8, 1024], BF16)) for i in range(2)]
        w2b = [FE(nc.sbuf_tensor(f"w2b{i}", [128, 8, 1024], BF16)) for i in range(2)]
        b1sb = [FE(nc.sbuf_tensor(f"b1sb{i}", [128, 16], F32)) for i in range(2)]
        b2sb = FE(nc.sbuf_tensor("b2sb", [16, 1024], BF16))
        ffst = [FE(nc.sbuf_tensor(f"ffst{i}", [128, 1024], F32)) for i in range(2)]
        qToc = [FE(nc.sbuf_tensor(f"qToc{i}", [128, 2048], BF16)) for i in range(2)]
        kToc = [FE(nc.sbuf_tensor(f"kToc{i}", [128, 4096], BF16)) for i in range(2)]
        vAugP = [FE(nc.sbuf_tensor(f"vAugP{i}", [128, 32, 130], BF16)) for i in range(2)]
        PT = [FE(nc.sbuf_tensor(f"PT{i}", [128, 512], BF16)) for i in range(2)]
        den_sb = [FE(nc.sbuf_tensor(f"den_sb{i}", [1, 512], F32)) for i in range(2)]
        rden = [FE(nc.sbuf_tensor(f"rden{i}", [1, 512], F32)) for i in range(2)]
        rdb = FE(nc.sbuf_tensor("rdb", [64, 512], F32))
        otn_st = [FE(nc.sbuf_tensor(f"otn_st{i}", [128, 512], BF16)) for i in range(2)]
        ps_h2 = [FE(nc.psum_tensor(f"ps_h{i}", [128, 512], F32)) for i in range(2)]
        ps_o2 = [FE(nc.psum_tensor(f"ps_o{i}", [128, 512], F32)) for i in range(2)]
        ps_s2 = [FE(nc.psum_tensor(f"ps_s{i}", [128, 512], F32)) for i in range(2)]
        ps_av = [FE(nc.psum_tensor(f"ps_av{i}", [128, 512], F32)) for i in range(2)]
        ps_rd = ps_s2[0][0:64, 0:512]

        b2_rd = P.dma("gpsimd", lambda e: e.dma_start(out=b2sb[:], in_=b2d), sem="sb2")

        kq_free = [0, 0]
        kT_rd = [0, 0]
        qT_rd = [0, 0]
        va_rd = [0, 0]
        exp_done = [0, 0]
        av_done = [0, 0]
        otn_free = [0, 0]
        avps_free = [0, 0]
        rdb_free = [0]
        rdb_read = [0]
        rden_free = [0, 0]

        def load_oc(o):
            sl = o % 2
            kT_rd[sl] = P.dma("sync", lambda e, oo=o, s=sl: e.dma_start(out=kToc[s][:], in_=kT_s[oo]),
                              waits=[("tE", kq_free[sl])], sem=f"skT{sl}")
            qT_rd[sl] = P.dma("sync", lambda e, oo=o, s=sl: e.dma_start(out=qToc[s][:], in_=qT_s[oo]),
                              waits=[("tE", kq_free[sl])], sem=f"sqT{sl}")
            va_rd[sl] = P.dma("sync", lambda e, oo=o, s=sl: e.dma_start(
                out=vAugP[s][:], in_=vAug_s[oo]),
                waits=[("tE", kq_free[sl])], sem=f"sva{sl}")

        def a_gen():
            load_oc(0)
            load_oc(1)
            for oc in range(8):
                sl = oc % 2

                def scores(hh, m, n, s_=sl):
                    pb = hh * 64
                    tp = (64, 0) if hh else (0, 0)
                    return P.op("tensor", lambda e, s=hh, mm=m, p=pb, t=tp, nn=n, sll=s_: e.matmul(
                        ps_s2[s][:], kToc[sll][p:p + 64, mm * 128:(mm + 1) * 128],
                        qToc[sll][p:p + 64, nn * 512:(nn + 1) * 512],
                        start=True, stop=True, tile_position=t),
                        waits=[(f"skT{s_}", kT_rd[s_]), (f"sqT{s_}", qT_rd[s_]),
                               ("sE", exp_done[hh])] +
                              ([("vE", rdb_free[0])] if hh == 0 else []),
                        inc="tE")

                for n in range(4):
                    sc = [scores(0, 0, n), scores(1, 0, n)]
                    for m in range(32):
                        for hh in range(2):
                            exp_done[hh] = P.op("scalar", lambda e, s=hh: e.activation(
                                PT[s][:], ps_s2[s][:], AF.Exp),
                                waits=[("tE", sc[hh]), ("tE", av_done[hh])], inc="sE")
                        if m + 1 < 32:
                            sc = [scores(0, m + 1, n), scores(1, m + 1, n)]
                        for hh in range(2):
                            av_done[hh] = P.op("tensor", lambda e, s=hh, mm=m, sll=sl: e.matmul(
                                ps_av[s][0:65, :],
                                vAugP[sll][:].rearrange("p a (b c) -> p a b c", b=2)[:, mm, s, :],
                                PT[s][:], start=(mm == 0), stop=(mm == 31)),
                                waits=[("sE", exp_done[hh]), ("vE", avps_free[hh]), (f"sva{sl}", va_rd[sl])],
                                inc="tE")
                        yield
                    for hh in range(2):
                        pb = hh * 64
                        sd = P.op("scalar", lambda e, a=hh: e.copy(den_sb[a][:], ps_av[a][64:65, :]),
                                  waits=[("tE", av_done[hh])], inc="sE")
                        sd = P.op("scalar", lambda e, a=hh: e.activation(rden[a][:], den_sb[a][:], AF.Ln),
                                  waits=[("sE", sd), ("tE", rden_free[hh])], inc="sE")
                        sd = P.op("scalar", lambda e, a=hh: e.activation(rden[a][:], rden[a][:], AF.Exp, scale=-1.0),
                                  waits=[("sE", sd)], inc="sE")
                        tb = P.op("tensor", lambda e, a=hh: e.matmul(ps_rd, ones_rf[0:1, 0:64], rden[a][:], start=True, stop=True),
                                  waits=[("sE", sd), ("vE", rdb_free[0]), ("sE", exp_done[0])], inc="tE")
                        rden_free[hh] = tb
                        vv = P.op("vector", lambda e: e.tensor_copy(rdb[:], ps_rd),
                                  waits=[("tE", tb), ("vE", rdb_read[0])], inc="vE")
                        rdb_free[0] = vv
                        avps_free[hh] = P.op("vector", lambda e, p=pb, a=hh, s=n % 2: e.tensor_tensor(
                            otn_st[s][p:p + 64, :], ps_av[a][0:64, :], rdb[:], OP.mult),
                            waits=[("vE", vv), (f"sotn{n % 2}", otn_free[n % 2])], inc="vE")
                        rdb_read[0] = avps_free[hh]
                    vadd = P.op("vector", lambda e, o=oc, s=n % 2: e.tensor_scalar_add(
                        otn_st[s][:], otn_st[s][:], rcol[:, o:o + 1]),
                        waits=[("vE", avps_free[0]), ("vE", avps_free[1])], inc="vE")
                    otn_free[n % 2] = P.dma("sync", lambda e, o=oc, nn=n, s=n % 2: e.dma_start(
                        out=otn_s[o, :, nn * 512:(nn + 1) * 512], in_=otn_st[s][:]),
                        waits=[("vE", vadd)], sem=f"sotn{n % 2}")
                    yield
                kq_free[sl] = max(sc[0], sc[1], av_done[0], av_done[1], rden_free[0], rden_free[1])
                if oc + 2 < 8:
                    load_oc(oc + 2)

        agen = a_gen()

        def inject(k=1):
            for _ in range(k):
                next(agen, None)

        o_done = [0, 0]
        acc_init_done = 0
        acc_cnt = [0] * 16
        ffst_free = [("sacc0", 0), ("sacc0", 0)]
        ob_ctr = [0]
        for tt in range(16):
            sl = tt % 2
            evs = 0
            for oc2 in range(2):
                ob = ob_ctr[0] % 2
                ob_ctr[0] += 1
                tvv = P.op("tensor", lambda e, t=tt, o=oc2, b=ob: e.matmul(
                    ps_o2[b][:], cT_sb[0:16, t * 128:(t + 1) * 128],
                    b2sb[0:16, o * 512:(o + 1) * 512], start=True, stop=True),
                    waits=[("sb2", b2_rd), ("vE", o_done[ob])], inc="tE")
                evs = P.op("vector", lambda e, o=oc2, s=sl, b=ob: e.tensor_copy(
                    ffst[s][:, o * 512:(o + 1) * 512], ps_o2[b][:]),
                    waits=[("tE", tvv), ffst_free[sl]], inc="vE")
                o_done[ob] = evs
            acc_cnt[tt] += 1
            v = P.dma("gpsimd", lambda e, t=tt, s=sl: e.dma_start(
                out=acc_d[t], in_=ffst[s][:]),
                waits=[("vE", evs)], sem=f"sacc{tt}")
            ffst_free[sl] = (f"sacc{tt}", v)
        acc_init_done = P.now("vE")

        w1_rd = [0, 0]
        w2_rd = [0, 0]
        w1_free = [0, 0]
        w2_free = [0, 0]
        b1_rd = [0, 0]
        b1_free = [0, 0]
        h_done = [0, 0]

        def load_w(e_, hf):
            sp = (2 * e_ + hf) % 2
            w1_rd[sp] = P.dma("gpsimd", lambda e, ee=e_, h=hf, s=sp: e.dma_start(
                out=w1b[s][:], in_=w1d[ee].rearrange("a p n -> p a n")[:, :, h * 1024:(h + 1) * 1024]),
                waits=[("tE", w1_free[sp])], sem=f"sw1{sp}")
            w2_rd[sp] = P.dma("gpsimd", lambda e, ee=e_, h=hf, s=sp: e.dma_start(
                out=w2b[s][:], in_=w2d[ee, h * 8:(h + 1) * 8].rearrange("a p n -> p a n")),
                waits=[("tE", w2_free[sp])], sem=f"sw2{sp}")
            if hf == 0:
                b1_rd[e_ % 2] = P.dma("sync", lambda e, ee=e_: e.dma_start(
                    out=b1sb[ee % 2][:], in_=b1c[ee]), waits=[("sE", b1_free[e_ % 2])], sem=f"sb1{e_ % 2}")

        load_w(0, 0)
        load_w(0, 1)

        hb_ctr = [0]
        for e_ in range(16):
            for hf in range(2):
                sp = (2 * e_ + hf) % 2
                for ic in range(8):
                    for np2 in range(2):
                        for sb2 in range(2):
                            hb = hb_ctr[0] % 2
                            hb_ctr[0] += 1
                            for dc in range(8):
                                tvv = P.op("tensor", lambda e, d=dc, i=ic, n2=np2, s2=sb2, p=sp, b=hb: e.matmul(
                                    ps_h2[b][:],
                                    w1b[p][:, d, i * 128:(i + 1) * 128],
                                    xnT_sb[:, d, n2 * 1024 + s2 * 512:n2 * 1024 + (s2 + 1) * 512],
                                    start=(d == 0), stop=(d == 7)),
                                    waits=[(f"sw1{sp}", w1_rd[sp]), (f"sb1{e_ % 2}", b1_rd[e_ % 2]),
                                           ("sE", h_done[hb]),
                                           ("vE", acc_init_done)],
                                    inc="tE" if dc == 7 else None)
                            h_done[hb] = P.op("scalar", lambda e, i=ic, n2=np2, s2=sb2, ee=e_, h=hf, b=hb: e.activation(
                                h1T[:, i, n2 * 1024 + s2 * 512:n2 * 1024 + (s2 + 1) * 512], ps_h2[b][:], AF.Gelu,
                                bias=b1sb[ee % 2][:, h * 8 + i:h * 8 + i + 1]),
                                waits=[("tE", tvv)], inc="sE")
                w1_free[sp] = P.now("tE")
                gelu_all = P.now("sE")
                if hf == 1:
                    b1_free[e_ % 2] = gelu_all
                for tt in range(16):
                    sl = tt % 2
                    evs = 0
                    for oc2 in range(2):
                        ob = ob_ctr[0] % 2
                        ob_ctr[0] += 1
                        for ic in range(8):
                            tvv = P.op("tensor", lambda e, i=ic, t=tt, o=oc2, p=sp, b=ob: e.matmul(
                                ps_o2[b][:],
                                h1T[:, i, t * 128:(t + 1) * 128],
                                w2b[p][:, i, o * 512:(o + 1) * 512],
                                start=(i == 0), stop=(i == 7)),
                                waits=[("sE", gelu_all), (f"sw2{sp}", w2_rd[sp]), ("vE", o_done[ob])],
                                inc="tE" if ic == 7 else None)
                        evs = P.op("vector", lambda e, t=tt, o=oc2, s=sl, ee=e_, b=ob: e.tensor_single_scalar(
                            ffst[s][:, o * 512:(o + 1) * 512], ps_o2[b][:],
                            c_sb[:, t * 16 + ee:t * 16 + ee + 1], OP.mult),
                            waits=[("tE", tvv), ffst_free[sl]], inc="vE")
                        o_done[ob] = evs
                        inject(1)
                    acc_cnt[tt] += 1
                    v = P.dma("gpsimd", lambda e, t=tt, s=sl: e.dma_start(
                        out=acc_d[t], in_=ffst[s][:], accum_op=OP.add),
                        waits=[("vE", evs)], sem=f"sacc{tt}")
                    ffst_free[sl] = (f"sacc{tt}", v)
                w2_free[sp] = P.now("tE")
                nxt = 2 * e_ + hf + 2
                if nxt < 32:
                    load_w(nxt // 2, nxt % 2)

        for _ in agen:
            pass

        P.barrier()
        FCTX.close()

        if "D" not in stages:
            finish()
            return nc

        DCTX = ExitStack()
        DE = DCTX.enter_context
        NSL = 3
        acc_t = [DE(nc.sbuf_tensor(f"acc_t{i}", [128, 1024], F32)) for i in range(NSL)]
        xq_t = [DE(nc.sbuf_tensor(f"xq_t{i}", [128, 1024], F32)) for i in range(NSL)]
        otn_t = [DE(nc.sbuf_tensor(f"otn_t{i}", [128, 8, 128], BF16)) for i in range(NSL)]
        outb = [DE(nc.sbuf_tensor(f"outb{i}", [128, 1024], F32)) for i in range(NSL)]
        owT_sb = DE(nc.sbuf_tensor("owT_sb", [128, 8, 1024], BF16))
        ob_bc = DE(nc.sbuf_tensor("ob_bc", [128, 1024], F32))
        obrow = DE(nc.sbuf_tensor("obrow", [1, 1024], F32))
        t_xn = DE(nc.sbuf_tensor("t_xn", [128, 512], F32))
        t_s = DE(nc.sbuf_tensor("t_s", [128, 512], F32))
        ps_d = [DE(nc.psum_tensor(f"ps_d{i}", [128, 1024], F32)) for i in range(2)]
        ps_b = DE(nc.psum_tensor("ps_b", [128, 512], F32))

        ow_rd = P.dma("gpsimd", lambda e: e.dma_start(
            out=owT_sb[:], in_=owT.rearrange("(a p) n -> p a n", p=128)), sem="sow")
        ob_rd = P.dma("sync", lambda e: e.dma_start(out=obrow[:], in_=ob_r), sem="sob")
        for half in range(2):
            tb = P.op("tensor", lambda e, hf=half: e.matmul(
                ps_b[:], ones_rf[:], obrow[:, hf * 512:(hf + 1) * 512], start=True, stop=True),
                waits=[("sob", ob_rd), ("vE", P.now("vE"))], inc="tE")
            P.op("vector", lambda e, hf=half: e.tensor_copy(ob_bc[:, hf * 512:(hf + 1) * 512], ps_b[:]),
                 waits=[("tE", tb)], inc="vE")

        otn_fin = [(s, P.now(s)) for s in ("sotn0", "sotn1")]
        acc_ld = [0] * NSL
        xq_ld = [0] * NSL
        otn_ld = [0] * NSL
        accq_free = [0] * NSL
        xq_free = [0] * NSL
        otn_mm_free = [0] * NSL
        out_free = [0] * NSL
        d_done = [0, 0]
        ts_read = [0]

        def load_tile(tt):
            slot = tt % NSL
            acc_ld[slot] = P.dma("sync", lambda e, t=tt, s=slot: e.dma_start(
                out=acc_t[s][:], in_=acc_d[t]),
                waits=[(f"sacc{tt}", acc_cnt[tt] * 16), ("vE", accq_free[slot])],
                sem=f"sdac{slot}")
            xq_ld[slot] = P.dma("gpsimd", lambda e, t=tt, s=slot: e.dma_start(
                out=xq_t[s][:], in_=x[t * 128:(t + 1) * 128, :]),
                waits=[("vE", xq_free[slot])], sem=f"sdxq{slot}")
            otn_ld[slot] = P.dma("sync", lambda e, t=tt, s=slot: e.dma_start(
                out=otn_t[s][:], in_=otn_s[:, :, t * 128:(t + 1) * 128].rearrange("a p t -> p a t")),
                waits=[otn_fin[0], otn_fin[1], ("tE", otn_mm_free[slot])],
                sem=f"sdot{slot}")

        for tt in range(NSL):
            load_tile(tt)

        for tt in range(16):
            slot = tt % NSL
            sil = P.op("scalar", lambda e, s=slot: e.activation(acc_t[s][:], acc_t[s][:], AF.Silu),
                       waits=[(f"sdac{slot}", acc_ld[slot])], inc="sE")
            for oc2 in range(2):
                sl = (2 * tt + oc2) % 2
                for dc in range(8):
                    tvv = P.op("tensor", lambda e, d=dc, o=oc2, s=sl, st=slot: e.matmul(
                        ps_d[s][:, o * 512:(o + 1) * 512],
                        otn_t[st][:, d, :],
                        owT_sb[:, d, o * 512:(o + 1) * 512],
                        start=(d == 0), stop=(d == 7)),
                        waits=[(f"sdot{slot}", otn_ld[slot]), ("sow", ow_rd), ("vE", d_done[sl])],
                        inc="tE" if dc == 7 else None)
                vv = P.op("vector", lambda e, t=tt, o=oc2, st=slot: e.tensor_scalar(
                    t_xn[:], xq_t[st][:, o * 512:(o + 1) * 512], negmu1[:, t:t + 1], r1sb[:, t:t + 1],
                    OP.add, OP.mult),
                    waits=[(f"sdxq{slot}", xq_ld[slot]), ("tE", tvv), ("sE", sil),
                           ("vE", ts_read[0])], inc="vE")
                vv = P.op("vector", lambda e, o=oc2, s=sl: e.scalar_tensor_tensor(
                    t_s[:], ps_d[s][:, o * 512:(o + 1) * 512], 1.0, ob_bc[:, o * 512:(o + 1) * 512],
                    OP.mult, OP.add),
                    waits=[("vE", vv)], inc="vE")
                d_done[sl] = vv
                vv = P.op("vector", lambda e: e.tensor_add(t_s[:], t_s[:], t_xn[:]),
                          waits=[("vE", vv)], inc="vE")
                vv = P.op("vector", lambda e, o=oc2, st=slot: e.tensor_mul(
                    t_s[:], t_s[:], acc_t[st][:, o * 512:(o + 1) * 512]),
                    waits=[("vE", vv)], inc="vE")
                accq_free[slot] = vv
                vv = P.op("vector", lambda e, o=oc2, st=slot: e.tensor_add(
                    outb[st][:, o * 512:(o + 1) * 512], t_s[:], xq_t[st][:, o * 512:(o + 1) * 512]),
                    waits=[("vE", vv), (f"sdout{slot}", out_free[slot])], inc="vE")
                xq_free[slot] = vv
                ts_read[0] = vv
            otn_mm_free[slot] = P.now("tE")
            out_free[slot] = P.dma("sync", lambda e, t=tt, s=slot: e.dma_start(
                out=out[t * 128:(t + 1) * 128, :], in_=outb[s][:]),
                waits=[("vE", xq_free[slot])], sem=f"sdout{slot}")
            if tt + NSL < 16:
                load_tile(tt + NSL)

        P.barrier()

        finish()
        DCTX.close()
    return nc


def _host_inputs(kw):
    f32 = np.float32

    def C(a):
        return np.ascontiguousarray(np.asarray(a), dtype=f32)

    def col8(v):
        return C(np.asarray(v).reshape(8, 128).T)

    base = {
        "qwT": C(np.asarray(kw["L0_attn_q_w"]).T),
        "kwT": C(np.asarray(kw["L0_attn_k_w"]).T),
        "vwT": C(np.asarray(kw["L0_attn_v_w"]).T),
        "owT": C(np.asarray(kw["L0_attn_o_w"]).T),
        "g_q": col8(kw["L0_attn_ln_q_g"]), "b_qn": col8(kw["L0_attn_ln_q_b"]),
        "g_k": col8(kw["L0_attn_ln_k_g"]), "b_kn": col8(kw["L0_attn_ln_k_b"]),
        "g_v": col8(kw["L0_attn_ln_v_g"]), "b_vn": col8(kw["L0_attn_ln_v_b"]),
        "qb_c": col8(kw["L0_attn_q_b"]), "kb_c": col8(kw["L0_attn_k_b"]),
        "vb_c": col8(kw["L0_attn_v_b"]),
        "ob_r": C(np.asarray(kw["L0_attn_o_b"]).reshape(1, 1024)),
        "beta_r": C(np.asarray(kw["L0_attn_beta_param"]).reshape(1, 16)),
        "rw": C(np.asarray(kw["L0_ffn_router_w"]).reshape(8, 128, 16).transpose(1, 0, 2)),
        "rb_r": C(np.asarray(kw["L0_ffn_router_b"]).reshape(1, 16)),
        "w1": C(np.asarray(kw["L0_ffn_w1"]).reshape(16, 8, 128, 2048)),
        "b1c": C(np.asarray(kw["L0_ffn_b1"]).reshape(16, 16, 128).transpose(0, 2, 1)),
        "w2": C(np.asarray(kw["L0_ffn_w2"]).reshape(16, 16, 128, 1024)),
        "b2": C(np.asarray(kw["L0_ffn_b2"]).reshape(16, 1024)),
        "ident": np.eye(128, dtype=ml_dtypes.bfloat16),
        "ident32": np.eye(128, dtype=np.float32),
    }
    x = np.asarray(kw["L0_attn_query"], dtype=f32)
    in_maps = []
    for c in range(8):
        b, hf = c // 2, c % 2
        xb = x[b]
        xc = xb if hf == 0 else np.concatenate([xb[2048:], xb[:2048]], axis=0)
        in_maps.append(dict(base, x=np.ascontiguousarray(xc)))
    return in_maps


def kernel(**kw):
    in_maps = _host_inputs(kw)
    nc = build()
    res = run_bass_kernel_spmd(nc, in_maps, core_ids=list(range(8)))
    out = np.empty((4, 4096, 1024), dtype=np.float32)
    for c in range(8):
        b, hf = c // 2, c % 2
        out[b, hf * 2048:(hf + 1) * 2048, :] = res.results[c]["out"]
    return out
